# version 04
# baseline (speedup 1.0000x reference)
import sys
if "/opt/trn_rl_repo" not in sys.path:
    sys.path.insert(0, "/opt/trn_rl_repo")
import math
import numpy as np
import concourse.bass as bass
import concourse.mybir as mybir
from concourse.bass_utils import run_bass_kernel_spmd

F32 = mybir.dt.float32
BF16 = mybir.dt.bfloat16
AF = mybir.ActivationFunctionType
ALU = mybir.AluOpType
AX = mybir.AxisListType

D = 2048
NT = 16
S = 4096
NH = 16
CT = 410
NCH = 10
MW = 41
NSEL = 410
FH = 8192
FHT = 64
QSCALE = 1.0 / math.sqrt(128.0)
EPS = 1e-5

ENGS = ("sync", "gpsimd", "vector", "scalar", "tensor")


class Bld:
    AUTO = {"vector": "sDVE", "scalar": "sACT"}

    def __init__(self, nc):
        self.nc = nc
        self.ops = {e: [] for e in ENGS}
        self.sems = {}
        self.cnt = {}

    def sem(self, name):
        if name not in self.sems:
            self.sems[name] = self.nc.ctx.enter_context(self.nc.semaphore(name))
            self.cnt[name] = 0
        return self.sems[name]

    def wait(self, eng, sem, val=None):
        s = self.sem(sem)
        v = self.cnt[sem] if val is None else val
        if v > 0:
            self.ops[eng].append(lambda e, s=s, v=v: e.wait_ge(s, v))

    def op(self, eng, fn, inc=None, incby=1):
        a = self.AUTO.get(eng)
        if a is not None:
            s = self.sem(a)
            v = self.cnt[a]
            if v > 0:
                self.ops[eng].append(lambda e, s=s, v=v: e.wait_ge(s, v))
            if inc is None:
                inc = a
                incby = 1
        if inc is None:
            self.ops[eng].append(fn)
            return 0
        s = self.sem(inc)
        self.cnt[inc] += incby
        self.ops[eng].append(lambda e, fn=fn, s=s, b=incby: fn(e).then_inc(s, b))
        return self.cnt[inc]

    def dma(self, eng, dst, src, sem, **kw):
        return self.op(eng, lambda e, d=dst, s_=src, kw=kw: e.dma_start(d, s_, **kw),
                       inc=sem, incby=16)


def r_pit(dram_ap, p=128):
    return dram_ap.rearrange("(n p) t -> p n t", p=p)


def build(debug=False):
    nc = bass.Bass("TRN2", target_bir_lowering=False)

    xT = nc.dram_tensor("xT", [D, S], F32, kind="ExternalInput")
    wq = nc.dram_tensor("wq", [D, D], F32, kind="ExternalInput")
    wk = nc.dram_tensor("wk", [D, D], F32, kind="ExternalInput")
    wv = nc.dram_tensor("wv", [D, D], F32, kind="ExternalInput")
    wo = nc.dram_tensor("wo", [D, D], F32, kind="ExternalInput")
    bq = nc.dram_tensor("bq", [128, NT], F32, kind="ExternalInput")
    bk = nc.dram_tensor("bk", [128, NT], F32, kind="ExternalInput")
    bv = nc.dram_tensor("bv", [128, NT], F32, kind="ExternalInput")
    bo = nc.dram_tensor("bo", [128, NT], F32, kind="ExternalInput")
    wgT = nc.dram_tensor("wgT", [D, FH], F32, kind="ExternalInput")
    wuT = nc.dram_tensor("wuT", [D, FH], F32, kind="ExternalInput")
    wdT = nc.dram_tensor("wdT", [FH, D], F32, kind="ExternalInput")
    outT = nc.dram_tensor("outT", [D, S], F32, kind="ExternalOutput")

    dbg = "ExternalOutput" if debug else "Internal"
    xn_hbm = nc.dram_tensor("xn_hbm", [D, S], BF16, kind=dbg)
    trow_hbm = nc.dram_tensor("trow_hbm", [NCH, NH, 416], BF16, kind=dbg)
    recip_hbm = nc.dram_tensor("recip_hbm", [NCH, NH * MW], BF16, kind=dbg)
    gated_hbm = nc.dram_tensor("gated_hbm", [FH, S], BF16, kind=dbg)
    wd_bf = nc.dram_tensor("wd_bf", [NT, 128, FHT * 128], BF16, kind="Internal")
    if debug:
        qsel_dbg = nc.dram_tensor("qsel_dbg", [128, NT * NSEL], BF16, kind="ExternalOutput")
        snum_dbg = nc.dram_tensor("snum_dbg", [128, NT * NSEL], BF16, kind="ExternalOutput")
        ctx_dbg = nc.dram_tensor("ctx_dbg", [128, NT * 411], BF16, kind="ExternalOutput")
        delta_dbg = nc.dram_tensor("delta_dbg", [128, NT * NSEL], BF16, kind="ExternalOutput")
        u_dbg = nc.dram_tensor("u_dbg", [128, NT], F32, kind="ExternalOutput")
        vsum_dbg = nc.dram_tensor("vsum_dbg", [128, NT], F32, kind="ExternalOutput")

    BASE = 16544

    def al(name, shape, dtype, offset):
        return nc.alloc_sbuf_tensor_at(name, shape, dtype, offset=BASE + offset)

    A = al("A", [128, 32768], BF16, offset=0)
    Bw = al("Bw", [128, 32768], BF16, offset=65536)
    C = al("C", [128, 13120], BF16, offset=131072)
    P1 = al("P1", [128, 6560], BF16, offset=157312)
    P2 = al("P2", [128, 6576], BF16, offset=170432)
    DL = al("DL", [128, 6560], BF16, offset=183584)
    T = al("T", [128, 2048], F32, offset=196704)
    XF0 = al("XF0", [128, 8192], F32, offset=32768)
    XF1 = al("XF1", [128, 8192], F32, offset=65536)
    WDS0 = al("WDS0", [128, 8192], BF16, offset=32768)
    WDS1 = al("WDS1", [128, 8192], BF16, offset=49152)
    xbf = al("xbf", [128, 1024], BF16, offset=196704)
    sqbf = al("sqbf", [128, 512], BF16, offset=198752)
    m_sb = al("m_sb", [128, 512], F32, offset=200800)
    r_sb = al("r_sb", [128, 512], F32, offset=202848)
    t1_sb = al("t1_sb", [128, 512], F32, offset=196704)
    trw = [al("trw%d" % j, [1, 8 * CT], BF16, offset=157312 + 6560 * j) for j in range(2)]
    trr = [al("trr%d" % j, [1, 8 * CT], BF16, offset=170432 + 6560 * j) for j in range(2)]
    gs_sb = al("gs_sb", [128, 2048], BF16, offset=131072)
    gb0 = al("gb0", [128, 2048], BF16, offset=135168)
    gb1 = al("gb1", [128, 2048], BF16, offset=139264)
    outb0 = al("outb0", [128, 512], F32, offset=131072)
    outb1 = al("outb1", [128, 512], F32, offset=133120)
    xb0 = al("xb0", [128, 2048], F32, offset=135168)
    xb1 = al("xb1", [128, 2048], F32, offset=143360)
    sm = 204896
    bq_sb = al("bq_sb", [128, NT], F32, offset=sm); sm += 64
    bk_sb = al("bk_sb", [128, NT], F32, offset=sm); sm += 64
    bv_sb = al("bv_sb", [128, NT], F32, offset=sm); sm += 64
    bo_sb = al("bo_sb", [128, NT], F32, offset=sm); sm += 64
    u_sb = al("u_sb", [128, NT], F32, offset=sm); sm += 64
    vsum_sb = al("vsum_sb", [128, NT], F32, offset=sm); sm += 64
    upre_sb = al("upre_sb", [128, NT], BF16, offset=sm); sm += 32
    den_sb = al("den_sb", [128, MW], F32, offset=sm); sm += 192
    num_sb = al("num_sb", [128, MW], F32, offset=sm); sm += 192
    rrow_sb = al("rrow_sb", [1, NH * MW], BF16, offset=sm); sm += 1344
    ones128 = al("ones128", [128, 128], BF16, offset=sm); sm += 256
    ones1 = al("ones1", [1, 128], BF16, offset=sm); sm += 256
    onesk = al("onesk", [128, 8], BF16, offset=sm); sm += 32
    colt_sb = al("colt_sb", [128, 1], F32, offset=sm); sm += 32
    vt_sb = al("vt_sb", [128, 1], F32, offset=sm); sm += 32
    eps_sb = al("eps_sb", [128, 1], F32, offset=sm); sm += 32
    denr_sb = al("denr_sb", [1, 64], F32, offset=sm); sm += 256
    Eb4 = [al("E%d" % j, [128, CT], BF16, offset=200800 + 832 * j) for j in range(4)]
    rrw = []
    for j in range(2):
        rrw.append(al("rrw%d" % j, [1, NH * MW], BF16, offset=sm)); sm += 1344
    assert BASE + sm <= 229376 - 64, sm

    psA = nc.alloc_psum_tensor("psA", [128, 2048], F32)
    psS = nc.alloc_psum_tensor("psS", [128, 1024], F32)
    psX = nc.alloc_psum_tensor("psX", [128, 1024], F32)

    b = Bld(nc)

    def wtile(i, o):
        return Bw[:, o * 2048 + i * 128:o * 2048 + (i + 1) * 128]

    def load_w_slabs(wsrc, guard=None, olist=range(NT)):
        w3 = r_pit(wsrc[:, :])
        for o in olist:
            if guard is not None:
                b.wait("gpsimd", guard[0], guard[1])
            dst = Bw[:, o * 2048:(o + 1) * 2048].rearrange("p (i c) -> p i c", i=NT)
            b.dma("gpsimd", dst, w3[:, :, o * 128:(o + 1) * 128], "dW%d" % o)

    s_reg = [(psS, 0), (psS, 512), (psX, 0), (psX, 512)]

    b.dma("gpsimd", bq_sb[:, :], bq[:, :], "dB")
    b.dma("gpsimd", bk_sb[:, :], bk[:, :], "dB")
    b.dma("gpsimd", bv_sb[:, :], bv[:, :], "dB")
    b.dma("gpsimd", bo_sb[:, :], bo[:, :], "dB")
    load_w_slabs(wq, None, range(8, 16))
    b.op("vector", lambda e: e.memset(ones128[:, :], 1.0))
    b.op("vector", lambda e: e.memset(ones1[:, :], 1.0))
    b.op("vector", lambda e: e.memset(onesk[:, :], 1.0))
    b.op("vector", lambda e: e.memset(eps_sb[:, :], EPS))
    b.op("vector", lambda e: e.memset(vsum_sb[:, :], 0.0), inc="sDVE")

    XF = (XF0, XF1)
    xstage = (A[:, 0:8192], A[:, 8192:16384])
    xT3 = r_pit(xT[:, :])
    xn3 = r_pit(xn_hbm[:, :])
    xf_free_mark = [0, 0]
    xstage_free_mark = [0, 0]
    selpos = []
    m0 = 0
    for c in range(8):
        tl0 = (-512 * c) % 10
        cnt_ = (512 - tl0 + 9) // 10
        selpos.append((tl0, cnt_, m0))
        m0 += cnt_
    assert m0 == NSEL

    def s0_dma(c):
        par = c % 2
        b.wait("sync", "sDVE", xf_free_mark[par])
        dst = XF[par][:, :].rearrange("p (n t) -> p n t", n=NT)
        b.dma("sync", dst, xT3[:, :, 512 * c:512 * (c + 1)], "dX%d" % par)

    s0_dma(0)
    s0_dma(1)
    for c in range(8):
        par = c % 2
        b.wait("vector", "dX%d" % par, 16 * (c // 2 + 1))
        for i in range(NT):
            ip = i % 2
            xsl = XF[par][:, 512 * i:512 * (i + 1)]
            if i >= 2:
                b.wait("vector", "sPE", b.cnt["sPE"])
            b.op("vector", lambda e, d=xbf[:, 512 * ip:512 * (ip + 1)], s=xsl:
                 e.tensor_copy(d, s), inc="sDVE")
            b.wait("scalar", "sDVE")
            if i >= 1:
                b.wait("scalar", "sPE", b.cnt["sPE"])
            b.op("scalar", lambda e, s=xbf[:, 512 * ip:512 * (ip + 1)]:
                 e.activation(sqbf[:, :], s, AF.Square), inc="sACT")
            b.wait("tensor", "sDVE")
            b.op("tensor", lambda e, s=xbf[:, 512 * ip:512 * (ip + 1)], st=(i == 0),
                 sp=(i == NT - 1): e.matmul(psX[:, 0:512], ones128[:, :], s, start=st, stop=sp))
            b.wait("tensor", "sACT")
            b.op("tensor", lambda e, st=(i == 0), sp=(i == NT - 1):
                 e.matmul(psX[:, 512:1024], ones128[:, :], sqbf[:, :], start=st, stop=sp),
                 inc="sPE")
        b.wait("vector", "sPE")
        b.op("vector", lambda e: e.tensor_scalar(m_sb[:, :], psX[:, 0:512], 1.0 / D, None, ALU.mult))
        b.op("vector", lambda e: e.tensor_mul(r_sb[:, :], m_sb[:, :], m_sb[:, :]))
        b.op("vector", lambda e: e.scalar_tensor_tensor(
            r_sb[:, :], psX[:, 512:1024], 1.0 / D, r_sb[:, :], ALU.mult, ALU.subtract), inc="sDVE")
        b.wait("scalar", "sDVE")
        b.op("scalar", lambda e: e.activation(r_sb[:, :], r_sb[:, :], AF.Sqrt, bias=eps_sb[:, 0:1]),
             inc="sACT")
        b.wait("vector", "sACT")
        b.op("vector", lambda e: e.reciprocal(r_sb[:, :], r_sb[:, :]))
        b.wait("vector", "dXNW%d" % par, xstage_free_mark[par])
        tl0, cnt_, mst = selpos[c]
        for i in range(NT):
            xsl = XF[par][:, 512 * i:512 * (i + 1)]
            stg = xstage[par][:, 512 * i:512 * (i + 1)]
            b.op("vector", lambda e, x=xsl: e.tensor_sub(t1_sb[:, :], x, m_sb[:, :]))
            b.op("vector", lambda e, d=stg: e.tensor_mul(d, t1_sb[:, :], r_sb[:, :]))
            b.op("vector", lambda e, d=P1[:, i * NSEL + mst:i * NSEL + mst + cnt_],
                 s=xstage[par][:, 512 * i + tl0:512 * (i + 1):10]: e.tensor_copy(d, s))
        xf_free_mark[par] = b.op("vector", lambda e: e.engine_nop(), inc="sDVE")
        b.wait("sync", "sDVE", xf_free_mark[par])
        src = xstage[par][:, :].rearrange("p (n t) -> p n t", n=NT)
        xstage_free_mark[par] = b.dma("sync", xn3[:, :, 512 * c:512 * (c + 1)], src,
                                      "dXNW%d" % par)
        if c + 2 < 8:
            s0_dma(c + 2)
    sel_ready = b.cnt["sDVE"]

    slot_evac = [("sDVE", 0), ("sDVE", 0), ("sDVE", 0), ("sDVE", 0)]

    load_w_slabs(wq, ("sDVE", xf_free_mark[1]), range(0, 8))
    for o in range(NT):
        b.wait("tensor", "dW%d" % o, b.cnt["dW%d" % o])
    b.wait("tensor", "sDVE", sel_ready)
    b.wait("vector", "dB", 64)
    for g in range(4):
        if g > 0:
            b.wait("tensor", "sDVE", b.cnt["sDVE"])
        for i in range(NT):
            for o in range(4 * g, 4 * g + 4):
                last = (i == NT - 1 and o == 4 * g + 3)
                m = b.op("tensor", lambda e, i=i, o=o, st=(i == 0), sp=(i == NT - 1):
                     e.matmul(psA[:, (o % 4) * 512:(o % 4) * 512 + NSEL], wtile(i, o),
                              P1[:, i * NSEL:(i + 1) * NSEL], start=st, stop=sp),
                     inc="sPE" if last else None)
        b.wait("vector", "sPE", m)
        for o in range(4 * g, 4 * g + 4):
            slot_evac[o % 4] = ("sDVE", b.op("vector", lambda e, o=o:
                 e.tensor_scalar(P2[:, o * NSEL:(o + 1) * NSEL],
                                 psA[:, (o % 4) * 512:(o % 4) * 512 + NSEL],
                                 bq_sb[:, o:o + 1], QSCALE, ALU.add, ALU.mult), inc="sDVE"))
    qsel_ready = b.cnt["sDVE"]
    s1_pe_done = b.cnt["sPE"]
    if debug:
        b.wait("sync", "sDVE", qsel_ready)
        b.dma("sync", qsel_dbg[:, :], P2[:, 0:NT * NSEL], "dDBG")
        b.wait("vector", "dDBG", b.cnt["dDBG"])

    Cb = (C[:, 0:NT * CT], C[:, NT * CT:2 * NT * CT])
    xnbuf_free = [0, 0]
    cb_free = [0, 0]
    pms = {}

    def xn_dma(c, dve_guard=0):
        par = c % 2
        b.wait("sync", "sPE", xnbuf_free[par])
        if dve_guard:
            b.wait("sync", "sDVE", dve_guard)
        b.wait("sync", "dXNW0", 64)
        b.wait("sync", "dXNW1", 64)
        dst = A[:, par * NT * CT:(par + 1) * NT * CT].rearrange("p (n t) -> p n t", n=NT)
        L = CT if c < NCH - 1 else S - (NCH - 1) * CT
        b.dma("sync", dst[:, :, 0:L], xn3[:, :, c * CT:c * CT + L], "dXN%d" % par)

    def proj_chunk(c, wcnt, bias_col):
        par = c % 2
        kv_len = CT if c < NCH - 1 else S - (NCH - 1) * CT
        xb_ = A[:, par * NT * CT:(par + 1) * NT * CT]
        b.wait("tensor", "dXN%d" % par, b.cnt["dXN%d" % par])
        b.wait("scalar", "sDVE", cb_free[par])
        for o in range(NT):
            reg = (o % 4) * 512
            b.wait("tensor", slot_evac[o % 4][0], slot_evac[o % 4][1])
            b.wait("tensor", "dW%d" % o, wcnt[o])
            for i in range(NT):
                m = b.op("tensor", lambda e, i=i, o=o, r=reg, L=kv_len, x=xb_,
                         st=(i == 0), sp=(i == NT - 1):
                         e.matmul(psA[:, r:r + L], wtile(i, o), x[:, i * CT:i * CT + L],
                                  start=st, stop=sp),
                         inc="sPE" if i == NT - 1 else None)
            b.wait("scalar", "sPE", m)
            ev = b.op("scalar", lambda e, o=o, r=reg, L=kv_len, bc=bias_col, dst=Cb[par]:
                 e.activation(dst[:, o * CT:o * CT + L], psA[:, r:r + L], AF.Identity,
                              bias=bc[:, o:o + 1]), inc="sACT")
            if kv_len < CT:
                b.op("vector", lambda e, o=o, L=kv_len, dst=Cb[par]:
                     e.memset(dst[:, o * CT + L:(o + 1) * CT], 0.0))
            slot_evac[o % 4] = ("sACT", ev)
        pms[c] = m
        return m

    def kslice3(h, par):
        return Cb[par][:, h * CT:(h + 1) * CT].rearrange("p (m d) -> p d m", m=MW, d=10)

    E_free = [0, 0, 0, 0]
    psR_free = [0, 0, 0, 0]
    trw_free = [0, 0]
    trr_free = [0, 0]

    def scores_chunk(c, evac_mark):
        par = c % 2
        b.wait("vector", "sACT", evac_mark)
        for h in range(NH):
            rot = h % 4
            hb = h // 8
            ps_, off = s_reg[rot]
            Eb = Eb4[rot]
            b.wait("vector", "sPE", E_free[rot])
            qsl = P2[:, h * NSEL + MW * c:h * NSEL + MW * (c + 1)]
            b.op("vector", lambda e, q=qsl, h=h, Eb=Eb, par=par:
                 e.tensor_mul(Eb[:, :].rearrange("p (d m) -> p d m", d=10),
                              q.rearrange("p (o m) -> p o m", o=1).broadcast_to((128, 10, MW)),
                              kslice3(h, par)), inc="sDVE")
            E_mark = b.cnt["sDVE"]
            b.wait("tensor", "sDVE", E_mark)
            b.wait("tensor", "sDVE", psR_free[rot])
            E_free[rot] = b.op("tensor", lambda e, Eb=Eb, ps=ps_, off=off:
                 e.matmul(ps[0:1, off:off + CT], onesk[:, 0:1], Eb[:, 0:CT],
                          start=True, stop=True), inc="sPE")
            b.wait("scalar", "sPE", E_free[rot])
            b.op("scalar", lambda e, ps=ps_, off=off:
                 e.activation(ps[0:1, off:off + CT], ps[0:1, off:off + CT], AF.Exp), inc="sACT")
            b.wait("vector", "sACT", b.cnt["sACT"])
            if h % 8 == 0:
                b.wait("vector", "dT%d" % hb, trw_free[hb])
            b.op("vector", lambda e, ps=ps_, off=off, t=trw[hb], hl=h % 8:
                 e.tensor_scalar(t[0:1, hl * CT:(hl + 1) * CT], ps[0:1, off:off + CT],
                                 -1.0, None, ALU.add))
            b.op("vector", lambda e, ps=ps_, off=off:
                 e.tensor_reduce(denr_sb[0:1, 0:MW],
                                 ps[0:1, off:off + CT].rearrange("o (d m) -> o m d", d=10),
                                 AX.X, ALU.add))
            psR_free[rot] = b.cnt["sDVE"]
            b.op("vector", lambda e: e.tensor_scalar(denr_sb[0:1, 0:MW], denr_sb[0:1, 0:MW],
                                                     float(S - 10), None, ALU.add))
            b.op("vector", lambda e: e.reciprocal(denr_sb[0:1, 0:MW], denr_sb[0:1, 0:MW]))
            if h == 0:
                b.wait("vector", "dRW", rrow_free[0])
            b.op("vector", lambda e, h=h, t=rrw[c % 2]:
                 e.tensor_copy(t[0:1, h * MW:(h + 1) * MW], denr_sb[0:1, 0:MW]), inc="sDVE")
            if h % 8 == 7:
                m2 = b.cnt["sDVE"]
                b.wait("scalar", "sDVE", m2)
                trw_free[hb] = b.dma("scalar", trow_hbm[c, hb * 8:hb * 8 + 8, 0:CT],
                                     trw[hb][0:1, :].rearrange("o (g m) -> o g m", g=8),
                                     "dT%d" % hb)
        m2 = b.cnt["sDVE"]
        b.wait("scalar", "sDVE", m2)
        rrow_free[0] = b.dma("scalar", recip_hbm[c:c + 1, :], rrw[c % 2][0:1, :], "dRW")
        cb_free[par] = b.cnt["sDVE"]

    rrow_free = [0]
    def num_chunk(c, evac_mark):
        par = c % 2
        b.wait("vector", "sACT", evac_mark)
        b.wait("scalar", "sPE", rr_guard[0])
        b.dma("scalar", rrow_sb[0:1, :], recip_hbm[c:c + 1, :], "dRR")
        for i in range(NT):
            b.op("vector", lambda e, i=i, src=Cb[par]:
                 e.tensor_reduce(vt_sb[:, :], src[:, i * CT:(i + 1) * CT], AX.X, ALU.add))
            b.op("vector", lambda e, i=i: e.tensor_add(vsum_sb[:, i:i + 1], vsum_sb[:, i:i + 1],
                                                       vt_sb[:, :]))
        for h in range(NH):
            rot = h % 4
            hb = h // 8
            ps_, off = s_reg[rot]
            if h % 8 == 0:
                b.wait("scalar", "sPE", trr_free[hb])
                b.dma("scalar", trr[hb][0:1, :].rearrange("o (g m) -> o g m", g=8),
                      trow_hbm[c, hb * 8:hb * 8 + 8, 0:CT], "dTR%d" % hb)
            b.wait("tensor", "dTR%d" % hb, b.cnt["dTR%d" % hb])
            b.wait("tensor", "dRR", b.cnt["dRR"])
            b.wait("tensor", "sDVE", psR_free[rot])
            b.op("tensor", lambda e, t=trr[hb], hl=h % 8, ps=ps_, off=off:
                 e.matmul(ps[:, off:off + CT], ones1[:, :], t[0:1, hl * CT:(hl + 1) * CT],
                          start=True, stop=True))
            trr_free[hb] = b.op("tensor", lambda e, h=h, ps=ps_, off=off:
                 e.matmul(ps[:, off + CT:off + CT + MW], ones1[:, :],
                          rrow_sb[0:1, h * MW:(h + 1) * MW], start=True, stop=True),
                 inc="sPE")
            b.wait("vector", "sPE", trr_free[hb])
            b.op("vector", lambda e, ps=ps_, off=off, h=h, par=par:
                 e.tensor_mul(T[:, 0:CT].rearrange("p (d m) -> p d m", d=10),
                              ps[:, off:off + CT].rearrange("p (d m) -> p d m", d=10),
                              kslice3(h, par)))
            b.op("vector", lambda e:
                 e.tensor_reduce(num_sb[:, :], T[:, 0:CT].rearrange("p (d m) -> p m d", d=10),
                                 AX.X, ALU.add))
            psR_free[rot] = b.op("vector", lambda e, ps=ps_, off=off, h=h, c=c:
                 e.tensor_mul(P1[:, h * NSEL + MW * c:h * NSEL + MW * (c + 1)],
                              num_sb[:, :], ps[:, off + CT:off + CT + MW]), inc="sDVE")
        cb_free[par] = b.cnt["sDVE"]

    rr_guard = [0]

    wdT3 = r_pit(wdT[:, :])
    WDS = (WDS0, WDS1)
    wds_free = [0, 0]

    def wd_precast(o):
        pw = o % 2
        b.wait("gpsimd", "dWP%d" % pw, wds_free[pw])
        b.dma("gpsimd", WDS[pw][:, :].rearrange("p (g c) -> p g c", g=FHT),
              wdT3[:, :, o * 128:(o + 1) * 128], "dWC")
        b.wait("gpsimd", "dWC", b.cnt["dWC"])
        wds_free[pw] = b.dma("gpsimd", wd_bf[o, :, :], WDS[pw][:, :], "dWP%d" % pw)

    load_w_slabs(wk, ("sPE", s1_pe_done))
    wk_cnt = [b.cnt["dW%d" % o] for o in range(NT)]
    xn_dma(0, dve_guard=sel_ready)
    xn_dma(1, dve_guard=sel_ready)
    ev_mark = {}
    for c in range(NCH):
        pm = proj_chunk(c, wk_cnt, bk_sb)
        ev_mark[c] = b.cnt["sACT"]
        if c + 2 < NCH:
            xnbuf_free[c % 2] = pm
            xn_dma(c + 2)
        if c >= 1:
            scores_chunk(c - 1, ev_mark[c - 1])
        if c < 10:
            wd_precast(c)
    scores_chunk(NCH - 1, ev_mark[NCH - 1])

    load_w_slabs(wv, ("sPE", pms[NCH - 1]))
    wv_cnt = [b.cnt["dW%d" % o] for o in range(NT)]
    for j in range(2):
        b.wait("scalar", "dT%d" % j, b.cnt["dT%d" % j])
    b.wait("scalar", "dRW", b.cnt["dRW"])
    xnbuf_free = [pms[NCH - 2], pms[NCH - 1]]
    xn_dma(0)
    xn_dma(1)
    for c in range(NCH):
        pm = proj_chunk(c, wv_cnt, bv_sb)
        ev_mark[c] = b.cnt["sACT"]
        if c + 2 < NCH:
            xnbuf_free[c % 2] = pm
            xn_dma(c + 2)
        if c >= 1:
            num_chunk(c - 1, ev_mark[c - 1])
            rr_guard[0] = b.cnt["sPE"]
        if c + 10 < NT:
            wd_precast(c + 10)
    num_chunk(NCH - 1, ev_mark[NCH - 1])
    s3_last_pm = pms[NCH - 1]
    if debug:
        m = b.op("vector", lambda e: e.engine_nop(), inc="sDVE")
        b.wait("sync", "sDVE", m)
        b.dma("sync", snum_dbg[:, :], P1[:, :], "dDBG")
        b.dma("sync", vsum_dbg[:, :], vsum_sb[:, :], "dDBG")
        b.wait("vector", "dDBG", b.cnt["dDBG"])

    b.op("vector", lambda e: e.tensor_scalar(upre_sb[:, :], vsum_sb[:, :], 1.0 / S, None,
                                             ALU.mult), inc="sDVE")
    psR_free = [b.cnt["sDVE"]] * 4
    rr_guard2 = b.cnt["sPE"]
    for c in range(NCH):
        b.wait("sync", "dRW", b.cnt["dRW"])
        b.wait("sync", "sPE", rr_guard2)
        b.dma("sync", rrow_sb[0:1, :], recip_hbm[c:c + 1, :], "dRR")
        for h in range(NH):
            rot = h % 4
            ps_, off = s_reg[rot]
            b.wait("tensor", "dRR", b.cnt["dRR"])
            b.wait("tensor", "sDVE", psR_free[rot])
            b.op("tensor", lambda e, h=h, ps=ps_, off=off:
                 e.matmul(ps[:, off:off + MW], ones1[:, :], rrow_sb[0:1, h * MW:(h + 1) * MW],
                          start=True, stop=True), inc="sPE")
            b.wait("vector", "sPE", b.cnt["sPE"])
            psR_free[rot] = b.op("vector", lambda e, h=h, c=c, ps=ps_, off=off:
                 e.scalar_tensor_tensor(P2[:, h * 411 + MW * c:h * 411 + MW * (c + 1)],
                                        ps[:, off:off + MW], vsum_sb[:, h:h + 1],
                                        P1[:, h * NSEL + MW * c:h * NSEL + MW * (c + 1)],
                                        ALU.mult, ALU.add), inc="sDVE")
        rr_guard2 = b.cnt["sPE"]
    b.op("vector", lambda e: e.tensor_copy(P2[:, 410:16 * 411:411], upre_sb[:, :]))
    ctx_ready = b.cnt["sDVE"]
    if debug:
        b.wait("sync", "sDVE", ctx_ready)
        b.dma("sync", ctx_dbg[:, :], P2[:, :], "dDBG")
        b.wait("vector", "dDBG", b.cnt["dDBG"])

    load_w_slabs(wo, ("sPE", s3_last_pm))
    b.wait("tensor", "sDVE", ctx_ready)
    for g in range(4):
        for sl in range(4):
            b.wait("tensor", slot_evac[sl][0], slot_evac[sl][1])
        for o in range(4 * g, 4 * g + 4):
            b.wait("tensor", "dW%d" % o, b.cnt["dW%d" % o])
        for i in range(NT):
            for o in range(4 * g, 4 * g + 4):
                last = (i == NT - 1 and o == 4 * g + 3)
                m = b.op("tensor", lambda e, i=i, o=o, st=(i == 0), sp=(i == NT - 1):
                     e.matmul(psA[:, (o % 4) * 512:(o % 4) * 512 + 411], wtile(i, o),
                              P2[:, i * 411:(i + 1) * 411], start=st, stop=sp),
                     inc="sPE" if last else None)
        b.wait("vector", "sPE", m)
        for o in range(4 * g, 4 * g + 4):
            r = (o % 4) * 512
            b.op("vector", lambda e, o=o, r=r:
                 e.tensor_scalar(u_sb[:, o:o + 1], psA[:, r + 410:r + 411],
                                 bo_sb[:, o:o + 1], None, ALU.add))
            b.op("vector", lambda e, o=o:
                 e.tensor_sub(colt_sb[:, :], bo_sb[:, o:o + 1], u_sb[:, o:o + 1]))
            slot_evac[o % 4] = ("sDVE", b.op("vector", lambda e, o=o, r=r:
                 e.tensor_scalar(DL[:, o * NSEL:(o + 1) * NSEL], psA[:, r:r + NSEL],
                                 colt_sb[:, :], None, ALU.add), inc="sDVE"))
    if debug:
        m = b.cnt["sDVE"]
        b.wait("sync", "sDVE", m)
        b.dma("sync", delta_dbg[:, :], DL[:, :], "dDBG")
        b.dma("sync", u_dbg[:, :], u_sb[:, :], "dDBG")
        b.wait("vector", "dDBG", b.cnt["dDBG"])

    wg3 = r_pit(wgT[:, :])
    wu3 = r_pit(wuT[:, :])
    g3 = r_pit(gated_hbm[:, :])
    slabs = [Bw[:, 0:8192], Bw[:, 8192:16384], Bw[:, 16384:24576], Bw[:, 24576:32768]]
    slab_free = [b.cnt["sPE"], b.cnt["sPE"]]
    gb = (gb0, gb1)
    gb_free = [0, 0]
    gs_free = b.cnt["sDVE"]
    psA_free = 0
    psB_free = b.cnt["sDVE"]
    half_free = b.cnt["sPE"]
    b.wait("tensor", "sDVE", b.cnt["sDVE"])
    b.wait("sync", "dWP0", b.cnt["dWP0"])
    b.wait("sync", "dWP1", b.cnt["dWP1"])
    for half in range(2):
        b.wait("sync", "sPE", half_free)
        b.dma("sync", A[:, :].rearrange("p (n t) -> p n t", n=NT),
              xn3[:, :, half * 2048:(half + 1) * 2048], "dXN0")
        for s_ in range(16):
            ps_ = s_ % 2
            b.wait("gpsimd", "sPE", slab_free[ps_])
            b.dma("gpsimd", slabs[2 * ps_][:, :].rearrange("p (n t) -> p n t", n=NT),
                  wg3[:, :, s_ * 512:(s_ + 1) * 512], "dSL%d" % ps_)
            b.dma("gpsimd", slabs[2 * ps_ + 1][:, :].rearrange("p (n t) -> p n t", n=NT),
                  wu3[:, :, s_ * 512:(s_ + 1) * 512], "dSL%d" % ps_)
            for ht in range(4):
                gidx = half * 64 + s_ * 4 + ht
                pg = gidx % 2
                b.wait("tensor", "dSL%d" % ps_, b.cnt["dSL%d" % ps_])
                b.wait("tensor", "dXN0", b.cnt["dXN0"])
                b.wait("tensor", "sACT", psA_free)
                for i in range(NT):
                    for ch in range(4):
                        gate_done = b.op("tensor", lambda e, i=i, ch=ch, sl=slabs[2 * ps_],
                             ht=ht, st=(i == 0), sp=(i == NT - 1):
                             e.matmul(psA[:, ch * 512:(ch + 1) * 512],
                                      sl[:, i * 512 + ht * 128:i * 512 + (ht + 1) * 128],
                                      A[:, i * 2048 + ch * 512:i * 2048 + (ch + 1) * 512],
                                      start=st, stop=sp),
                             inc="sPE" if (i == NT - 1 and ch == 3) else None)
                b.wait("tensor", "sDVE", psB_free)
                for i in range(NT):
                    for ch in range(4):
                        tgt = psS if ch < 2 else psX
                        up_done = b.op("tensor", lambda e, i=i, ch=ch, sl=slabs[2 * ps_ + 1],
                             ht=ht, t=tgt, st=(i == 0), sp=(i == NT - 1):
                             e.matmul(t[:, (ch % 2) * 512:(ch % 2 + 1) * 512],
                                      sl[:, i * 512 + ht * 128:i * 512 + (ht + 1) * 128],
                                      A[:, i * 2048 + ch * 512:i * 2048 + (ch + 1) * 512],
                                      start=st, stop=sp),
                             inc="sPE" if (i == NT - 1 and ch == 3) else None)
                b.wait("scalar", "sPE", gate_done)
                b.wait("scalar", "sDVE", gs_free)
                psA_free = b.op("scalar", lambda e:
                     e.activation(gs_sb[:, :], psA[:, 0:2048], AF.Silu), inc="sACT")
                b.wait("vector", "sPE", up_done)
                b.wait("vector", "sACT", psA_free)
                b.wait("vector", "dGW%d" % pg, gb_free[pg])
                b.op("vector", lambda e, d=gb[pg]:
                     e.tensor_mul(d[:, 0:1024], psS[:, 0:1024], gs_sb[:, 0:1024]))
                psB_free = b.op("vector", lambda e, d=gb[pg]:
                     e.tensor_mul(d[:, 1024:2048], psX[:, 0:1024], gs_sb[:, 1024:2048]),
                     inc="sDVE")
                gs_free = psB_free
                b.wait("scalar", "sDVE", psB_free)
                gb_free[pg] = b.dma("scalar", g3[:, s_ * 4 + ht, half * 2048:(half + 1) * 2048],
                                    gb[pg][:, :], "dGW%d" % pg)
            slab_free[ps_] = b.cnt["sPE"]
        half_free = b.cnt["sPE"]

    xbb = (xb0, xb1)
    outb = (outb0, outb1)
    wdq_free = b.cnt["sPE"]
    ge_free = [b.cnt["sPE"], b.cnt["sPE"]]
    xb_free = [0, 0]
    outb_free = [0, 0]
    psA_slot_free = [0, 0, 0, 0]
    b.wait("sync", "dGW0", b.cnt["dGW0"])
    b.wait("sync", "dGW1", b.cnt["dGW1"])
    b.wait("scalar", "dGW0", b.cnt["dGW0"])
    b.wait("scalar", "dGW1", b.cnt["dGW1"])
    b.wait("tensor", "sDVE", b.cnt["sDVE"])
    b.wait("tensor", "sACT", b.cnt["sACT"])
    for dq in range(4):
        b.wait("sync", "sPE", wdq_free)
        for dtl in range(4):
            b.dma("sync", Bw[:, dtl * 8192:(dtl + 1) * 8192], wd_bf[dq * 4 + dtl, :, :], "dWD")
        for e_ in range(8):
            pe_ = e_ % 2
            b.wait("scalar", "sDVE", xb_free[pe_])
            b.dma("scalar", xbb[pe_][:, :].rearrange("p (n t) -> p n t", n=4),
                  xT3[:, 4 * dq:4 * dq + 4, e_ * 512:(e_ + 1) * 512], "dXS%d" % pe_)
            for hh in range(2):
                q_ = "sync" if hh == 0 else "scalar"
                b.wait(q_, "sPE", ge_free[hh])
                dst = A[:, hh * 16384:(hh + 1) * 16384].rearrange("p (n t) -> p n t", n=32)
                b.dma(q_, dst, g3[:, hh * 32:(hh + 1) * 32, e_ * 512:(e_ + 1) * 512],
                      "dGE%d" % hh)
                b.wait("tensor", "dGE%d" % hh, b.cnt["dGE%d" % hh])
                if hh == 0:
                    b.wait("tensor", "dWD", b.cnt["dWD"])
                    for dt in range(4):
                        b.wait("tensor", "sDVE", psA_slot_free[dt])
                for h32 in range(32):
                    g_ = hh * 32 + h32
                    for dt in range(4):
                        ge_free[hh] = b.op("tensor", lambda e, g=g_, dt=dt, hh=hh, h32=h32,
                             st=(hh == 0 and h32 == 0), sp=(hh == 1 and h32 == 31):
                             e.matmul(psA[:, dt * 512:(dt + 1) * 512],
                                      Bw[:, dt * 8192 + g * 128:dt * 8192 + (g + 1) * 128],
                                      A[:, hh * 16384 + h32 * 512:hh * 16384 + (h32 + 1) * 512],
                                      start=st, stop=sp),
                             inc="sPE" if (h32 == 31 and dt == 3) else None)
            mm_done = ge_free[1]
            tl0 = (-512 * e_) % 10
            msta = (512 * e_ + 9) // 10
            scnt = (512 * (e_ + 1) - 1 - (512 * e_ + tl0)) // 10 + 1
            for dt in range(4):
                o = dq * 4 + dt
                po = (e_ * 4 + dt) % 2
                b.wait("vector", "sPE", mm_done)
                b.wait("vector", "dXS%d" % pe_, b.cnt["dXS%d" % pe_])
                b.wait("vector", "dOUT%d" % po, outb_free[po])
                b.op("vector", lambda e, dt=dt, o=o, po=po, pe2=pe_:
                     e.scalar_tensor_tensor(outb[po][:, :], psA[:, dt * 512:(dt + 1) * 512],
                                            u_sb[:, o:o + 1], xbb[pe2][:, dt * 512:(dt + 1) * 512],
                                            ALU.add, ALU.add))
                psA_slot_free[dt] = b.op("vector", lambda e, o=o, po=po, t0=tl0, ms=msta, cn=scnt:
                     e.tensor_add(outb[po][:, t0:t0 + (cn - 1) * 10 + 1:10],
                                  outb[po][:, t0:t0 + (cn - 1) * 10 + 1:10],
                                  DL[:, o * NSEL + ms:o * NSEL + ms + cn]), inc="sDVE")
                xb_free[pe_] = psA_slot_free[dt]
                b.wait("scalar", "sDVE", psA_slot_free[dt])
                outb_free[po] = b.dma("scalar", r_pit(outT[:, :])[:, o, e_ * 512:(e_ + 1) * 512],
                                      outb[po][:, :], "dOUT%d" % po)
        wdq_free = b.cnt["sPE"]
    b.wait("scalar", "dOUT0", b.cnt["dOUT0"])
    b.wait("scalar", "dOUT1", b.cnt["dOUT1"])

    with nc.allow_low_precision("bf16 activations/weights by design (tolerance allows)"):
        with nc.Block() as block:
            @block.sync
            def _(e):
                for fn in b.ops["sync"]:
                    fn(e)

            @block.gpsimd
            def _(e):
                for fn in b.ops["gpsimd"]:
                    fn(e)

            @block.vector
            def _(e):
                for fn in b.ops["vector"]:
                    fn(e)

            @block.scalar
            def _(e):
                for fn in b.ops["scalar"]:
                    fn(e)

            @block.tensor
            def _(e):
                for fn in b.ops["tensor"]:
                    fn(e)

    return nc


def prep_inputs(L0_attn_hidden_states, L0_attn_wq, L0_attn_bq, L0_attn_wk, L0_attn_bk,
                L0_attn_wv, L0_attn_bv, L0_attn_wo, L0_attn_bo,
                L0_ffn_w_gate, L0_ffn_w_up, L0_ffn_w_down):
    x = np.asarray(L0_attn_hidden_states, dtype=np.float32)
    com = {
        "wq": np.ascontiguousarray(np.asarray(L0_attn_wq, dtype=np.float32)),
        "wk": np.ascontiguousarray(np.asarray(L0_attn_wk, dtype=np.float32)),
        "wv": np.ascontiguousarray(np.asarray(L0_attn_wv, dtype=np.float32)),
        "wo": np.ascontiguousarray(np.asarray(L0_attn_wo, dtype=np.float32)),
        "bq": np.ascontiguousarray(np.asarray(L0_attn_bq, dtype=np.float32).reshape(NT, 128).T),
        "bk": np.ascontiguousarray(np.asarray(L0_attn_bk, dtype=np.float32).reshape(NT, 128).T),
        "bv": np.ascontiguousarray(np.asarray(L0_attn_bv, dtype=np.float32).reshape(NT, 128).T),
        "bo": np.ascontiguousarray(np.asarray(L0_attn_bo, dtype=np.float32).reshape(NT, 128).T),
        "wgT": np.ascontiguousarray(np.asarray(L0_ffn_w_gate, dtype=np.float32).T),
        "wuT": np.ascontiguousarray(np.asarray(L0_ffn_w_up, dtype=np.float32).T),
        "wdT": np.ascontiguousarray(np.asarray(L0_ffn_w_down, dtype=np.float32).T),
    }
    in_maps = []
    for bi in range(8):
        m = dict(com)
        m["xT"] = np.ascontiguousarray(x[bi].T)
        in_maps.append(m)
    return in_maps


def kernel(**inputs):
    in_maps = prep_inputs(**inputs)
    nc = build(debug=False)
    res = run_bass_kernel_spmd(nc, in_maps, core_ids=list(range(8)))
    return np.stack([np.ascontiguousarray(r["outT"].T) for r in res.results], axis=0)


# version 05
# speedup vs baseline: 1.0060x; 1.0060x over previous
import sys
if "/opt/trn_rl_repo" not in sys.path:
    sys.path.insert(0, "/opt/trn_rl_repo")
import math
import numpy as np
import concourse.bass as bass
import concourse.mybir as mybir
from concourse.bass_utils import run_bass_kernel_spmd

F32 = mybir.dt.float32
BF16 = mybir.dt.bfloat16
AF = mybir.ActivationFunctionType
ALU = mybir.AluOpType
AX = mybir.AxisListType

D = 2048
NT = 16
S = 4096
NH = 16
CT = 410
NCH = 10
MW = 41
NSEL = 410
FH = 8192
FHT = 64
QSCALE = 1.0 / math.sqrt(128.0)
EPS = 1e-5

ENGS = ("sync", "gpsimd", "vector", "scalar", "tensor")


class Bld:
    AUTO = {"vector": "sDVE", "scalar": "sACT"}

    def __init__(self, nc):
        self.nc = nc
        self.ops = {e: [] for e in ENGS}
        self.sems = {}
        self.cnt = {}

    def sem(self, name):
        if name not in self.sems:
            self.sems[name] = self.nc.ctx.enter_context(self.nc.semaphore(name))
            self.cnt[name] = 0
        return self.sems[name]

    def wait(self, eng, sem, val=None):
        s = self.sem(sem)
        v = self.cnt[sem] if val is None else val
        if v > 0:
            self.ops[eng].append(lambda e, s=s, v=v: e.wait_ge(s, v))

    def op(self, eng, fn, inc=None, incby=1):
        a = self.AUTO.get(eng)
        if a is not None:
            s = self.sem(a)
            v = self.cnt[a]
            if v > 0:
                self.ops[eng].append(lambda e, s=s, v=v: e.wait_ge(s, v))
            if inc is None:
                inc = a
                incby = 1
        if inc is None:
            self.ops[eng].append(fn)
            return 0
        s = self.sem(inc)
        self.cnt[inc] += incby
        self.ops[eng].append(lambda e, fn=fn, s=s, b=incby: fn(e).then_inc(s, b))
        return self.cnt[inc]

    def dma(self, eng, dst, src, sem, **kw):
        return self.op(eng, lambda e, d=dst, s_=src, kw=kw: e.dma_start(d, s_, **kw),
                       inc=sem, incby=16)


def r_pit(dram_ap, p=128):
    return dram_ap.rearrange("(n p) t -> p n t", p=p)


def build(debug=False):
    nc = bass.Bass("TRN2", target_bir_lowering=False)

    xT = nc.dram_tensor("xT", [D, S], F32, kind="ExternalInput")
    wq = nc.dram_tensor("wq", [D, D], F32, kind="ExternalInput")
    wk = nc.dram_tensor("wk", [D, D], F32, kind="ExternalInput")
    wv = nc.dram_tensor("wv", [D, D], F32, kind="ExternalInput")
    wo = nc.dram_tensor("wo", [D, D], F32, kind="ExternalInput")
    bq = nc.dram_tensor("bq", [128, NT], F32, kind="ExternalInput")
    bk = nc.dram_tensor("bk", [128, NT], F32, kind="ExternalInput")
    bv = nc.dram_tensor("bv", [128, NT], F32, kind="ExternalInput")
    bo = nc.dram_tensor("bo", [128, NT], F32, kind="ExternalInput")
    wgT = nc.dram_tensor("wgT", [D, FH], F32, kind="ExternalInput")
    wuT = nc.dram_tensor("wuT", [D, FH], F32, kind="ExternalInput")
    wdT = nc.dram_tensor("wdT", [FH, D], F32, kind="ExternalInput")
    outT = nc.dram_tensor("outT", [D, S], F32, kind="ExternalOutput")

    dbg = "ExternalOutput" if debug else "Internal"
    xn_hbm = nc.dram_tensor("xn_hbm", [D, S], BF16, kind=dbg)
    trow_hbm = nc.dram_tensor("trow_hbm", [NCH, NH, 416], BF16, kind=dbg)
    recip_hbm = nc.dram_tensor("recip_hbm", [NCH, NH * MW], BF16, kind=dbg)
    gated_hbm = nc.dram_tensor("gated_hbm", [FH, S], BF16, kind=dbg)
    wd_bf = nc.dram_tensor("wd_bf", [NT, 128, FHT * 128], BF16, kind="Internal")
    if debug:
        qsel_dbg = nc.dram_tensor("qsel_dbg", [128, NT * NSEL], BF16, kind="ExternalOutput")
        snum_dbg = nc.dram_tensor("snum_dbg", [128, NT * NSEL], BF16, kind="ExternalOutput")
        ctx_dbg = nc.dram_tensor("ctx_dbg", [128, NT * 411], BF16, kind="ExternalOutput")
        delta_dbg = nc.dram_tensor("delta_dbg", [128, NT * NSEL], BF16, kind="ExternalOutput")
        u_dbg = nc.dram_tensor("u_dbg", [128, NT], F32, kind="ExternalOutput")
        vsum_dbg = nc.dram_tensor("vsum_dbg", [128, NT], F32, kind="ExternalOutput")

    BASE = 16544

    def al(name, shape, dtype, offset):
        return nc.alloc_sbuf_tensor_at(name, shape, dtype, offset=BASE + offset)

    A = al("A", [128, 32768], BF16, offset=0)
    Bw = al("Bw", [128, 32768], BF16, offset=65536)
    C = al("C", [128, 13120], BF16, offset=131072)
    P1 = al("P1", [128, 6560], BF16, offset=157312)
    P2 = al("P2", [128, 6576], BF16, offset=170432)
    DL = al("DL", [128, 6560], BF16, offset=183584)
    T = al("T", [128, 2048], F32, offset=196704)
    XF0 = al("XF0", [128, 8192], F32, offset=32768)
    XF1 = al("XF1", [128, 8192], F32, offset=65536)
    WDS0 = al("WDS0", [128, 8192], BF16, offset=32768)
    WDS1 = al("WDS1", [128, 8192], BF16, offset=49152)
    xbf = al("xbf", [128, 1024], BF16, offset=196704)
    sqbf = al("sqbf", [128, 512], BF16, offset=198752)
    m_sb = al("m_sb", [128, 512], F32, offset=200800)
    r_sb = al("r_sb", [128, 512], F32, offset=202848)
    t1_sb = al("t1_sb", [128, 512], F32, offset=196704)
    trw = [al("trw%d" % j, [1, 8 * CT], BF16, offset=157312 + 6560 * j) for j in range(2)]
    trr = [al("trr%d" % j, [1, 8 * CT], BF16, offset=170432 + 6560 * j) for j in range(2)]
    gs_sb = al("gs_sb", [128, 2048], BF16, offset=131072)
    gb0 = al("gb0", [128, 2048], BF16, offset=135168)
    gb1 = al("gb1", [128, 2048], BF16, offset=139264)
    outb0 = al("outb0", [128, 512], F32, offset=131072)
    outb1 = al("outb1", [128, 512], F32, offset=133120)
    xb0 = al("xb0", [128, 2048], F32, offset=135168)
    xb1 = al("xb1", [128, 2048], F32, offset=143360)
    sm = 204896
    bq_sb = al("bq_sb", [128, NT], F32, offset=sm); sm += 64
    bk_sb = al("bk_sb", [128, NT], F32, offset=sm); sm += 64
    bv_sb = al("bv_sb", [128, NT], F32, offset=sm); sm += 64
    bo_sb = al("bo_sb", [128, NT], F32, offset=sm); sm += 64
    u_sb = al("u_sb", [128, NT], F32, offset=sm); sm += 64
    vsum_sb = al("vsum_sb", [128, NT], F32, offset=sm); sm += 64
    upre_sb = al("upre_sb", [128, NT], BF16, offset=sm); sm += 32
    den_sb = al("den_sb", [128, 2 * MW], F32, offset=sm); sm += 352
    num_sb = al("num_sb", [128, 2 * MW], F32, offset=sm); sm += 352
    rrow_sb = al("rrow_sb", [1, NH * MW], BF16, offset=sm); sm += 1344
    ones128 = al("ones128", [128, 128], BF16, offset=sm); sm += 256
    ones1 = al("ones1", [1, 128], BF16, offset=sm); sm += 256
    onesk = al("onesk", [128, 8], BF16, offset=sm); sm += 32
    colt_sb = al("colt_sb", [128, 1], F32, offset=sm); sm += 32
    vt_sb = al("vt_sb", [128, 1], F32, offset=sm); sm += 32
    eps_sb = al("eps_sb", [128, 1], F32, offset=sm); sm += 32
    denr_sb = al("denr_sb", [1, 64], F32, offset=sm); sm += 256
    Eb4 = [al("E%d" % j, [128, 2 * CT], BF16, offset=196704 + 1664 * j) for j in range(4)]
    rrw = []
    for j in range(2):
        rrw.append(al("rrw%d" % j, [1, NH * MW], BF16, offset=sm)); sm += 1344
    assert BASE + sm <= 229376 - 64, sm

    psA = nc.alloc_psum_tensor("psA", [128, 2048], F32)
    psS = nc.alloc_psum_tensor("psS", [128, 1024], F32)
    psX = nc.alloc_psum_tensor("psX", [128, 1024], F32)

    b = Bld(nc)

    def wtile(i, o):
        return Bw[:, o * 2048 + i * 128:o * 2048 + (i + 1) * 128]

    def load_w_slabs(wsrc, guard=None, olist=range(NT)):
        w3 = r_pit(wsrc[:, :])
        for o in olist:
            if guard is not None:
                b.wait("gpsimd", guard[0], guard[1])
            dst = Bw[:, o * 2048:(o + 1) * 2048].rearrange("p (i c) -> p i c", i=NT)
            b.dma("gpsimd", dst, w3[:, :, o * 128:(o + 1) * 128], "dW%d" % o)

    s_reg = [(psS, 0), (psS, 512), (psX, 0), (psX, 512)]

    b.dma("gpsimd", bq_sb[:, :], bq[:, :], "dB")
    b.dma("gpsimd", bk_sb[:, :], bk[:, :], "dB")
    b.dma("gpsimd", bv_sb[:, :], bv[:, :], "dB")
    b.dma("gpsimd", bo_sb[:, :], bo[:, :], "dB")
    load_w_slabs(wq, None, range(8, 16))
    b.op("vector", lambda e: e.memset(ones128[:, :], 1.0))
    b.op("vector", lambda e: e.memset(ones1[:, :], 1.0))
    b.op("vector", lambda e: e.memset(onesk[:, :], 1.0))
    b.op("vector", lambda e: e.memset(eps_sb[:, :], EPS))
    b.op("vector", lambda e: e.memset(vsum_sb[:, :], 0.0), inc="sDVE")

    XF = (XF0, XF1)
    xstage = (A[:, 0:8192], A[:, 8192:16384])
    xT3 = r_pit(xT[:, :])
    xn3 = r_pit(xn_hbm[:, :])
    xf_free_mark = [0, 0]
    xstage_free_mark = [0, 0]
    selpos = []
    m0 = 0
    for c in range(8):
        tl0 = (-512 * c) % 10
        cnt_ = (512 - tl0 + 9) // 10
        selpos.append((tl0, cnt_, m0))
        m0 += cnt_
    assert m0 == NSEL

    def s0_dma(c):
        par = c % 2
        b.wait("sync", "sDVE", xf_free_mark[par])
        dst = XF[par][:, :].rearrange("p (n t) -> p n t", n=NT)
        b.dma("sync", dst, xT3[:, :, 512 * c:512 * (c + 1)], "dX%d" % par)

    s0_dma(0)
    s0_dma(1)
    for c in range(8):
        par = c % 2
        b.wait("vector", "dX%d" % par, 16 * (c // 2 + 1))
        for i in range(NT):
            ip = i % 2
            xsl = XF[par][:, 512 * i:512 * (i + 1)]
            if i >= 2:
                b.wait("vector", "sPE", b.cnt["sPE"])
            b.op("vector", lambda e, d=xbf[:, 512 * ip:512 * (ip + 1)], s=xsl:
                 e.tensor_copy(d, s), inc="sDVE")
            b.wait("scalar", "sDVE")
            if i >= 1:
                b.wait("scalar", "sPE", b.cnt["sPE"])
            b.op("scalar", lambda e, s=xbf[:, 512 * ip:512 * (ip + 1)]:
                 e.activation(sqbf[:, :], s, AF.Square), inc="sACT")
            b.wait("tensor", "sDVE")
            b.op("tensor", lambda e, s=xbf[:, 512 * ip:512 * (ip + 1)], st=(i == 0),
                 sp=(i == NT - 1): e.matmul(psX[:, 0:512], ones128[:, :], s, start=st, stop=sp))
            b.wait("tensor", "sACT")
            b.op("tensor", lambda e, st=(i == 0), sp=(i == NT - 1):
                 e.matmul(psX[:, 512:1024], ones128[:, :], sqbf[:, :], start=st, stop=sp),
                 inc="sPE")
        b.wait("vector", "sPE")
        b.op("vector", lambda e: e.tensor_scalar(m_sb[:, :], psX[:, 0:512], 1.0 / D, None, ALU.mult))
        b.op("vector", lambda e: e.tensor_mul(r_sb[:, :], m_sb[:, :], m_sb[:, :]))
        b.op("vector", lambda e: e.scalar_tensor_tensor(
            r_sb[:, :], psX[:, 512:1024], 1.0 / D, r_sb[:, :], ALU.mult, ALU.subtract), inc="sDVE")
        b.wait("scalar", "sDVE")
        b.op("scalar", lambda e: e.activation(r_sb[:, :], r_sb[:, :], AF.Sqrt, bias=eps_sb[:, 0:1]),
             inc="sACT")
        b.wait("vector", "sACT")
        b.op("vector", lambda e: e.reciprocal(r_sb[:, :], r_sb[:, :]))
        b.wait("vector", "dXNW%d" % par, xstage_free_mark[par])
        tl0, cnt_, mst = selpos[c]
        for i in range(NT):
            xsl = XF[par][:, 512 * i:512 * (i + 1)]
            stg = xstage[par][:, 512 * i:512 * (i + 1)]
            b.op("vector", lambda e, x=xsl: e.tensor_sub(t1_sb[:, :], x, m_sb[:, :]))
            b.op("vector", lambda e, d=stg: e.tensor_mul(d, t1_sb[:, :], r_sb[:, :]))
            b.op("vector", lambda e, d=P1[:, i * NSEL + mst:i * NSEL + mst + cnt_],
                 s=xstage[par][:, 512 * i + tl0:512 * (i + 1):10]: e.tensor_copy(d, s))
        xf_free_mark[par] = b.op("vector", lambda e: e.engine_nop(), inc="sDVE")
        b.wait("sync", "sDVE", xf_free_mark[par])
        src = xstage[par][:, :].rearrange("p (n t) -> p n t", n=NT)
        xstage_free_mark[par] = b.dma("sync", xn3[:, :, 512 * c:512 * (c + 1)], src,
                                      "dXNW%d" % par)
        if c + 2 < 8:
            s0_dma(c + 2)
    sel_ready = b.cnt["sDVE"]

    slot_evac = [("sDVE", 0), ("sDVE", 0), ("sDVE", 0), ("sDVE", 0)]

    load_w_slabs(wq, ("sDVE", xf_free_mark[1]), range(0, 8))
    for o in range(NT):
        b.wait("tensor", "dW%d" % o, b.cnt["dW%d" % o])
    b.wait("tensor", "sDVE", sel_ready)
    b.wait("vector", "dB", 64)
    for g in range(4):
        if g > 0:
            b.wait("tensor", "sDVE", b.cnt["sDVE"])
        for i in range(NT):
            for o in range(4 * g, 4 * g + 4):
                last = (i == NT - 1 and o == 4 * g + 3)
                m = b.op("tensor", lambda e, i=i, o=o, st=(i == 0), sp=(i == NT - 1):
                     e.matmul(psA[:, (o % 4) * 512:(o % 4) * 512 + NSEL], wtile(i, o),
                              P1[:, i * NSEL:(i + 1) * NSEL], start=st, stop=sp),
                     inc="sPE" if last else None)
        b.wait("vector", "sPE", m)
        for o in range(4 * g, 4 * g + 4):
            slot_evac[o % 4] = ("sDVE", b.op("vector", lambda e, o=o:
                 e.tensor_scalar(P2[:, o * NSEL:(o + 1) * NSEL],
                                 psA[:, (o % 4) * 512:(o % 4) * 512 + NSEL],
                                 bq_sb[:, o:o + 1], QSCALE, ALU.add, ALU.mult), inc="sDVE"))
    qsel_ready = b.cnt["sDVE"]
    s1_pe_done = b.cnt["sPE"]
    if debug:
        b.wait("sync", "sDVE", qsel_ready)
        b.dma("sync", qsel_dbg[:, :], P2[:, 0:NT * NSEL], "dDBG")
        b.wait("vector", "dDBG", b.cnt["dDBG"])

    Cb = (C[:, 0:NT * CT], C[:, NT * CT:2 * NT * CT])
    xnbuf_free = [0, 0]
    cb_free = [0, 0]
    pms = {}

    def xn_dma(c, dve_guard=0):
        par = c % 2
        b.wait("sync", "sPE", xnbuf_free[par])
        if dve_guard:
            b.wait("sync", "sDVE", dve_guard)
        b.wait("sync", "dXNW0", 64)
        b.wait("sync", "dXNW1", 64)
        dst = A[:, par * NT * CT:(par + 1) * NT * CT].rearrange("p (n t) -> p n t", n=NT)
        L = CT if c < NCH - 1 else S - (NCH - 1) * CT
        b.dma("sync", dst[:, :, 0:L], xn3[:, :, c * CT:c * CT + L], "dXN%d" % par)

    def proj_chunk(c, wcnt, bias_col):
        par = c % 2
        kv_len = CT if c < NCH - 1 else S - (NCH - 1) * CT
        xb_ = A[:, par * NT * CT:(par + 1) * NT * CT]
        b.wait("tensor", "dXN%d" % par, b.cnt["dXN%d" % par])
        b.wait("scalar", "sDVE", cb_free[par])
        for o in range(NT):
            reg = (o % 4) * 512
            b.wait("tensor", slot_evac[o % 4][0], slot_evac[o % 4][1])
            b.wait("tensor", "dW%d" % o, wcnt[o])
            for i in range(NT):
                m = b.op("tensor", lambda e, i=i, o=o, r=reg, L=kv_len, x=xb_,
                         st=(i == 0), sp=(i == NT - 1):
                         e.matmul(psA[:, r:r + L], wtile(i, o), x[:, i * CT:i * CT + L],
                                  start=st, stop=sp),
                         inc="sPE" if i == NT - 1 else None)
            b.wait("scalar", "sPE", m)
            ev = b.op("scalar", lambda e, o=o, r=reg, L=kv_len, bc=bias_col, dst=Cb[par]:
                 e.activation(dst[:, o * CT:o * CT + L], psA[:, r:r + L], AF.Identity,
                              bias=bc[:, o:o + 1]), inc="sACT")
            if kv_len < CT:
                b.op("vector", lambda e, o=o, L=kv_len, dst=Cb[par]:
                     e.memset(dst[:, o * CT + L:(o + 1) * CT], 0.0))
            slot_evac[o % 4] = ("sACT", ev)
        pms[c] = m
        return m

    def kslice3(h, par):
        return Cb[par][:, h * CT:(h + 1) * CT].rearrange("p (m d) -> p d m", m=MW, d=10)

    E_free = [0, 0, 0, 0]
    psR_free = [0, 0, 0, 0]
    trw_free = [0, 0]
    trr_free = [0, 0]

    def scores_chunk(c, evac_mark):
        par = c % 2
        b.wait("vector", "sACT", evac_mark)
        for g in range(8):
            h0 = 2 * g
            rot = g % 4
            pg = g % 2
            hb = h0 // 8
            ps_ = psS if pg == 0 else psX
            Eb = Eb4[rot]
            b.wait("vector", "sPE", E_free[rot])
            q2 = P2[:, h0 * NSEL + MW * c:(h0 + 1) * NSEL + MW * (c + 1)]
            qap = bass.AP(q2.tensor, q2.offset, [[6576, 128], [NSEL, 2], [0, 10], [1, MW]])
            kap = Cb[par][:, h0 * CT:(h0 + 2) * CT].rearrange(
                "p (g m d) -> p g d m", g=2, m=MW, d=10)
            b.op("vector", lambda e, qap=qap, kap=kap, Eb=Eb:
                 e.tensor_mul(Eb[:, :].rearrange("p (g d m) -> p g d m", g=2, d=10),
                              qap, kap), inc="sDVE")
            E_mark = b.cnt["sDVE"]
            b.wait("tensor", "sDVE", E_mark)
            b.wait("tensor", "sDVE", psR_free[pg])
            b.op("tensor", lambda e, Eb=Eb, ps=ps_:
                 e.matmul(ps[0:1, 0:512], onesk[:, 0:1], Eb[:, 0:512],
                          start=True, stop=True))
            E_free[rot] = b.op("tensor", lambda e, Eb=Eb, ps=ps_:
                 e.matmul(ps[0:1, 512:820], onesk[:, 0:1], Eb[:, 512:820],
                          start=True, stop=True), inc="sPE")
            b.wait("scalar", "sPE", E_free[rot])
            b.op("scalar", lambda e, ps=ps_:
                 e.activation(ps[0:1, 0:820], ps[0:1, 0:820], AF.Exp), inc="sACT")
            b.wait("vector", "sACT", b.cnt["sACT"])
            if g % 4 == 0:
                b.wait("vector", "dT%d" % hb, trw_free[hb])
            psR_free[pg] = b.op("vector", lambda e, ps=ps_, t=trw[hb], hl=h0 % 8:
                 e.tensor_scalar(t[0:1, hl * CT:(hl + 2) * CT], ps[0:1, 0:820],
                                 -1.0, None, ALU.add), inc="sDVE")
            if g % 4 == 3:
                m2 = b.cnt["sDVE"]
                b.wait("scalar", "sDVE", m2)
                trw_free[hb] = b.dma("scalar", trow_hbm[c, hb * 8:hb * 8 + 8, 0:CT],
                                     trw[hb][0:1, :].rearrange("o (g m) -> o g m", g=8),
                                     "dT%d" % hb)

    rrow_free = [0]
    def num_chunk(c, evac_mark):
        par = c % 2
        b.wait("vector", "sACT", evac_mark)
        for i in range(NT):
            b.op("vector", lambda e, i=i, src=Cb[par]:
                 e.tensor_reduce(vt_sb[:, :], src[:, i * CT:(i + 1) * CT], AX.X, ALU.add))
            b.op("vector", lambda e, i=i: e.tensor_add(vsum_sb[:, i:i + 1], vsum_sb[:, i:i + 1],
                                                       vt_sb[:, :]))
        for g in range(8):
            h0 = 2 * g
            pg = g % 2
            hb = h0 // 8
            ps_ = psS if pg == 0 else psX
            if h0 % 8 == 0:
                b.wait("scalar", "sPE", trr_free[hb])
                b.dma("scalar", trr[hb][0:1, :].rearrange("o (g m) -> o g m", g=8),
                      trow_hbm[c, hb * 8:hb * 8 + 8, 0:CT], "dTR%d" % hb)
            b.wait("tensor", "dTR%d" % hb, b.cnt["dTR%d" % hb])
            b.wait("tensor", "sDVE", psR_free[pg])
            b.op("tensor", lambda e, t=trr[hb], hl=h0 % 8, ps=ps_:
                 e.matmul(ps[:, 0:512], ones1[:, :], t[0:1, hl * CT:hl * CT + 512],
                          start=True, stop=True))
            trr_free[hb] = b.op("tensor", lambda e, t=trr[hb], hl=h0 % 8, ps=ps_:
                 e.matmul(ps[:, 512:820], ones1[:, :], t[0:1, hl * CT + 512:hl * CT + 820],
                          start=True, stop=True), inc="sPE")
            b.wait("vector", "sPE", trr_free[hb])
            kap = Cb[par][:, h0 * CT:(h0 + 2) * CT].rearrange(
                "p (g m d) -> p g d m", g=2, m=MW, d=10)
            b.op("vector", lambda e, ps=ps_, kap=kap:
                 e.tensor_mul(T[:, 0:820].rearrange("p (g d m) -> p g d m", g=2, d=10),
                              ps[:, 0:820].rearrange("p (g d m) -> p g d m", g=2, d=10),
                              kap))
            b.op("vector", lambda e:
                 e.tensor_reduce(num_sb[:, :],
                                 T[:, 0:820].rearrange("p (g d m) -> p g m d", g=2, d=10),
                                 AX.X, ALU.add))
            psR_free[pg] = b.op("vector", lambda e, ps=ps_:
                 e.tensor_reduce(den_sb[:, :],
                                 ps[:, 0:820].rearrange("p (g d m) -> p g m d", g=2, d=10),
                                 AX.X, ALU.add), inc="sDVE")
            b.op("vector", lambda e: e.tensor_scalar(den_sb[:, :], den_sb[:, :], float(S),
                                                     None, ALU.add))
            b.op("vector", lambda e: e.reciprocal(den_sb[:, :], den_sb[:, :]))
            pap = P1[:, h0 * NSEL + MW * c:(h0 + 1) * NSEL + MW * (c + 1)]
            pap2 = bass.AP(pap.tensor, pap.offset, [[6560, 128], [NSEL, 2], [1, MW]])
            b.op("vector", lambda e, pap2=pap2:
                 e.tensor_mul(pap2, num_sb[:, :].rearrange("p (g m) -> p g m", g=2),
                              den_sb[:, :].rearrange("p (g m) -> p g m", g=2)))
            if g == 0:
                b.wait("vector", "dRW", rrow_free[0])
            b.op("vector", lambda e, h0=h0, t=rrw[c % 2]:
                 e.tensor_copy(t[0:1, h0 * MW:(h0 + 2) * MW], den_sb[0:1, :]), inc="sDVE")
        cb_free[par] = b.cnt["sDVE"]
        m2 = b.cnt["sDVE"]
        b.wait("scalar", "sDVE", m2)
        rrow_free[0] = b.dma("scalar", recip_hbm[c:c + 1, :], rrw[c % 2][0:1, :], "dRW")

    rr_guard = [0]

    wdT3 = r_pit(wdT[:, :])
    WDS = (WDS0, WDS1)
    wds_free = [0, 0]

    def wd_precast(o):
        pw = o % 2
        b.wait("gpsimd", "dWP%d" % pw, wds_free[pw])
        b.dma("gpsimd", WDS[pw][:, :].rearrange("p (g c) -> p g c", g=FHT),
              wdT3[:, :, o * 128:(o + 1) * 128], "dWC")
        b.wait("gpsimd", "dWC", b.cnt["dWC"])
        wds_free[pw] = b.dma("gpsimd", wd_bf[o, :, :], WDS[pw][:, :], "dWP%d" % pw)

    load_w_slabs(wk, ("sPE", s1_pe_done))
    wk_cnt = [b.cnt["dW%d" % o] for o in range(NT)]
    xn_dma(0, dve_guard=sel_ready)
    xn_dma(1, dve_guard=sel_ready)
    ev_mark = {}
    for c in range(NCH):
        pm = proj_chunk(c, wk_cnt, bk_sb)
        ev_mark[c] = b.cnt["sACT"]
        if c + 2 < NCH:
            xnbuf_free[c % 2] = pm
            xn_dma(c + 2)
        if c >= 1:
            scores_chunk(c - 1, ev_mark[c - 1])
        if c < 10:
            wd_precast(c)
    scores_chunk(NCH - 1, ev_mark[NCH - 1])

    load_w_slabs(wv, ("sPE", pms[NCH - 1]))
    wv_cnt = [b.cnt["dW%d" % o] for o in range(NT)]
    for j in range(2):
        b.wait("scalar", "dT%d" % j, b.cnt["dT%d" % j])
    xnbuf_free = [pms[NCH - 2], pms[NCH - 1]]
    xn_dma(0)
    xn_dma(1)
    for c in range(NCH):
        pm = proj_chunk(c, wv_cnt, bv_sb)
        ev_mark[c] = b.cnt["sACT"]
        if c + 2 < NCH:
            xnbuf_free[c % 2] = pm
            xn_dma(c + 2)
        if c >= 1:
            num_chunk(c - 1, ev_mark[c - 1])
            rr_guard[0] = b.cnt["sPE"]
        if c + 10 < NT:
            wd_precast(c + 10)
    num_chunk(NCH - 1, ev_mark[NCH - 1])
    s3_last_pm = pms[NCH - 1]
    if debug:
        m = b.op("vector", lambda e: e.engine_nop(), inc="sDVE")
        b.wait("sync", "sDVE", m)
        b.dma("sync", snum_dbg[:, :], P1[:, :], "dDBG")
        b.dma("sync", vsum_dbg[:, :], vsum_sb[:, :], "dDBG")
        b.wait("vector", "dDBG", b.cnt["dDBG"])

    b.op("vector", lambda e: e.tensor_scalar(upre_sb[:, :], vsum_sb[:, :], 1.0 / S, None,
                                             ALU.mult), inc="sDVE")
    psR_free = [b.cnt["sDVE"]] * 4
    rr_guard2 = b.cnt["sPE"]
    for c in range(NCH):
        b.wait("sync", "dRW", b.cnt["dRW"])
        b.wait("sync", "sPE", rr_guard2)
        b.dma("sync", rrow_sb[0:1, :], recip_hbm[c:c + 1, :], "dRR")
        for h in range(NH):
            rot = h % 4
            ps_, off = s_reg[rot]
            b.wait("tensor", "dRR", b.cnt["dRR"])
            b.wait("tensor", "sDVE", psR_free[rot])
            b.op("tensor", lambda e, h=h, ps=ps_, off=off:
                 e.matmul(ps[:, off:off + MW], ones1[:, :], rrow_sb[0:1, h * MW:(h + 1) * MW],
                          start=True, stop=True), inc="sPE")
            b.wait("vector", "sPE", b.cnt["sPE"])
            psR_free[rot] = b.op("vector", lambda e, h=h, c=c, ps=ps_, off=off:
                 e.scalar_tensor_tensor(P2[:, h * 411 + MW * c:h * 411 + MW * (c + 1)],
                                        ps[:, off:off + MW], vsum_sb[:, h:h + 1],
                                        P1[:, h * NSEL + MW * c:h * NSEL + MW * (c + 1)],
                                        ALU.mult, ALU.add), inc="sDVE")
        rr_guard2 = b.cnt["sPE"]
    b.op("vector", lambda e: e.tensor_copy(P2[:, 410:16 * 411:411], upre_sb[:, :]))
    ctx_ready = b.cnt["sDVE"]
    if debug:
        b.wait("sync", "sDVE", ctx_ready)
        b.dma("sync", ctx_dbg[:, :], P2[:, :], "dDBG")
        b.wait("vector", "dDBG", b.cnt["dDBG"])

    load_w_slabs(wo, ("sPE", s3_last_pm))
    b.wait("tensor", "sDVE", ctx_ready)
    for g in range(4):
        for sl in range(4):
            b.wait("tensor", slot_evac[sl][0], slot_evac[sl][1])
        for o in range(4 * g, 4 * g + 4):
            b.wait("tensor", "dW%d" % o, b.cnt["dW%d" % o])
        for i in range(NT):
            for o in range(4 * g, 4 * g + 4):
                last = (i == NT - 1 and o == 4 * g + 3)
                m = b.op("tensor", lambda e, i=i, o=o, st=(i == 0), sp=(i == NT - 1):
                     e.matmul(psA[:, (o % 4) * 512:(o % 4) * 512 + 411], wtile(i, o),
                              P2[:, i * 411:(i + 1) * 411], start=st, stop=sp),
                     inc="sPE" if last else None)
        b.wait("vector", "sPE", m)
        for o in range(4 * g, 4 * g + 4):
            r = (o % 4) * 512
            b.op("vector", lambda e, o=o, r=r:
                 e.tensor_scalar(u_sb[:, o:o + 1], psA[:, r + 410:r + 411],
                                 bo_sb[:, o:o + 1], None, ALU.add))
            b.op("vector", lambda e, o=o:
                 e.tensor_sub(colt_sb[:, :], bo_sb[:, o:o + 1], u_sb[:, o:o + 1]))
            slot_evac[o % 4] = ("sDVE", b.op("vector", lambda e, o=o, r=r:
                 e.tensor_scalar(DL[:, o * NSEL:(o + 1) * NSEL], psA[:, r:r + NSEL],
                                 colt_sb[:, :], None, ALU.add), inc="sDVE"))
    if debug:
        m = b.cnt["sDVE"]
        b.wait("sync", "sDVE", m)
        b.dma("sync", delta_dbg[:, :], DL[:, :], "dDBG")
        b.dma("sync", u_dbg[:, :], u_sb[:, :], "dDBG")
        b.wait("vector", "dDBG", b.cnt["dDBG"])

    wg3 = r_pit(wgT[:, :])
    wu3 = r_pit(wuT[:, :])
    g3 = r_pit(gated_hbm[:, :])
    slabs = [Bw[:, 0:8192], Bw[:, 8192:16384], Bw[:, 16384:24576], Bw[:, 24576:32768]]
    slab_free = [b.cnt["sPE"], b.cnt["sPE"]]
    gb = (gb0, gb1)
    gb_free = [0, 0]
    gs_free = b.cnt["sDVE"]
    psA_free = 0
    psB_free = b.cnt["sDVE"]
    half_free = b.cnt["sPE"]
    b.wait("tensor", "sDVE", b.cnt["sDVE"])
    b.wait("sync", "dWP0", b.cnt["dWP0"])
    b.wait("sync", "dWP1", b.cnt["dWP1"])
    for half in range(2):
        b.wait("sync", "sPE", half_free)
        b.dma("sync", A[:, :].rearrange("p (n t) -> p n t", n=NT),
              xn3[:, :, half * 2048:(half + 1) * 2048], "dXN0")
        for s_ in range(16):
            ps_ = s_ % 2
            b.wait("gpsimd", "sPE", slab_free[ps_])
            b.dma("gpsimd", slabs[2 * ps_][:, :].rearrange("p (n t) -> p n t", n=NT),
                  wg3[:, :, s_ * 512:(s_ + 1) * 512], "dSL%d" % ps_)
            b.dma("gpsimd", slabs[2 * ps_ + 1][:, :].rearrange("p (n t) -> p n t", n=NT),
                  wu3[:, :, s_ * 512:(s_ + 1) * 512], "dSL%d" % ps_)
            for ht in range(4):
                gidx = half * 64 + s_ * 4 + ht
                pg = gidx % 2
                b.wait("tensor", "dSL%d" % ps_, b.cnt["dSL%d" % ps_])
                b.wait("tensor", "dXN0", b.cnt["dXN0"])
                b.wait("tensor", "sACT", psA_free)
                for i in range(NT):
                    for ch in range(4):
                        gate_done = b.op("tensor", lambda e, i=i, ch=ch, sl=slabs[2 * ps_],
                             ht=ht, st=(i == 0), sp=(i == NT - 1):
                             e.matmul(psA[:, ch * 512:(ch + 1) * 512],
                                      sl[:, i * 512 + ht * 128:i * 512 + (ht + 1) * 128],
                                      A[:, i * 2048 + ch * 512:i * 2048 + (ch + 1) * 512],
                                      start=st, stop=sp),
                             inc="sPE" if (i == NT - 1 and ch == 3) else None)
                b.wait("tensor", "sDVE", psB_free)
                for i in range(NT):
                    for ch in range(4):
                        tgt = psS if ch < 2 else psX
                        up_done = b.op("tensor", lambda e, i=i, ch=ch, sl=slabs[2 * ps_ + 1],
                             ht=ht, t=tgt, st=(i == 0), sp=(i == NT - 1):
                             e.matmul(t[:, (ch % 2) * 512:(ch % 2 + 1) * 512],
                                      sl[:, i * 512 + ht * 128:i * 512 + (ht + 1) * 128],
                                      A[:, i * 2048 + ch * 512:i * 2048 + (ch + 1) * 512],
                                      start=st, stop=sp),
                             inc="sPE" if (i == NT - 1 and ch == 3) else None)
                b.wait("scalar", "sPE", gate_done)
                b.wait("scalar", "sDVE", gs_free)
                psA_free = b.op("scalar", lambda e:
                     e.activation(gs_sb[:, :], psA[:, 0:2048], AF.Silu), inc="sACT")
                b.wait("vector", "sPE", up_done)
                b.wait("vector", "sACT", psA_free)
                b.wait("vector", "dGW%d" % pg, gb_free[pg])
                b.op("vector", lambda e, d=gb[pg]:
                     e.tensor_mul(d[:, 0:1024], psS[:, 0:1024], gs_sb[:, 0:1024]))
                psB_free = b.op("vector", lambda e, d=gb[pg]:
                     e.tensor_mul(d[:, 1024:2048], psX[:, 0:1024], gs_sb[:, 1024:2048]),
                     inc="sDVE")
                gs_free = psB_free
                b.wait("scalar", "sDVE", psB_free)
                gb_free[pg] = b.dma("scalar", g3[:, s_ * 4 + ht, half * 2048:(half + 1) * 2048],
                                    gb[pg][:, :], "dGW%d" % pg)
            slab_free[ps_] = b.cnt["sPE"]
        half_free = b.cnt["sPE"]

    xbb = (xb0, xb1)
    outb = (outb0, outb1)
    wdq_free = b.cnt["sPE"]
    ge_free = [b.cnt["sPE"], b.cnt["sPE"]]
    xb_free = [0, 0]
    outb_free = [0, 0]
    psA_slot_free = [0, 0, 0, 0]
    b.wait("sync", "dGW0", b.cnt["dGW0"])
    b.wait("sync", "dGW1", b.cnt["dGW1"])
    b.wait("scalar", "dGW0", b.cnt["dGW0"])
    b.wait("scalar", "dGW1", b.cnt["dGW1"])
    b.wait("tensor", "sDVE", b.cnt["sDVE"])
    b.wait("tensor", "sACT", b.cnt["sACT"])
    for dq in range(4):
        b.wait("sync", "sPE", wdq_free)
        for dtl in range(4):
            b.dma("sync", Bw[:, dtl * 8192:(dtl + 1) * 8192], wd_bf[dq * 4 + dtl, :, :], "dWD")
        for e_ in range(8):
            pe_ = e_ % 2
            b.wait("scalar", "sDVE", xb_free[pe_])
            b.dma("scalar", xbb[pe_][:, :].rearrange("p (n t) -> p n t", n=4),
                  xT3[:, 4 * dq:4 * dq + 4, e_ * 512:(e_ + 1) * 512], "dXS%d" % pe_)
            for hh in range(2):
                q_ = "sync" if hh == 0 else "scalar"
                b.wait(q_, "sPE", ge_free[hh])
                dst = A[:, hh * 16384:(hh + 1) * 16384].rearrange("p (n t) -> p n t", n=32)
                b.dma(q_, dst, g3[:, hh * 32:(hh + 1) * 32, e_ * 512:(e_ + 1) * 512],
                      "dGE%d" % hh)
                b.wait("tensor", "dGE%d" % hh, b.cnt["dGE%d" % hh])
                if hh == 0:
                    b.wait("tensor", "dWD", b.cnt["dWD"])
                    for dt in range(4):
                        b.wait("tensor", "sDVE", psA_slot_free[dt])
                for h32 in range(32):
                    g_ = hh * 32 + h32
                    for dt in range(4):
                        ge_free[hh] = b.op("tensor", lambda e, g=g_, dt=dt, hh=hh, h32=h32,
                             st=(hh == 0 and h32 == 0), sp=(hh == 1 and h32 == 31):
                             e.matmul(psA[:, dt * 512:(dt + 1) * 512],
                                      Bw[:, dt * 8192 + g * 128:dt * 8192 + (g + 1) * 128],
                                      A[:, hh * 16384 + h32 * 512:hh * 16384 + (h32 + 1) * 512],
                                      start=st, stop=sp),
                             inc="sPE" if (h32 == 31 and dt == 3) else None)
            mm_done = ge_free[1]
            tl0 = (-512 * e_) % 10
            msta = (512 * e_ + 9) // 10
            scnt = (512 * (e_ + 1) - 1 - (512 * e_ + tl0)) // 10 + 1
            for dt in range(4):
                o = dq * 4 + dt
                po = (e_ * 4 + dt) % 2
                b.wait("vector", "sPE", mm_done)
                b.wait("vector", "dXS%d" % pe_, b.cnt["dXS%d" % pe_])
                b.wait("vector", "dOUT%d" % po, outb_free[po])
                b.op("vector", lambda e, dt=dt, o=o, po=po, pe2=pe_:
                     e.scalar_tensor_tensor(outb[po][:, :], psA[:, dt * 512:(dt + 1) * 512],
                                            u_sb[:, o:o + 1], xbb[pe2][:, dt * 512:(dt + 1) * 512],
                                            ALU.add, ALU.add))
                psA_slot_free[dt] = b.op("vector", lambda e, o=o, po=po, t0=tl0, ms=msta, cn=scnt:
                     e.tensor_add(outb[po][:, t0:t0 + (cn - 1) * 10 + 1:10],
                                  outb[po][:, t0:t0 + (cn - 1) * 10 + 1:10],
                                  DL[:, o * NSEL + ms:o * NSEL + ms + cn]), inc="sDVE")
                xb_free[pe_] = psA_slot_free[dt]
                b.wait("scalar", "sDVE", psA_slot_free[dt])
                outb_free[po] = b.dma("scalar", r_pit(outT[:, :])[:, o, e_ * 512:(e_ + 1) * 512],
                                      outb[po][:, :], "dOUT%d" % po)
        wdq_free = b.cnt["sPE"]
    b.wait("scalar", "dOUT0", b.cnt["dOUT0"])
    b.wait("scalar", "dOUT1", b.cnt["dOUT1"])

    with nc.allow_low_precision("bf16 activations/weights by design (tolerance allows)"):
        with nc.Block() as block:
            @block.sync
            def _(e):
                for fn in b.ops["sync"]:
                    fn(e)

            @block.gpsimd
            def _(e):
                for fn in b.ops["gpsimd"]:
                    fn(e)

            @block.vector
            def _(e):
                for fn in b.ops["vector"]:
                    fn(e)

            @block.scalar
            def _(e):
                for fn in b.ops["scalar"]:
                    fn(e)

            @block.tensor
            def _(e):
                for fn in b.ops["tensor"]:
                    fn(e)

    return nc


def prep_inputs(L0_attn_hidden_states, L0_attn_wq, L0_attn_bq, L0_attn_wk, L0_attn_bk,
                L0_attn_wv, L0_attn_bv, L0_attn_wo, L0_attn_bo,
                L0_ffn_w_gate, L0_ffn_w_up, L0_ffn_w_down):
    x = np.asarray(L0_attn_hidden_states, dtype=np.float32)
    com = {
        "wq": np.ascontiguousarray(np.asarray(L0_attn_wq, dtype=np.float32)),
        "wk": np.ascontiguousarray(np.asarray(L0_attn_wk, dtype=np.float32)),
        "wv": np.ascontiguousarray(np.asarray(L0_attn_wv, dtype=np.float32)),
        "wo": np.ascontiguousarray(np.asarray(L0_attn_wo, dtype=np.float32)),
        "bq": np.ascontiguousarray(np.asarray(L0_attn_bq, dtype=np.float32).reshape(NT, 128).T),
        "bk": np.ascontiguousarray(np.asarray(L0_attn_bk, dtype=np.float32).reshape(NT, 128).T),
        "bv": np.ascontiguousarray(np.asarray(L0_attn_bv, dtype=np.float32).reshape(NT, 128).T),
        "bo": np.ascontiguousarray(np.asarray(L0_attn_bo, dtype=np.float32).reshape(NT, 128).T),
        "wgT": np.ascontiguousarray(np.asarray(L0_ffn_w_gate, dtype=np.float32).T),
        "wuT": np.ascontiguousarray(np.asarray(L0_ffn_w_up, dtype=np.float32).T),
        "wdT": np.ascontiguousarray(np.asarray(L0_ffn_w_down, dtype=np.float32).T),
    }
    in_maps = []
    for bi in range(8):
        m = dict(com)
        m["xT"] = np.ascontiguousarray(x[bi].T)
        in_maps.append(m)
    return in_maps


def kernel(**inputs):
    in_maps = prep_inputs(**inputs)
    nc = build(debug=False)
    res = run_bass_kernel_spmd(nc, in_maps, core_ids=list(range(8)))
    return np.stack([np.ascontiguousarray(r["outT"].T) for r in res.results], axis=0)


# version 06
# speedup vs baseline: 1.0077x; 1.0017x over previous
import sys
if "/opt/trn_rl_repo" not in sys.path:
    sys.path.insert(0, "/opt/trn_rl_repo")
import math
import numpy as np
import concourse.bass as bass
import concourse.mybir as mybir
from concourse.bass_utils import run_bass_kernel_spmd

F32 = mybir.dt.float32
BF16 = mybir.dt.bfloat16
AF = mybir.ActivationFunctionType
ALU = mybir.AluOpType
AX = mybir.AxisListType

D = 2048
NT = 16
S = 4096
NH = 16
CT = 410
NCH = 10
MW = 41
NSEL = 410
FH = 8192
FHT = 64
QSCALE = 1.0 / math.sqrt(128.0)
EPS = 1e-5

ENGS = ("sync", "gpsimd", "vector", "scalar", "tensor")


class Bld:
    AUTO = {"vector": "sDVE", "scalar": "sACT"}

    def __init__(self, nc):
        self.nc = nc
        self.ops = {e: [] for e in ENGS}
        self.sems = {}
        self.cnt = {}

    def sem(self, name):
        if name not in self.sems:
            self.sems[name] = self.nc.ctx.enter_context(self.nc.semaphore(name))
            self.cnt[name] = 0
        return self.sems[name]

    def wait(self, eng, sem, val=None):
        s = self.sem(sem)
        v = self.cnt[sem] if val is None else val
        if v > 0:
            self.ops[eng].append(lambda e, s=s, v=v: e.wait_ge(s, v))

    def op(self, eng, fn, inc=None, incby=1):
        a = self.AUTO.get(eng)
        if a is not None:
            s = self.sem(a)
            v = self.cnt[a]
            if v > 0:
                self.ops[eng].append(lambda e, s=s, v=v: e.wait_ge(s, v))
            if inc is None:
                inc = a
                incby = 1
        if inc is None:
            self.ops[eng].append(fn)
            return 0
        s = self.sem(inc)
        self.cnt[inc] += incby
        self.ops[eng].append(lambda e, fn=fn, s=s, b=incby: fn(e).then_inc(s, b))
        return self.cnt[inc]

    def dma(self, eng, dst, src, sem, **kw):
        return self.op(eng, lambda e, d=dst, s_=src, kw=kw: e.dma_start(d, s_, **kw),
                       inc=sem, incby=16)


def r_pit(dram_ap, p=128):
    return dram_ap.rearrange("(n p) t -> p n t", p=p)


def build(debug=False):
    nc = bass.Bass("TRN2", target_bir_lowering=False)

    xT = nc.dram_tensor("xT", [D, S], F32, kind="ExternalInput")
    wq = nc.dram_tensor("wq", [D, D], F32, kind="ExternalInput")
    wk = nc.dram_tensor("wk", [D, D], F32, kind="ExternalInput")
    wv = nc.dram_tensor("wv", [D, D], F32, kind="ExternalInput")
    wo = nc.dram_tensor("wo", [D, D], F32, kind="ExternalInput")
    bq = nc.dram_tensor("bq", [128, NT], F32, kind="ExternalInput")
    bk = nc.dram_tensor("bk", [128, NT], F32, kind="ExternalInput")
    bv = nc.dram_tensor("bv", [128, NT], F32, kind="ExternalInput")
    bo = nc.dram_tensor("bo", [128, NT], F32, kind="ExternalInput")
    wgT = nc.dram_tensor("wgT", [D, FH], F32, kind="ExternalInput")
    wuT = nc.dram_tensor("wuT", [D, FH], F32, kind="ExternalInput")
    wdT = nc.dram_tensor("wdT", [FH, D], F32, kind="ExternalInput")
    outT = nc.dram_tensor("outT", [D, S], F32, kind="ExternalOutput")

    dbg = "ExternalOutput" if debug else "Internal"
    xn_hbm = nc.dram_tensor("xn_hbm", [D, S], BF16, kind=dbg)
    trow_hbm = nc.dram_tensor("trow_hbm", [NCH, NH, 416], BF16, kind=dbg)
    recip_hbm = nc.dram_tensor("recip_hbm", [NCH, NH * MW], BF16, kind=dbg)
    gated_hbm = nc.dram_tensor("gated_hbm", [FH, S], BF16, kind=dbg)
    wd_bf = nc.dram_tensor("wd_bf", [NT, 128, FHT * 128], BF16, kind="Internal")
    if debug:
        qsel_dbg = nc.dram_tensor("qsel_dbg", [128, NT * NSEL], BF16, kind="ExternalOutput")
        snum_dbg = nc.dram_tensor("snum_dbg", [128, NT * NSEL], BF16, kind="ExternalOutput")
        ctx_dbg = nc.dram_tensor("ctx_dbg", [128, NT * 411], BF16, kind="ExternalOutput")
        delta_dbg = nc.dram_tensor("delta_dbg", [128, NT * NSEL], BF16, kind="ExternalOutput")
        u_dbg = nc.dram_tensor("u_dbg", [128, NT], F32, kind="ExternalOutput")
        vsum_dbg = nc.dram_tensor("vsum_dbg", [128, NT], F32, kind="ExternalOutput")

    BASE = 16544

    def al(name, shape, dtype, offset):
        return nc.alloc_sbuf_tensor_at(name, shape, dtype, offset=BASE + offset)

    A = al("A", [128, 32768], BF16, offset=0)
    Bw = al("Bw", [128, 32768], BF16, offset=65536)
    C = al("C", [128, 13120], BF16, offset=131072)
    P1 = al("P1", [128, 6560], BF16, offset=157312)
    P2 = al("P2", [128, 6576], BF16, offset=170432)
    DL = al("DL", [128, 6560], BF16, offset=183584)
    T = al("T", [128, 2048], F32, offset=196704)
    XF0 = al("XF0", [128, 8192], F32, offset=32768)
    XF1 = al("XF1", [128, 8192], F32, offset=65536)
    WDS0 = al("WDS0", [128, 8192], BF16, offset=32768)
    WDS1 = al("WDS1", [128, 8192], BF16, offset=49152)
    m_sb = al("m_sb", [128, 512], F32, offset=200800)
    r_sb = al("r_sb", [128, 512], F32, offset=202848)
    t1_sb = None
    trw = [al("trw%d" % j, [1, 8 * CT], BF16, offset=157312 + 6560 * j) for j in range(2)]
    trr = [al("trr%d" % j, [1, 8 * CT], BF16, offset=170432 + 6560 * j) for j in range(2)]
    gs_sb = al("gs_sb", [128, 2048], BF16, offset=131072)
    gb0 = al("gb0", [128, 2048], BF16, offset=135168)
    gb1 = al("gb1", [128, 2048], BF16, offset=139264)
    outb0 = al("outb0", [128, 512], F32, offset=131072)
    outb1 = al("outb1", [128, 512], F32, offset=133120)
    xb0 = al("xb0", [128, 2048], F32, offset=135168)
    xb1 = al("xb1", [128, 2048], F32, offset=143360)
    sm = 204896
    bq_sb = al("bq_sb", [128, NT], F32, offset=sm); sm += 64
    bk_sb = al("bk_sb", [128, NT], F32, offset=sm); sm += 64
    bv_sb = al("bv_sb", [128, NT], F32, offset=sm); sm += 64
    bo_sb = al("bo_sb", [128, NT], F32, offset=sm); sm += 64
    u_sb = al("u_sb", [128, NT], F32, offset=sm); sm += 64
    vsum_sb = al("vsum_sb", [128, NT], F32, offset=sm); sm += 64
    upre_sb = al("upre_sb", [128, NT], BF16, offset=sm); sm += 32
    den_sb = al("den_sb", [128, 2 * MW], F32, offset=sm); sm += 352
    num_sb = al("num_sb", [128, 2 * MW], F32, offset=sm); sm += 352
    rrow_sb = al("rrow_sb", [1, NH * MW], BF16, offset=sm); sm += 1344
    ones128 = al("ones128", [128, 128], BF16, offset=sm); sm += 256
    ones1 = al("ones1", [1, 128], BF16, offset=sm); sm += 256
    onesk = al("onesk", [128, 8], BF16, offset=sm); sm += 32
    colt_sb = al("colt_sb", [128, 1], F32, offset=sm); sm += 32
    vt_sb = al("vt_sb", [128, 1], F32, offset=sm); sm += 32
    eps_sb = al("eps_sb", [128, 1], F32, offset=sm); sm += 32
    denr_sb = al("denr_sb", [1, 64], F32, offset=sm); sm += 256
    t1_sb = al("t1_sb", [128, 512], F32, offset=183584)
    ones128f = al("ones128f", [128, 128], F32, offset=sm); sm += 512
    Eb4 = [al("E%d" % j, [128, 2 * CT], BF16, offset=196704 + 1664 * j) for j in range(4)]
    rrw = []
    for j in range(2):
        rrw.append(al("rrw%d" % j, [1, NH * MW], BF16, offset=sm)); sm += 1344
    assert BASE + sm <= 229376 - 64, sm

    psA = nc.alloc_psum_tensor("psA", [128, 2048], F32)
    psS = nc.alloc_psum_tensor("psS", [128, 1024], F32)
    psX = nc.alloc_psum_tensor("psX", [128, 1024], F32)

    b = Bld(nc)

    def wtile(i, o):
        return Bw[:, o * 2048 + i * 128:o * 2048 + (i + 1) * 128]

    def load_w_slabs(wsrc, guard=None, olist=range(NT)):
        w3 = r_pit(wsrc[:, :])
        for o in olist:
            if guard is not None:
                b.wait("gpsimd", guard[0], guard[1])
            dst = Bw[:, o * 2048:(o + 1) * 2048].rearrange("p (i c) -> p i c", i=NT)
            b.dma("gpsimd", dst, w3[:, :, o * 128:(o + 1) * 128], "dW%d" % o)

    s_reg = [(psS, 0), (psS, 512), (psX, 0), (psX, 512)]

    b.dma("gpsimd", bq_sb[:, :], bq[:, :], "dB")
    b.dma("gpsimd", bk_sb[:, :], bk[:, :], "dB")
    b.dma("gpsimd", bv_sb[:, :], bv[:, :], "dB")
    b.dma("gpsimd", bo_sb[:, :], bo[:, :], "dB")
    load_w_slabs(wq, None, range(8, 16))
    b.op("vector", lambda e: e.memset(ones128[:, :], 1.0))
    b.op("vector", lambda e: e.memset(ones128f[:, :], 1.0))
    b.op("vector", lambda e: e.memset(ones1[:, :], 1.0))
    b.op("vector", lambda e: e.memset(onesk[:, :], 1.0))
    b.op("vector", lambda e: e.memset(eps_sb[:, :], EPS))
    b.op("vector", lambda e: e.memset(vsum_sb[:, :], 0.0), inc="sDVE")

    XF = (XF0, XF1)
    xstage = (A[:, 0:8192], A[:, 8192:16384])
    xT3 = r_pit(xT[:, :])
    xn3 = r_pit(xn_hbm[:, :])
    xf_free_mark = [0, 0]
    xstage_free_mark = [0, 0]
    selpos = []
    m0 = 0
    for c in range(8):
        tl0 = (-512 * c) % 10
        cnt_ = (512 - tl0 + 9) // 10
        selpos.append((tl0, cnt_, m0))
        m0 += cnt_
    assert m0 == NSEL

    def s0_dma(c):
        par = c % 2
        b.wait("sync", "sDVE", xf_free_mark[par])
        dst = XF[par][:, :].rearrange("p (n t) -> p n t", n=NT)
        b.dma("sync", dst, xT3[:, :, 512 * c:512 * (c + 1)], "dX%d" % par)

    s0_dma(0)
    s0_dma(1)
    sq32 = (al("sq32a", [128, 512], F32, offset=196704),
            al("sq32b", [128, 512], F32, offset=198752))
    mmk = [0] * NT
    statfree = 0
    b.wait("tensor", "sDVE", b.cnt["sDVE"])
    for c in range(8):
        par = c % 2
        for i in range(NT):
            ip = i % 2
            xsl = XF[par][:, 512 * i:512 * (i + 1)]
            if i >= 2:
                b.wait("scalar", "sPE", mmk[i - 2])
            if i == 0:
                b.wait("scalar", "dX%d" % par, 16 * (c // 2 + 1))
            b.op("scalar", lambda e, s=xsl, d=sq32[ip]:
                 e.activation(d[:, :], s, AF.Square), inc="sACT")
            if i == 0:
                b.wait("tensor", "dX%d" % par, 16 * (c // 2 + 1))
                b.wait("tensor", "sDVE", statfree)
            b.op("tensor", lambda e, s=xsl, st=(i == 0), sp=(i == NT - 1):
                 e.matmul(psX[:, 0:512], ones128f[:, :], s, start=st, stop=sp))
            b.wait("tensor", "sACT")
            mmk[i] = b.op("tensor", lambda e, d=sq32[ip], st=(i == 0), sp=(i == NT - 1):
                 e.matmul(psX[:, 512:1024], ones128f[:, :], d[:, :], start=st, stop=sp),
                 inc="sPE")
        b.wait("vector", "sPE", mmk[NT - 1])
        b.op("vector", lambda e: e.tensor_scalar(m_sb[:, :], psX[:, 0:512], 1.0 / D, None, ALU.mult))
        b.op("vector", lambda e: e.tensor_mul(r_sb[:, :], m_sb[:, :], m_sb[:, :]))
        statfree = b.op("vector", lambda e: e.scalar_tensor_tensor(
            r_sb[:, :], psX[:, 512:1024], 1.0 / D, r_sb[:, :], ALU.mult, ALU.subtract), inc="sDVE")
        b.wait("scalar", "sDVE", statfree)
        b.op("scalar", lambda e: e.activation(r_sb[:, :], r_sb[:, :], AF.Sqrt, bias=eps_sb[:, 0:1]),
             inc="sACT")
        b.wait("vector", "sACT")
        b.op("vector", lambda e: e.reciprocal(r_sb[:, :], r_sb[:, :]))
        b.wait("vector", "dXNW%d" % par, xstage_free_mark[par])
        tl0, cnt_, mst = selpos[c]
        for i in range(NT):
            xsl = XF[par][:, 512 * i:512 * (i + 1)]
            stg = xstage[par][:, 512 * i:512 * (i + 1)]
            b.op("vector", lambda e, x=xsl: e.tensor_sub(t1_sb[:, :], x, m_sb[:, :]))
            b.op("vector", lambda e, d=stg: e.tensor_mul(d, t1_sb[:, :], r_sb[:, :]))
            b.op("vector", lambda e, d=P1[:, i * NSEL + mst:i * NSEL + mst + cnt_],
                 s=xstage[par][:, 512 * i + tl0:512 * (i + 1):10]: e.tensor_copy(d, s))
        xf_free_mark[par] = b.op("vector", lambda e: e.engine_nop(), inc="sDVE")
        b.wait("sync", "sDVE", xf_free_mark[par])
        src2 = xstage[par][:, :].rearrange("p (n t) -> p n t", n=NT)
        xstage_free_mark[par] = b.dma("sync", xn3[:, :, 512 * c:512 * (c + 1)], src2,
                                      "dXNW%d" % par)
        if c + 2 < 8:
            s0_dma(c + 2)
    sel_ready = b.cnt["sDVE"]

    slot_evac = [("sDVE", 0), ("sDVE", 0), ("sDVE", 0), ("sDVE", 0)]

    load_w_slabs(wq, ("sDVE", xf_free_mark[1]), range(0, 8))
    b.wait("tensor", "sDVE", sel_ready)
    b.wait("vector", "dB", 64)
    first_g = True
    for g in (2, 3, 0, 1):
        if not first_g:
            b.wait("tensor", "sDVE", b.cnt["sDVE"])
        first_g = False
        for o in range(4 * g, 4 * g + 4):
            b.wait("tensor", "dW%d" % o, b.cnt["dW%d" % o])
        for i in range(NT):
            for o in range(4 * g, 4 * g + 4):
                last = (i == NT - 1 and o == 4 * g + 3)
                m = b.op("tensor", lambda e, i=i, o=o, st=(i == 0), sp=(i == NT - 1):
                     e.matmul(psA[:, (o % 4) * 512:(o % 4) * 512 + NSEL], wtile(i, o),
                              P1[:, i * NSEL:(i + 1) * NSEL], start=st, stop=sp),
                     inc="sPE" if last else None)
        b.wait("vector", "sPE", m)
        for o in range(4 * g, 4 * g + 4):
            slot_evac[o % 4] = ("sDVE", b.op("vector", lambda e, o=o:
                 e.tensor_scalar(P2[:, o * NSEL:(o + 1) * NSEL],
                                 psA[:, (o % 4) * 512:(o % 4) * 512 + NSEL],
                                 bq_sb[:, o:o + 1], QSCALE, ALU.add, ALU.mult), inc="sDVE"))
    qsel_ready = b.cnt["sDVE"]
    s1_pe_done = b.cnt["sPE"]
    if debug:
        b.wait("sync", "sDVE", qsel_ready)
        b.dma("sync", qsel_dbg[:, :], P2[:, 0:NT * NSEL], "dDBG")
        b.wait("vector", "dDBG", b.cnt["dDBG"])

    Cb = (C[:, 0:NT * CT], C[:, NT * CT:2 * NT * CT])
    xnbuf_free = [0, 0]
    cb_free = [0, 0]
    pms = {}

    def xn_dma(c, dve_guard=0):
        par = c % 2
        b.wait("sync", "sPE", xnbuf_free[par])
        if dve_guard:
            b.wait("sync", "sDVE", dve_guard)
        b.wait("sync", "dXNW0", 64)
        b.wait("sync", "dXNW1", 64)
        dst = A[:, par * NT * CT:(par + 1) * NT * CT].rearrange("p (n t) -> p n t", n=NT)
        L = CT if c < NCH - 1 else S - (NCH - 1) * CT
        b.dma("sync", dst[:, :, 0:L], xn3[:, :, c * CT:c * CT + L], "dXN%d" % par)

    def proj_chunk(c, wcnt, bias_col):
        par = c % 2
        kv_len = CT if c < NCH - 1 else S - (NCH - 1) * CT
        xb_ = A[:, par * NT * CT:(par + 1) * NT * CT]
        b.wait("tensor", "dXN%d" % par, b.cnt["dXN%d" % par])
        b.wait("scalar", "sDVE", cb_free[par])
        for o in range(NT):
            reg = (o % 4) * 512
            b.wait("tensor", slot_evac[o % 4][0], slot_evac[o % 4][1])
            b.wait("tensor", "dW%d" % o, wcnt[o])
            for i in range(NT):
                m = b.op("tensor", lambda e, i=i, o=o, r=reg, L=kv_len, x=xb_,
                         st=(i == 0), sp=(i == NT - 1):
                         e.matmul(psA[:, r:r + L], wtile(i, o), x[:, i * CT:i * CT + L],
                                  start=st, stop=sp),
                         inc="sPE" if i == NT - 1 else None)
            b.wait("scalar", "sPE", m)
            ev = b.op("scalar", lambda e, o=o, r=reg, L=kv_len, bc=bias_col, dst=Cb[par]:
                 e.activation(dst[:, o * CT:o * CT + L], psA[:, r:r + L], AF.Identity,
                              bias=bc[:, o:o + 1]), inc="sACT")
            if kv_len < CT:
                b.op("vector", lambda e, o=o, L=kv_len, dst=Cb[par]:
                     e.memset(dst[:, o * CT + L:(o + 1) * CT], 0.0))
            slot_evac[o % 4] = ("sACT", ev)
        pms[c] = m
        return m

    def kslice3(h, par):
        return Cb[par][:, h * CT:(h + 1) * CT].rearrange("p (m d) -> p d m", m=MW, d=10)

    E_free = [0, 0, 0, 0]
    psR_free = [0, 0, 0, 0]
    trw_free = [0, 0]
    trr_free = [0, 0]

    def scores_chunk(c, evac_mark):
        par = c % 2
        b.wait("vector", "sACT", evac_mark)
        for g in range(8):
            h0 = 2 * g
            rot = g % 4
            pg = g % 2
            hb = h0 // 8
            ps_ = psS if pg == 0 else psX
            Eb = Eb4[rot]
            b.wait("vector", "sPE", E_free[rot])
            q2 = P2[:, h0 * NSEL + MW * c:(h0 + 1) * NSEL + MW * (c + 1)]
            qap = bass.AP(q2.tensor, q2.offset, [[6576, 128], [NSEL, 2], [0, 10], [1, MW]])
            kap = Cb[par][:, h0 * CT:(h0 + 2) * CT].rearrange(
                "p (g m d) -> p g d m", g=2, m=MW, d=10)
            b.op("vector", lambda e, qap=qap, kap=kap, Eb=Eb:
                 e.tensor_mul(Eb[:, :].rearrange("p (g d m) -> p g d m", g=2, d=10),
                              qap, kap), inc="sDVE")
            E_mark = b.cnt["sDVE"]
            b.wait("tensor", "sDVE", E_mark)
            b.wait("tensor", "sDVE", psR_free[pg])
            b.op("tensor", lambda e, Eb=Eb, ps=ps_:
                 e.matmul(ps[0:1, 0:512], onesk[:, 0:1], Eb[:, 0:512],
                          start=True, stop=True))
            E_free[rot] = b.op("tensor", lambda e, Eb=Eb, ps=ps_:
                 e.matmul(ps[0:1, 512:820], onesk[:, 0:1], Eb[:, 512:820],
                          start=True, stop=True), inc="sPE")
            b.wait("scalar", "sPE", E_free[rot])
            b.op("scalar", lambda e, ps=ps_:
                 e.activation(ps[0:1, 0:820], ps[0:1, 0:820], AF.Exp), inc="sACT")
            b.wait("vector", "sACT", b.cnt["sACT"])
            if g % 4 == 0:
                b.wait("vector", "dT%d" % hb, trw_free[hb])
            psR_free[pg] = b.op("vector", lambda e, ps=ps_, t=trw[hb], hl=h0 % 8:
                 e.tensor_scalar(t[0:1, hl * CT:(hl + 2) * CT], ps[0:1, 0:820],
                                 -1.0, None, ALU.add), inc="sDVE")
            if g % 4 == 3:
                m2 = b.cnt["sDVE"]
                b.wait("scalar", "sDVE", m2)
                trw_free[hb] = b.dma("scalar", trow_hbm[c, hb * 8:hb * 8 + 8, 0:CT],
                                     trw[hb][0:1, :].rearrange("o (g m) -> o g m", g=8),
                                     "dT%d" % hb)

    rrow_free = [0]
    def num_chunk(c, evac_mark):
        par = c % 2
        b.wait("vector", "sACT", evac_mark)
        for i in range(NT):
            b.op("vector", lambda e, i=i, src=Cb[par]:
                 e.tensor_reduce(vt_sb[:, :], src[:, i * CT:(i + 1) * CT], AX.X, ALU.add))
            b.op("vector", lambda e, i=i: e.tensor_add(vsum_sb[:, i:i + 1], vsum_sb[:, i:i + 1],
                                                       vt_sb[:, :]))
        for g in range(8):
            h0 = 2 * g
            pg = g % 2
            hb = h0 // 8
            ps_ = psS if pg == 0 else psX
            if h0 % 8 == 0:
                b.wait("scalar", "sPE", trr_free[hb])
                b.dma("scalar", trr[hb][0:1, :].rearrange("o (g m) -> o g m", g=8),
                      trow_hbm[c, hb * 8:hb * 8 + 8, 0:CT], "dTR%d" % hb)
            b.wait("tensor", "dTR%d" % hb, b.cnt["dTR%d" % hb])
            b.wait("tensor", "sDVE", psR_free[pg])
            b.op("tensor", lambda e, t=trr[hb], hl=h0 % 8, ps=ps_:
                 e.matmul(ps[:, 0:512], ones1[:, :], t[0:1, hl * CT:hl * CT + 512],
                          start=True, stop=True))
            trr_free[hb] = b.op("tensor", lambda e, t=trr[hb], hl=h0 % 8, ps=ps_:
                 e.matmul(ps[:, 512:820], ones1[:, :], t[0:1, hl * CT + 512:hl * CT + 820],
                          start=True, stop=True), inc="sPE")
            b.wait("vector", "sPE", trr_free[hb])
            kap = Cb[par][:, h0 * CT:(h0 + 2) * CT].rearrange(
                "p (g m d) -> p g d m", g=2, m=MW, d=10)
            b.op("vector", lambda e, ps=ps_, kap=kap:
                 e.tensor_mul(T[:, 0:820].rearrange("p (g d m) -> p g d m", g=2, d=10),
                              ps[:, 0:820].rearrange("p (g d m) -> p g d m", g=2, d=10),
                              kap))
            b.op("vector", lambda e:
                 e.tensor_reduce(num_sb[:, :],
                                 T[:, 0:820].rearrange("p (g d m) -> p g m d", g=2, d=10),
                                 AX.X, ALU.add))
            psR_free[pg] = b.op("vector", lambda e, ps=ps_:
                 e.tensor_reduce(den_sb[:, :],
                                 ps[:, 0:820].rearrange("p (g d m) -> p g m d", g=2, d=10),
                                 AX.X, ALU.add), inc="sDVE")
            b.op("vector", lambda e: e.tensor_scalar(den_sb[:, :], den_sb[:, :], float(S),
                                                     None, ALU.add))
            b.op("vector", lambda e: e.reciprocal(den_sb[:, :], den_sb[:, :]))
            pap = P1[:, h0 * NSEL + MW * c:(h0 + 1) * NSEL + MW * (c + 1)]
            pap2 = bass.AP(pap.tensor, pap.offset, [[6560, 128], [NSEL, 2], [1, MW]])
            b.op("vector", lambda e, pap2=pap2:
                 e.tensor_mul(pap2, num_sb[:, :].rearrange("p (g m) -> p g m", g=2),
                              den_sb[:, :].rearrange("p (g m) -> p g m", g=2)))
            if g == 0:
                b.wait("vector", "dRW", rrow_free[0])
            b.op("vector", lambda e, h0=h0, t=rrw[c % 2]:
                 e.tensor_copy(t[0:1, h0 * MW:(h0 + 2) * MW], den_sb[0:1, :]), inc="sDVE")
        cb_free[par] = b.cnt["sDVE"]
        m2 = b.cnt["sDVE"]
        b.wait("scalar", "sDVE", m2)
        rrow_free[0] = b.dma("scalar", recip_hbm[c:c + 1, :], rrw[c % 2][0:1, :], "dRW")

    rr_guard = [0]

    wdT3 = r_pit(wdT[:, :])
    WDS = (WDS0, WDS1)
    wds_free = [0, 0]

    def wd_precast(o):
        pw = o % 2
        b.wait("gpsimd", "dWP%d" % pw, wds_free[pw])
        b.dma("gpsimd", WDS[pw][:, :].rearrange("p (g c) -> p g c", g=FHT),
              wdT3[:, :, o * 128:(o + 1) * 128], "dWC")
        b.wait("gpsimd", "dWC", b.cnt["dWC"])
        wds_free[pw] = b.dma("gpsimd", wd_bf[o, :, :], WDS[pw][:, :], "dWP%d" % pw)

    load_w_slabs(wk, ("sPE", s1_pe_done))
    wk_cnt = [b.cnt["dW%d" % o] for o in range(NT)]
    xn_dma(0, dve_guard=sel_ready)
    xn_dma(1, dve_guard=sel_ready)
    ev_mark = {}
    for c in range(NCH):
        pm = proj_chunk(c, wk_cnt, bk_sb)
        ev_mark[c] = b.cnt["sACT"]
        if c + 2 < NCH:
            xnbuf_free[c % 2] = pm
            xn_dma(c + 2)
        if c >= 1:
            scores_chunk(c - 1, ev_mark[c - 1])
        if c < 10:
            wd_precast(c)
    scores_chunk(NCH - 1, ev_mark[NCH - 1])

    load_w_slabs(wv, ("sPE", pms[NCH - 1]))
    wv_cnt = [b.cnt["dW%d" % o] for o in range(NT)]
    for j in range(2):
        b.wait("scalar", "dT%d" % j, b.cnt["dT%d" % j])
    xnbuf_free = [pms[NCH - 2], pms[NCH - 1]]
    xn_dma(0)
    xn_dma(1)
    for c in range(NCH):
        pm = proj_chunk(c, wv_cnt, bv_sb)
        ev_mark[c] = b.cnt["sACT"]
        if c + 2 < NCH:
            xnbuf_free[c % 2] = pm
            xn_dma(c + 2)
        if c >= 1:
            num_chunk(c - 1, ev_mark[c - 1])
            rr_guard[0] = b.cnt["sPE"]
        if c + 10 < NT:
            wd_precast(c + 10)
    num_chunk(NCH - 1, ev_mark[NCH - 1])
    s3_last_pm = pms[NCH - 1]
    if debug:
        m = b.op("vector", lambda e: e.engine_nop(), inc="sDVE")
        b.wait("sync", "sDVE", m)
        b.dma("sync", snum_dbg[:, :], P1[:, :], "dDBG")
        b.dma("sync", vsum_dbg[:, :], vsum_sb[:, :], "dDBG")
        b.wait("vector", "dDBG", b.cnt["dDBG"])

    b.op("vector", lambda e: e.tensor_scalar(upre_sb[:, :], vsum_sb[:, :], 1.0 / S, None,
                                             ALU.mult), inc="sDVE")
    psR_free = [b.cnt["sDVE"]] * 4
    rr_guard2 = b.cnt["sPE"]
    for c in range(NCH):
        b.wait("sync", "dRW", b.cnt["dRW"])
        b.wait("sync", "sPE", rr_guard2)
        b.dma("sync", rrow_sb[0:1, :], recip_hbm[c:c + 1, :], "dRR")
        for h in range(NH):
            rot = h % 4
            ps_, off = s_reg[rot]
            b.wait("tensor", "dRR", b.cnt["dRR"])
            b.wait("tensor", "sDVE", psR_free[rot])
            b.op("tensor", lambda e, h=h, ps=ps_, off=off:
                 e.matmul(ps[:, off:off + MW], ones1[:, :], rrow_sb[0:1, h * MW:(h + 1) * MW],
                          start=True, stop=True), inc="sPE")
            b.wait("vector", "sPE", b.cnt["sPE"])
            psR_free[rot] = b.op("vector", lambda e, h=h, c=c, ps=ps_, off=off:
                 e.scalar_tensor_tensor(P2[:, h * 411 + MW * c:h * 411 + MW * (c + 1)],
                                        ps[:, off:off + MW], vsum_sb[:, h:h + 1],
                                        P1[:, h * NSEL + MW * c:h * NSEL + MW * (c + 1)],
                                        ALU.mult, ALU.add), inc="sDVE")
        rr_guard2 = b.cnt["sPE"]
    b.op("vector", lambda e: e.tensor_copy(P2[:, 410:16 * 411:411], upre_sb[:, :]))
    ctx_ready = b.cnt["sDVE"]
    if debug:
        b.wait("sync", "sDVE", ctx_ready)
        b.dma("sync", ctx_dbg[:, :], P2[:, :], "dDBG")
        b.wait("vector", "dDBG", b.cnt["dDBG"])

    load_w_slabs(wo, ("sPE", s3_last_pm))
    b.wait("tensor", "sDVE", ctx_ready)
    for g in range(4):
        for sl in range(4):
            b.wait("tensor", slot_evac[sl][0], slot_evac[sl][1])
        for o in range(4 * g, 4 * g + 4):
            b.wait("tensor", "dW%d" % o, b.cnt["dW%d" % o])
        for i in range(NT):
            for o in range(4 * g, 4 * g + 4):
                last = (i == NT - 1 and o == 4 * g + 3)
                m = b.op("tensor", lambda e, i=i, o=o, st=(i == 0), sp=(i == NT - 1):
                     e.matmul(psA[:, (o % 4) * 512:(o % 4) * 512 + 411], wtile(i, o),
                              P2[:, i * 411:(i + 1) * 411], start=st, stop=sp),
                     inc="sPE" if last else None)
        b.wait("vector", "sPE", m)
        for o in range(4 * g, 4 * g + 4):
            r = (o % 4) * 512
            b.op("vector", lambda e, o=o, r=r:
                 e.tensor_scalar(u_sb[:, o:o + 1], psA[:, r + 410:r + 411],
                                 bo_sb[:, o:o + 1], None, ALU.add))
            b.op("vector", lambda e, o=o:
                 e.tensor_sub(colt_sb[:, :], bo_sb[:, o:o + 1], u_sb[:, o:o + 1]))
            slot_evac[o % 4] = ("sDVE", b.op("vector", lambda e, o=o, r=r:
                 e.tensor_scalar(DL[:, o * NSEL:(o + 1) * NSEL], psA[:, r:r + NSEL],
                                 colt_sb[:, :], None, ALU.add), inc="sDVE"))
    if debug:
        m = b.cnt["sDVE"]
        b.wait("sync", "sDVE", m)
        b.dma("sync", delta_dbg[:, :], DL[:, :], "dDBG")
        b.dma("sync", u_dbg[:, :], u_sb[:, :], "dDBG")
        b.wait("vector", "dDBG", b.cnt["dDBG"])

    wg3 = r_pit(wgT[:, :])
    wu3 = r_pit(wuT[:, :])
    g3 = r_pit(gated_hbm[:, :])
    slabs = [Bw[:, 0:8192], Bw[:, 8192:16384], Bw[:, 16384:24576], Bw[:, 24576:32768]]
    slab_free = [b.cnt["sPE"], b.cnt["sPE"]]
    gb = (gb0, gb1)
    gb_free = [0, 0]
    gs_free = b.cnt["sDVE"]
    psA_free = 0
    psB_free = b.cnt["sDVE"]
    half_free = s3_last_pm
    b.wait("tensor", "sDVE", b.cnt["sDVE"])
    b.wait("sync", "dWP0", b.cnt["dWP0"])
    b.wait("sync", "dWP1", b.cnt["dWP1"])
    for half in range(2):
        b.wait("sync", "sPE", half_free)
        b.dma("sync", A[:, :].rearrange("p (n t) -> p n t", n=NT),
              xn3[:, :, half * 2048:(half + 1) * 2048], "dXN0")
        for s_ in range(16):
            ps_ = s_ % 2
            b.wait("gpsimd", "sPE", slab_free[ps_])
            b.dma("gpsimd", slabs[2 * ps_][:, :].rearrange("p (n t) -> p n t", n=NT),
                  wg3[:, :, s_ * 512:(s_ + 1) * 512], "dSL%d" % ps_)
            b.dma("gpsimd", slabs[2 * ps_ + 1][:, :].rearrange("p (n t) -> p n t", n=NT),
                  wu3[:, :, s_ * 512:(s_ + 1) * 512], "dSL%d" % ps_)
            for ht in range(4):
                gidx = half * 64 + s_ * 4 + ht
                pg = gidx % 2
                b.wait("tensor", "dSL%d" % ps_, b.cnt["dSL%d" % ps_])
                b.wait("tensor", "dXN0", b.cnt["dXN0"])
                b.wait("tensor", "sACT", psA_free)
                for i in range(NT):
                    for ch in range(4):
                        gate_done = b.op("tensor", lambda e, i=i, ch=ch, sl=slabs[2 * ps_],
                             ht=ht, st=(i == 0), sp=(i == NT - 1):
                             e.matmul(psA[:, ch * 512:(ch + 1) * 512],
                                      sl[:, i * 512 + ht * 128:i * 512 + (ht + 1) * 128],
                                      A[:, i * 2048 + ch * 512:i * 2048 + (ch + 1) * 512],
                                      start=st, stop=sp),
                             inc="sPE" if (i == NT - 1 and ch == 3) else None)
                b.wait("tensor", "sDVE", psB_free)
                for i in range(NT):
                    for ch in range(4):
                        tgt = psS if ch < 2 else psX
                        up_done = b.op("tensor", lambda e, i=i, ch=ch, sl=slabs[2 * ps_ + 1],
                             ht=ht, t=tgt, st=(i == 0), sp=(i == NT - 1):
                             e.matmul(t[:, (ch % 2) * 512:(ch % 2 + 1) * 512],
                                      sl[:, i * 512 + ht * 128:i * 512 + (ht + 1) * 128],
                                      A[:, i * 2048 + ch * 512:i * 2048 + (ch + 1) * 512],
                                      start=st, stop=sp),
                             inc="sPE" if (i == NT - 1 and ch == 3) else None)
                b.wait("scalar", "sPE", gate_done)
                b.wait("scalar", "sDVE", gs_free)
                psA_free = b.op("scalar", lambda e:
                     e.activation(gs_sb[:, :], psA[:, 0:2048], AF.Silu), inc="sACT")
                b.wait("vector", "sPE", up_done)
                b.wait("vector", "sACT", psA_free)
                b.wait("vector", "dGW%d" % pg, gb_free[pg])
                b.op("vector", lambda e, d=gb[pg]:
                     e.tensor_mul(d[:, 0:1024], psS[:, 0:1024], gs_sb[:, 0:1024]))
                psB_free = b.op("vector", lambda e, d=gb[pg]:
                     e.tensor_mul(d[:, 1024:2048], psX[:, 0:1024], gs_sb[:, 1024:2048]),
                     inc="sDVE")
                gs_free = psB_free
                b.wait("scalar", "sDVE", psB_free)
                gb_free[pg] = b.dma("scalar", g3[:, s_ * 4 + ht, half * 2048:(half + 1) * 2048],
                                    gb[pg][:, :], "dGW%d" % pg)
            slab_free[ps_] = b.cnt["sPE"]
        half_free = b.cnt["sPE"]

    xbb = (xb0, xb1)
    outb = (outb0, outb1)
    wdq_free = b.cnt["sPE"]
    ge_free = [b.cnt["sPE"], b.cnt["sPE"]]
    xb_free = [0, 0]
    outb_free = [0, 0]
    psA_slot_free = [0, 0, 0, 0]
    b.wait("sync", "dGW0", b.cnt["dGW0"])
    b.wait("sync", "dGW1", b.cnt["dGW1"])
    b.wait("scalar", "dGW0", b.cnt["dGW0"])
    b.wait("scalar", "dGW1", b.cnt["dGW1"])
    b.wait("tensor", "sDVE", b.cnt["sDVE"])
    b.wait("tensor", "sACT", b.cnt["sACT"])
    for dq in range(4):
        b.wait("sync", "sPE", wdq_free)
        for dtl in range(4):
            b.dma("sync", Bw[:, dtl * 8192:(dtl + 1) * 8192], wd_bf[dq * 4 + dtl, :, :],
                  "dWD%d" % dtl)
        for e_ in range(8):
            pe_ = e_ % 2
            b.wait("scalar", "sDVE", xb_free[pe_])
            b.dma("scalar", xbb[pe_][:, :].rearrange("p (n t) -> p n t", n=4),
                  xT3[:, 4 * dq:4 * dq + 4, e_ * 512:(e_ + 1) * 512], "dXS%d" % pe_)
            for hh in range(2):
                q_ = "sync" if hh == 0 else "scalar"
                b.wait(q_, "sPE", ge_free[hh])
                dst = A[:, hh * 16384:(hh + 1) * 16384].rearrange("p (n t) -> p n t", n=32)
                b.dma(q_, dst, g3[:, hh * 32:(hh + 1) * 32, e_ * 512:(e_ + 1) * 512],
                      "dGE%d" % hh)
                b.wait("tensor", "dGE%d" % hh, b.cnt["dGE%d" % hh])
                for dt in range(4):
                    if hh == 0:
                        b.wait("tensor", "dWD%d" % dt, b.cnt["dWD%d" % dt])
                        b.wait("tensor", "sDVE", psA_slot_free[dt])
                    for h32 in range(32):
                        g_ = hh * 32 + h32
                        ge_free[hh] = b.op("tensor", lambda e, g=g_, dt=dt, hh=hh, h32=h32,
                             st=(hh == 0 and h32 == 0), sp=(hh == 1 and h32 == 31):
                             e.matmul(psA[:, dt * 512:(dt + 1) * 512],
                                      Bw[:, dt * 8192 + g * 128:dt * 8192 + (g + 1) * 128],
                                      A[:, hh * 16384 + h32 * 512:hh * 16384 + (h32 + 1) * 512],
                                      start=st, stop=sp),
                             inc="sPE" if (h32 == 31 and dt == 3) else None)
            mm_done = ge_free[1]
            tl0 = (-512 * e_) % 10
            msta = (512 * e_ + 9) // 10
            scnt = (512 * (e_ + 1) - 1 - (512 * e_ + tl0)) // 10 + 1
            for dt in range(4):
                o = dq * 4 + dt
                po = (e_ * 4 + dt) % 2
                b.wait("vector", "sPE", mm_done)
                b.wait("vector", "dXS%d" % pe_, b.cnt["dXS%d" % pe_])
                b.wait("vector", "dOUT%d" % po, outb_free[po])
                b.op("vector", lambda e, dt=dt, o=o, po=po, pe2=pe_:
                     e.scalar_tensor_tensor(outb[po][:, :], psA[:, dt * 512:(dt + 1) * 512],
                                            u_sb[:, o:o + 1], xbb[pe2][:, dt * 512:(dt + 1) * 512],
                                            ALU.add, ALU.add))
                psA_slot_free[dt] = b.op("vector", lambda e, o=o, po=po, t0=tl0, ms=msta, cn=scnt:
                     e.tensor_add(outb[po][:, t0:t0 + (cn - 1) * 10 + 1:10],
                                  outb[po][:, t0:t0 + (cn - 1) * 10 + 1:10],
                                  DL[:, o * NSEL + ms:o * NSEL + ms + cn]), inc="sDVE")
                xb_free[pe_] = psA_slot_free[dt]
                b.wait("scalar", "sDVE", psA_slot_free[dt])
                outb_free[po] = b.dma("scalar", r_pit(outT[:, :])[:, o, e_ * 512:(e_ + 1) * 512],
                                      outb[po][:, :], "dOUT%d" % po)
        wdq_free = b.cnt["sPE"]
    b.wait("scalar", "dOUT0", b.cnt["dOUT0"])
    b.wait("scalar", "dOUT1", b.cnt["dOUT1"])

    with nc.allow_low_precision("bf16 activations/weights by design (tolerance allows)"):
        with nc.Block() as block:
            @block.sync
            def _(e):
                for fn in b.ops["sync"]:
                    fn(e)

            @block.gpsimd
            def _(e):
                for fn in b.ops["gpsimd"]:
                    fn(e)

            @block.vector
            def _(e):
                for fn in b.ops["vector"]:
                    fn(e)

            @block.scalar
            def _(e):
                for fn in b.ops["scalar"]:
                    fn(e)

            @block.tensor
            def _(e):
                for fn in b.ops["tensor"]:
                    fn(e)

    return nc


def prep_inputs(L0_attn_hidden_states, L0_attn_wq, L0_attn_bq, L0_attn_wk, L0_attn_bk,
                L0_attn_wv, L0_attn_bv, L0_attn_wo, L0_attn_bo,
                L0_ffn_w_gate, L0_ffn_w_up, L0_ffn_w_down):
    x = np.asarray(L0_attn_hidden_states, dtype=np.float32)
    com = {
        "wq": np.ascontiguousarray(np.asarray(L0_attn_wq, dtype=np.float32)),
        "wk": np.ascontiguousarray(np.asarray(L0_attn_wk, dtype=np.float32)),
        "wv": np.ascontiguousarray(np.asarray(L0_attn_wv, dtype=np.float32)),
        "wo": np.ascontiguousarray(np.asarray(L0_attn_wo, dtype=np.float32)),
        "bq": np.ascontiguousarray(np.asarray(L0_attn_bq, dtype=np.float32).reshape(NT, 128).T),
        "bk": np.ascontiguousarray(np.asarray(L0_attn_bk, dtype=np.float32).reshape(NT, 128).T),
        "bv": np.ascontiguousarray(np.asarray(L0_attn_bv, dtype=np.float32).reshape(NT, 128).T),
        "bo": np.ascontiguousarray(np.asarray(L0_attn_bo, dtype=np.float32).reshape(NT, 128).T),
        "wgT": np.ascontiguousarray(np.asarray(L0_ffn_w_gate, dtype=np.float32).T),
        "wuT": np.ascontiguousarray(np.asarray(L0_ffn_w_up, dtype=np.float32).T),
        "wdT": np.ascontiguousarray(np.asarray(L0_ffn_w_down, dtype=np.float32).T),
    }
    in_maps = []
    for bi in range(8):
        m = dict(com)
        m["xT"] = np.ascontiguousarray(x[bi].T)
        in_maps.append(m)
    return in_maps


def kernel(**inputs):
    in_maps = prep_inputs(**inputs)
    nc = build(debug=False)
    res = run_bass_kernel_spmd(nc, in_maps, core_ids=list(range(8)))
    return np.stack([np.ascontiguousarray(r["outT"].T) for r in res.results], axis=0)


# version 07
# speedup vs baseline: 1.1592x; 1.1503x over previous
import sys
if "/opt/trn_rl_repo" not in sys.path:
    sys.path.insert(0, "/opt/trn_rl_repo")
import math
import numpy as np
import concourse.bass as bass
import concourse.mybir as mybir
from concourse.bass_utils import run_bass_kernel_spmd

F32 = mybir.dt.float32
BF16 = mybir.dt.bfloat16
AF = mybir.ActivationFunctionType
ALU = mybir.AluOpType
AX = mybir.AxisListType

D = 2048
NT = 16
S = 4096
NH = 16
CT = 410
NCH = 10
MW = 41
NSEL = 410
FH = 8192
FHT = 64
QSCALE = 1.0 / math.sqrt(128.0)
EPS = 1e-5

ENGS = ("sync", "gpsimd", "vector", "scalar", "tensor")


class Bld:
    AUTO = {"vector": "sDVE", "scalar": "sACT"}

    def __init__(self, nc):
        self.nc = nc
        self.ops = {e: [] for e in ENGS}
        self.sems = {}
        self.cnt = {}

    def sem(self, name):
        if name not in self.sems:
            self.sems[name] = self.nc.ctx.enter_context(self.nc.semaphore(name))
            self.cnt[name] = 0
        return self.sems[name]

    def wait(self, eng, sem, val=None):
        s = self.sem(sem)
        v = self.cnt[sem] if val is None else val
        if v > 0:
            self.ops[eng].append(lambda e, s=s, v=v: e.wait_ge(s, v))

    def op(self, eng, fn, inc=None, incby=1):
        a = self.AUTO.get(eng)
        if a is not None:
            s = self.sem(a)
            v = self.cnt[a]
            if v > 0:
                self.ops[eng].append(lambda e, s=s, v=v: e.wait_ge(s, v))
            if inc is None:
                inc = a
                incby = 1
        if inc is None:
            self.ops[eng].append(fn)
            return 0
        s = self.sem(inc)
        self.cnt[inc] += incby
        self.ops[eng].append(lambda e, fn=fn, s=s, b=incby: fn(e).then_inc(s, b))
        return self.cnt[inc]

    def dma(self, eng, dst, src, sem, **kw):
        return self.op(eng, lambda e, d=dst, s_=src, kw=kw: e.dma_start(d, s_, **kw),
                       inc=sem, incby=16)


def r_pit(dram_ap, p=128):
    return dram_ap.rearrange("(n p) t -> p n t", p=p)


def build(debug=False):
    nc = bass.Bass("TRN2", target_bir_lowering=False)

    xT = nc.dram_tensor("xT", [D, S], F32, kind="ExternalInput")
    wq = nc.dram_tensor("wq", [D, D], F32, kind="ExternalInput")
    wk = nc.dram_tensor("wk", [D, D], F32, kind="ExternalInput")
    wv = nc.dram_tensor("wv", [D, D], F32, kind="ExternalInput")
    wo = nc.dram_tensor("wo", [D, D], F32, kind="ExternalInput")
    bq = nc.dram_tensor("bq", [128, NT], F32, kind="ExternalInput")
    bk = nc.dram_tensor("bk", [128, NT], F32, kind="ExternalInput")
    bv = nc.dram_tensor("bv", [128, NT], F32, kind="ExternalInput")
    bo = nc.dram_tensor("bo", [128, NT], F32, kind="ExternalInput")
    wgT = nc.dram_tensor("wgT", [D, FH], F32, kind="ExternalInput")
    wuT = nc.dram_tensor("wuT", [D, FH], F32, kind="ExternalInput")
    wdT = nc.dram_tensor("wdT", [FH, D], F32, kind="ExternalInput")
    outT = nc.dram_tensor("outT", [D, S], F32, kind="ExternalOutput")

    dbg = "ExternalOutput" if debug else "Internal"
    xn_hbm = nc.dram_tensor("xn_hbm", [D, S], BF16, kind=dbg)
    trow_hbm = nc.dram_tensor("trow_hbm", [NCH, NH, 416], BF16, kind=dbg)
    recip_hbm = nc.dram_tensor("recip_hbm", [NCH, NH * MW], BF16, kind=dbg)
    gated_hbm = nc.dram_tensor("gated_hbm", [FH, S], BF16, kind=dbg)
    wd_bf = nc.dram_tensor("wd_bf", [NT, 128, FHT * 128], BF16, kind="Internal")
    if debug:
        qsel_dbg = nc.dram_tensor("qsel_dbg", [128, NT * NSEL], BF16, kind="ExternalOutput")
        snum_dbg = nc.dram_tensor("snum_dbg", [128, NT * NSEL], BF16, kind="ExternalOutput")
        ctx_dbg = nc.dram_tensor("ctx_dbg", [128, NT * 411], BF16, kind="ExternalOutput")
        delta_dbg = nc.dram_tensor("delta_dbg", [128, NT * NSEL], BF16, kind="ExternalOutput")
        u_dbg = nc.dram_tensor("u_dbg", [128, NT], F32, kind="ExternalOutput")
        vsum_dbg = nc.dram_tensor("vsum_dbg", [128, NT], F32, kind="ExternalOutput")

    BASE = 16544

    def al(name, shape, dtype, offset):
        return nc.alloc_sbuf_tensor_at(name, shape, dtype, offset=BASE + offset)

    A = al("A", [128, 32768], BF16, offset=0)
    Bw = al("Bw", [128, 32768], BF16, offset=65536)
    C = al("C", [128, 13120], BF16, offset=131072)
    P1 = al("P1", [128, 6560], BF16, offset=157312)
    P2 = al("P2", [128, 6576], BF16, offset=170432)
    DL = al("DL", [128, 6560], BF16, offset=183584)
    T = al("T", [128, 2048], F32, offset=196704)
    XF0 = al("XF0", [128, 8192], F32, offset=32768)
    XF1 = al("XF1", [128, 8192], F32, offset=65536)
    WDS0 = al("WDS0", [128, 8192], BF16, offset=32768)
    WDS1 = al("WDS1", [128, 8192], BF16, offset=49152)
    m_sb = al("m_sb", [128, 512], F32, offset=200800)
    r_sb = al("r_sb", [128, 512], F32, offset=202848)
    t1_sb = None
    trw = [al("trw%d" % j, [1, 8 * CT], BF16, offset=157312 + 6560 * j) for j in range(2)]
    trr = [al("trr%d" % j, [1, 8 * CT], BF16, offset=170432 + 6560 * j) for j in range(2)]
    gs_sb = al("gs_sb", [128, 2048], BF16, offset=131072)
    gb0 = al("gb0", [128, 2048], BF16, offset=135168)
    gb1 = al("gb1", [128, 2048], BF16, offset=139264)
    outb0 = al("outb0", [128, 512], F32, offset=131072)
    outb1 = al("outb1", [128, 512], F32, offset=133120)
    xb0 = al("xb0", [128, 2048], F32, offset=135168)
    xb1 = al("xb1", [128, 2048], F32, offset=143360)
    sm = 204896
    bq_sb = al("bq_sb", [128, NT], F32, offset=sm); sm += 64
    bk_sb = al("bk_sb", [128, NT], F32, offset=sm); sm += 64
    bv_sb = al("bv_sb", [128, NT], F32, offset=sm); sm += 64
    bo_sb = al("bo_sb", [128, NT], F32, offset=sm); sm += 64
    u_sb = al("u_sb", [128, NT], F32, offset=sm); sm += 64
    vsum_sb = al("vsum_sb", [128, NT], F32, offset=sm); sm += 64
    upre_sb = al("upre_sb", [128, NT], BF16, offset=sm); sm += 32
    den_sb = al("den_sb", [128, 2 * MW], F32, offset=sm); sm += 352
    num_sb = al("num_sb", [128, 2 * MW], F32, offset=sm); sm += 352
    rrow_sb = al("rrow_sb", [1, NH * MW], BF16, offset=sm); sm += 1344
    ones128 = al("ones128", [128, 128], BF16, offset=sm); sm += 256
    ones1 = al("ones1", [1, 128], BF16, offset=sm); sm += 256
    onesk = al("onesk", [128, 8], BF16, offset=sm); sm += 32
    colt_sb = al("colt_sb", [128, 1], F32, offset=sm); sm += 32
    vt_sb = al("vt_sb", [128, 1], F32, offset=sm); sm += 32
    eps_sb = al("eps_sb", [128, 1], F32, offset=sm); sm += 32
    denr_sb = al("denr_sb", [1, 64], F32, offset=sm); sm += 256
    t1_sb = al("t1_sb", [128, 512], F32, offset=183584)
    ones128f = al("ones128f", [128, 128], F32, offset=sm); sm += 512
    Eb4 = [al("E%d" % j, [128, 2 * CT], BF16, offset=196704 + 1664 * j) for j in range(4)]
    rrw = []
    for j in range(2):
        rrw.append(al("rrw%d" % j, [1, NH * MW], BF16, offset=sm)); sm += 1344
    assert BASE + sm <= 229376 - 64, sm

    psA = nc.alloc_psum_tensor("psA", [128, 2048], F32)
    psS = nc.alloc_psum_tensor("psS", [128, 1024], F32)
    psX = nc.alloc_psum_tensor("psX", [128, 1024], F32)

    b = Bld(nc)
    import builtins
    builtins._BLD = b

    def wtile(i, o):
        return Bw[:, o * 2048 + i * 128:o * 2048 + (i + 1) * 128]

    def load_w_slabs(wsrc, guard=None, olist=range(NT)):
        w3 = r_pit(wsrc[:, :])
        for o in olist:
            if guard is not None:
                b.wait("gpsimd", guard[0], guard[1])
            dst = Bw[:, o * 2048:(o + 1) * 2048].rearrange("p (i c) -> p i c", i=NT)
            b.dma("gpsimd", dst, w3[:, :, o * 128:(o + 1) * 128], "dW%d" % o)

    s_reg = [(psS, 0), (psS, 512), (psX, 0), (psX, 512)]

    b.dma("gpsimd", bq_sb[:, :], bq[:, :], "dB")
    b.dma("gpsimd", bk_sb[:, :], bk[:, :], "dB")
    b.dma("gpsimd", bv_sb[:, :], bv[:, :], "dB")
    b.dma("gpsimd", bo_sb[:, :], bo[:, :], "dB")
    load_w_slabs(wq, None, range(8, 16))
    b.op("vector", lambda e: e.memset(ones128[:, :], 1.0))
    b.op("vector", lambda e: e.memset(ones128f[:, :], 1.0))
    b.op("vector", lambda e: e.memset(ones1[:, :], 1.0))
    b.op("vector", lambda e: e.memset(onesk[:, :], 1.0))
    b.op("vector", lambda e: e.memset(eps_sb[:, :], EPS))
    b.op("vector", lambda e: e.memset(vsum_sb[:, :], 0.0), inc="sDVE")

    XF = (XF0, XF1)
    xstage = (A[:, 0:8192], A[:, 8192:16384])
    xT3 = r_pit(xT[:, :])
    xn3 = r_pit(xn_hbm[:, :])
    xf_free_mark = [0, 0]
    xstage_free_mark = [0, 0]
    selpos = []
    m0 = 0
    for c in range(8):
        tl0 = (-512 * c) % 10
        cnt_ = (512 - tl0 + 9) // 10
        selpos.append((tl0, cnt_, m0))
        m0 += cnt_
    assert m0 == NSEL

    def s0_dma(c):
        par = c % 2
        b.wait("sync", "sDVE", xf_free_mark[par])
        dst = XF[par][:, :].rearrange("p (n t) -> p n t", n=NT)
        b.dma("sync", dst, xT3[:, :, 512 * c:512 * (c + 1)], "dX%d" % par)

    s0_dma(0)
    s0_dma(1)
    sq32 = (al("sq32a", [128, 512], F32, offset=196704),
            al("sq32b", [128, 512], F32, offset=198752))
    mmk = [0] * NT
    statfree = 0
    b.wait("tensor", "sDVE", b.cnt["sDVE"])
    for c in range(8):
        par = c % 2
        for i in range(NT):
            ip = i % 2
            xsl = XF[par][:, 512 * i:512 * (i + 1)]
            if i >= 2:
                b.wait("scalar", "sPE", mmk[i - 2])
            if i == 0:
                b.wait("scalar", "dX%d" % par, 16 * (c // 2 + 1))
            b.op("scalar", lambda e, s=xsl, d=sq32[ip]:
                 e.activation(d[:, :], s, AF.Square), inc="sACT")
            if i == 0:
                b.wait("tensor", "dX%d" % par, 16 * (c // 2 + 1))
                b.wait("tensor", "sDVE", statfree)
            b.op("tensor", lambda e, s=xsl, st=(i == 0), sp=(i == NT - 1):
                 e.matmul(psX[:, 0:512], ones128f[:, :], s, start=st, stop=sp))
            b.wait("tensor", "sACT")
            mmk[i] = b.op("tensor", lambda e, d=sq32[ip], st=(i == 0), sp=(i == NT - 1):
                 e.matmul(psX[:, 512:1024], ones128f[:, :], d[:, :], start=st, stop=sp),
                 inc="sPE")
        b.wait("vector", "sPE", mmk[NT - 1])
        b.op("vector", lambda e: e.tensor_scalar(m_sb[:, :], psX[:, 0:512], 1.0 / D, None, ALU.mult))
        b.op("vector", lambda e: e.tensor_mul(r_sb[:, :], m_sb[:, :], m_sb[:, :]))
        statfree = b.op("vector", lambda e: e.scalar_tensor_tensor(
            r_sb[:, :], psX[:, 512:1024], 1.0 / D, r_sb[:, :], ALU.mult, ALU.subtract), inc="sDVE")
        b.wait("scalar", "sDVE", statfree)
        b.op("scalar", lambda e: e.activation(r_sb[:, :], r_sb[:, :], AF.Sqrt, bias=eps_sb[:, 0:1]),
             inc="sACT")
        b.wait("vector", "sACT")
        b.op("vector", lambda e: e.reciprocal(r_sb[:, :], r_sb[:, :]))
        b.wait("vector", "dXNW%d" % par, xstage_free_mark[par])
        tl0, cnt_, mst = selpos[c]
        for i in range(NT):
            xsl = XF[par][:, 512 * i:512 * (i + 1)]
            stg = xstage[par][:, 512 * i:512 * (i + 1)]
            b.op("vector", lambda e, x=xsl: e.tensor_sub(t1_sb[:, :], x, m_sb[:, :]))
            b.op("vector", lambda e, d=stg: e.tensor_mul(d, t1_sb[:, :], r_sb[:, :]))
            b.op("vector", lambda e, d=P1[:, i * NSEL + mst:i * NSEL + mst + cnt_],
                 s=xstage[par][:, 512 * i + tl0:512 * (i + 1):10]: e.tensor_copy(d, s))
        xf_free_mark[par] = b.op("vector", lambda e: e.engine_nop(), inc="sDVE")
        b.wait("sync", "sDVE", xf_free_mark[par])
        src2 = xstage[par][:, :].rearrange("p (n t) -> p n t", n=NT)
        xstage_free_mark[par] = b.dma("sync", xn3[:, :, 512 * c:512 * (c + 1)], src2,
                                      "dXNW%d" % par)
        if c + 2 < 8:
            s0_dma(c + 2)
    sel_ready = b.cnt["sDVE"]

    slot_evac = [("sDVE", 0), ("sDVE", 0), ("sDVE", 0), ("sDVE", 0)]

    load_w_slabs(wq, ("sDVE", xf_free_mark[1]), range(0, 8))
    b.wait("tensor", "sDVE", sel_ready)
    b.wait("vector", "dB", 64)
    first_g = True
    for g in (2, 3, 0, 1):
        if not first_g:
            b.wait("tensor", "sDVE", b.cnt["sDVE"])
        first_g = False
        for o in range(4 * g, 4 * g + 4):
            b.wait("tensor", "dW%d" % o, b.cnt["dW%d" % o])
        for i in range(NT):
            for o in range(4 * g, 4 * g + 4):
                last = (i == NT - 1 and o == 4 * g + 3)
                m = b.op("tensor", lambda e, i=i, o=o, st=(i == 0), sp=(i == NT - 1):
                     e.matmul(psA[:, (o % 4) * 512:(o % 4) * 512 + NSEL], wtile(i, o),
                              P1[:, i * NSEL:(i + 1) * NSEL], start=st, stop=sp),
                     inc="sPE" if last else None)
        b.wait("vector", "sPE", m)
        for o in range(4 * g, 4 * g + 4):
            slot_evac[o % 4] = ("sDVE", b.op("vector", lambda e, o=o:
                 e.tensor_scalar(P2[:, o * NSEL:(o + 1) * NSEL],
                                 psA[:, (o % 4) * 512:(o % 4) * 512 + NSEL],
                                 bq_sb[:, o:o + 1], QSCALE, ALU.add, ALU.mult), inc="sDVE"))
    qsel_ready = b.cnt["sDVE"]
    s1_pe_done = b.cnt["sPE"]
    if debug:
        b.wait("sync", "sDVE", qsel_ready)
        b.dma("sync", qsel_dbg[:, :], P2[:, 0:NT * NSEL], "dDBG")
        b.wait("vector", "dDBG", b.cnt["dDBG"])

    Cb = (C[:, 0:NT * CT], C[:, NT * CT:2 * NT * CT])
    xnbuf_free = [0, 0]
    cb_free = [0, 0]
    pms = {}

    def xn_dma(c, dve_guard=0):
        par = c % 2
        b.wait("sync", "sPE", xnbuf_free[par])
        if dve_guard:
            b.wait("sync", "sDVE", dve_guard)
        b.wait("sync", "dXNW0", 64)
        b.wait("sync", "dXNW1", 64)
        dst = A[:, par * NT * CT:(par + 1) * NT * CT].rearrange("p (n t) -> p n t", n=NT)
        L = CT if c < NCH - 1 else S - (NCH - 1) * CT
        b.dma("sync", dst[:, :, 0:L], xn3[:, :, c * CT:c * CT + L], "dXN%d" % par)

    def proj_chunk(c, wcnt, bias_col):
        par = c % 2
        kv_len = CT if c < NCH - 1 else S - (NCH - 1) * CT
        xb_ = A[:, par * NT * CT:(par + 1) * NT * CT]
        b.wait("tensor", "dXN%d" % par, b.cnt["dXN%d" % par])
        b.wait("scalar", "sDVE", cb_free[par])
        for o in range(NT):
            reg = (o % 4) * 512
            b.wait("tensor", slot_evac[o % 4][0], slot_evac[o % 4][1])
            b.wait("tensor", "dW%d" % o, wcnt[o])
            for i in range(NT):
                m = b.op("tensor", lambda e, i=i, o=o, r=reg, L=kv_len, x=xb_,
                         st=(i == 0), sp=(i == NT - 1):
                         e.matmul(psA[:, r:r + L], wtile(i, o), x[:, i * CT:i * CT + L],
                                  start=st, stop=sp),
                         inc="sPE" if i == NT - 1 else None)
            b.wait("scalar", "sPE", m)
            ev = b.op("scalar", lambda e, o=o, r=reg, L=kv_len, bc=bias_col, dst=Cb[par]:
                 e.activation(dst[:, o * CT:o * CT + L], psA[:, r:r + L], AF.Identity,
                              bias=bc[:, o:o + 1]), inc="sACT")
            if kv_len < CT:
                b.op("vector", lambda e, o=o, L=kv_len, dst=Cb[par]:
                     e.memset(dst[:, o * CT + L:(o + 1) * CT], 0.0))
            slot_evac[o % 4] = ("sACT", ev)
        pms[c] = m
        return m

    def kslice3(h, par):
        return Cb[par][:, h * CT:(h + 1) * CT].rearrange("p (m d) -> p d m", m=MW, d=10)

    E_free = [0, 0, 0, 0]
    psR_free = [0, 0, 0, 0]
    trw_free = [0, 0]
    trr_free = [0, 0]

    def scores_chunk(c, evac_mark):
        par = c % 2
        b.wait("vector", "sACT", evac_mark)
        for g in range(8):
            h0 = 2 * g
            rot = g % 4
            pg = g % 2
            hb = h0 // 8
            ps_ = psS if pg == 0 else psX
            Eb = Eb4[rot]
            b.wait("vector", "sPE", E_free[rot])
            q2 = P2[:, h0 * NSEL + MW * c:(h0 + 1) * NSEL + MW * (c + 1)]
            qap = bass.AP(q2.tensor, q2.offset, [[6576, 128], [NSEL, 2], [0, 10], [1, MW]])
            kap = Cb[par][:, h0 * CT:(h0 + 2) * CT].rearrange(
                "p (g m d) -> p g d m", g=2, m=MW, d=10)
            b.op("vector", lambda e, qap=qap, kap=kap, Eb=Eb:
                 e.tensor_mul(Eb[:, :].rearrange("p (g d m) -> p g d m", g=2, d=10),
                              qap, kap), inc="sDVE")
            E_mark = b.cnt["sDVE"]
            b.wait("tensor", "sDVE", E_mark)
            b.wait("tensor", "sDVE", psR_free[pg])
            b.op("tensor", lambda e, Eb=Eb, ps=ps_:
                 e.matmul(ps[0:1, 0:512], onesk[:, 0:1], Eb[:, 0:512],
                          start=True, stop=True))
            E_free[rot] = b.op("tensor", lambda e, Eb=Eb, ps=ps_:
                 e.matmul(ps[0:1, 512:820], onesk[:, 0:1], Eb[:, 512:820],
                          start=True, stop=True), inc="sPE")
            b.wait("scalar", "sPE", E_free[rot])
            b.op("scalar", lambda e, ps=ps_:
                 e.activation(ps[0:1, 0:820], ps[0:1, 0:820], AF.Exp), inc="sACT")
            b.wait("vector", "sACT", b.cnt["sACT"])
            if g % 4 == 0:
                b.wait("vector", "dT%d" % hb, trw_free[hb])
            psR_free[pg] = b.op("vector", lambda e, ps=ps_, t=trw[hb], hl=h0 % 8:
                 e.tensor_scalar(t[0:1, hl * CT:(hl + 2) * CT], ps[0:1, 0:820],
                                 -1.0, None, ALU.add), inc="sDVE")
            if g % 4 == 3:
                m2 = b.cnt["sDVE"]
                b.wait("sync", "sDVE", m2)
                trw_free[hb] = b.dma("sync", trow_hbm[c, hb * 8:hb * 8 + 8, 0:CT],
                                     trw[hb][0:1, :].rearrange("o (g m) -> o g m", g=8),
                                     "dT%d" % hb)

    rrow_free = [0]
    def num_chunk(c, evac_mark):
        par = c % 2
        b.wait("vector", "sACT", evac_mark)
        for i in range(NT):
            b.op("vector", lambda e, i=i, src=Cb[par]:
                 e.tensor_reduce(vt_sb[:, :], src[:, i * CT:(i + 1) * CT], AX.X, ALU.add))
            b.op("vector", lambda e, i=i: e.tensor_add(vsum_sb[:, i:i + 1], vsum_sb[:, i:i + 1],
                                                       vt_sb[:, :]))
        for g in range(8):
            h0 = 2 * g
            pg = g % 2
            hb = h0 // 8
            ps_ = psS if pg == 0 else psX
            if h0 % 8 == 0:
                b.wait("sync", "sPE", trr_free[hb])
                b.dma("sync", trr[hb][0:1, :].rearrange("o (g m) -> o g m", g=8),
                      trow_hbm[c, hb * 8:hb * 8 + 8, 0:CT], "dTR%d" % hb)
            b.wait("tensor", "dTR%d" % hb, b.cnt["dTR%d" % hb])
            b.wait("tensor", "sDVE", psR_free[pg])
            b.op("tensor", lambda e, t=trr[hb], hl=h0 % 8, ps=ps_:
                 e.matmul(ps[:, 0:512], ones1[:, :], t[0:1, hl * CT:hl * CT + 512],
                          start=True, stop=True))
            trr_free[hb] = b.op("tensor", lambda e, t=trr[hb], hl=h0 % 8, ps=ps_:
                 e.matmul(ps[:, 512:820], ones1[:, :], t[0:1, hl * CT + 512:hl * CT + 820],
                          start=True, stop=True), inc="sPE")
            b.wait("vector", "sPE", trr_free[hb])
            kap = Cb[par][:, h0 * CT:(h0 + 2) * CT].rearrange(
                "p (g m d) -> p g d m", g=2, m=MW, d=10)
            b.op("vector", lambda e, ps=ps_, kap=kap:
                 e.tensor_mul(T[:, 0:820].rearrange("p (g d m) -> p g d m", g=2, d=10),
                              ps[:, 0:820].rearrange("p (g d m) -> p g d m", g=2, d=10),
                              kap))
            b.op("vector", lambda e:
                 e.tensor_reduce(num_sb[:, :],
                                 T[:, 0:820].rearrange("p (g d m) -> p g m d", g=2, d=10),
                                 AX.X, ALU.add))
            psR_free[pg] = b.op("vector", lambda e, ps=ps_:
                 e.tensor_reduce(den_sb[:, :],
                                 ps[:, 0:820].rearrange("p (g d m) -> p g m d", g=2, d=10),
                                 AX.X, ALU.add), inc="sDVE")
            b.op("vector", lambda e: e.tensor_scalar(den_sb[:, :], den_sb[:, :], float(S),
                                                     None, ALU.add))
            b.op("vector", lambda e: e.reciprocal(den_sb[:, :], den_sb[:, :]))
            pap = P1[:, h0 * NSEL + MW * c:(h0 + 1) * NSEL + MW * (c + 1)]
            pap2 = bass.AP(pap.tensor, pap.offset, [[6560, 128], [NSEL, 2], [1, MW]])
            b.op("vector", lambda e, pap2=pap2:
                 e.tensor_mul(pap2, num_sb[:, :].rearrange("p (g m) -> p g m", g=2),
                              den_sb[:, :].rearrange("p (g m) -> p g m", g=2)))
            if g == 0:
                b.wait("vector", "dRW", rrow_free[0])
            b.op("vector", lambda e, h0=h0, t=rrw[c % 2]:
                 e.tensor_copy(t[0:1, h0 * MW:(h0 + 2) * MW], den_sb[0:1, :]), inc="sDVE")
        cb_free[par] = b.cnt["sDVE"]
        m2 = b.cnt["sDVE"]
        b.wait("sync", "sDVE", m2)
        rrow_free[0] = b.dma("sync", recip_hbm[c:c + 1, :], rrw[c % 2][0:1, :], "dRW")

    rr_guard = [0]

    wdT3 = r_pit(wdT[:, :])
    WDS = (WDS0, WDS1)
    wds_free = [0, 0]

    def wd_precast(o):
        pw = o % 2
        b.wait("gpsimd", "dWP%d" % pw, wds_free[pw])
        b.dma("gpsimd", WDS[pw][:, :].rearrange("p (g c) -> p g c", g=FHT),
              wdT3[:, :, o * 128:(o + 1) * 128], "dWC")
        b.wait("gpsimd", "dWC", b.cnt["dWC"])
        wds_free[pw] = b.dma("gpsimd", wd_bf[o, :, :], WDS[pw][:, :], "dWP%d" % pw)

    load_w_slabs(wk, ("sPE", s1_pe_done))
    wk_cnt = [b.cnt["dW%d" % o] for o in range(NT)]
    xn_dma(0, dve_guard=sel_ready)
    xn_dma(1, dve_guard=sel_ready)
    ev_mark = {}
    for c in range(NCH):
        if c >= 1:
            scores_chunk(c - 1, ev_mark[c - 1])
        pm = proj_chunk(c, wk_cnt, bk_sb)
        ev_mark[c] = b.cnt["sACT"]
        if c + 2 < NCH:
            xnbuf_free[c % 2] = pm
            xn_dma(c + 2)
        if c < 10:
            wd_precast(c)
    scores_chunk(NCH - 1, ev_mark[NCH - 1])

    load_w_slabs(wv, ("sPE", pms[NCH - 1]))
    wv_cnt = [b.cnt["dW%d" % o] for o in range(NT)]
    b.wait("vector", "dT0", b.cnt["dT0"])
    b.wait("vector", "dT1", b.cnt["dT1"])
    xnbuf_free = [pms[NCH - 2], pms[NCH - 1]]
    xn_dma(0)
    xn_dma(1)
    for c in range(NCH):
        if c >= 1:
            num_chunk(c - 1, ev_mark[c - 1])
        pm = proj_chunk(c, wv_cnt, bv_sb)
        ev_mark[c] = b.cnt["sACT"]
        if c + 2 < NCH:
            xnbuf_free[c % 2] = pm
            xn_dma(c + 2)
        if c + 10 < NT:
            wd_precast(c + 10)
    num_chunk(NCH - 1, ev_mark[NCH - 1])
    s3_last_pm = pms[NCH - 1]
    if debug:
        m = b.op("vector", lambda e: e.engine_nop(), inc="sDVE")
        b.wait("sync", "sDVE", m)
        b.dma("sync", snum_dbg[:, :], P1[:, :], "dDBG")
        b.dma("sync", vsum_dbg[:, :], vsum_sb[:, :], "dDBG")
        b.wait("vector", "dDBG", b.cnt["dDBG"])

    b.op("vector", lambda e: e.tensor_scalar(upre_sb[:, :], vsum_sb[:, :], 1.0 / S, None,
                                             ALU.mult), inc="sDVE")
    psR_free = [b.cnt["sDVE"]] * 4
    rr_guard2 = b.cnt["sPE"]
    for c in range(NCH):
        b.wait("sync", "dRW", b.cnt["dRW"])
        b.wait("sync", "sPE", rr_guard2)
        b.dma("sync", rrow_sb[0:1, :], recip_hbm[c:c + 1, :], "dRR")
        for h in range(NH):
            rot = h % 4
            ps_, off = s_reg[rot]
            b.wait("tensor", "dRR", b.cnt["dRR"])
            b.wait("tensor", "sDVE", psR_free[rot])
            b.op("tensor", lambda e, h=h, ps=ps_, off=off:
                 e.matmul(ps[:, off:off + MW], ones1[:, :], rrow_sb[0:1, h * MW:(h + 1) * MW],
                          start=True, stop=True), inc="sPE")
            b.wait("vector", "sPE", b.cnt["sPE"])
            psR_free[rot] = b.op("vector", lambda e, h=h, c=c, ps=ps_, off=off:
                 e.scalar_tensor_tensor(P2[:, h * 411 + MW * c:h * 411 + MW * (c + 1)],
                                        ps[:, off:off + MW], vsum_sb[:, h:h + 1],
                                        P1[:, h * NSEL + MW * c:h * NSEL + MW * (c + 1)],
                                        ALU.mult, ALU.add), inc="sDVE")
        rr_guard2 = b.cnt["sPE"]
    b.op("vector", lambda e: e.tensor_copy(P2[:, 410:16 * 411:411], upre_sb[:, :]))
    ctx_ready = b.cnt["sDVE"]
    if debug:
        b.wait("sync", "sDVE", ctx_ready)
        b.dma("sync", ctx_dbg[:, :], P2[:, :], "dDBG")
        b.wait("vector", "dDBG", b.cnt["dDBG"])

    load_w_slabs(wo, ("sPE", s3_last_pm))
    b.wait("tensor", "sDVE", ctx_ready)
    for g in range(4):
        for sl in range(4):
            b.wait("tensor", slot_evac[sl][0], slot_evac[sl][1])
        for o in range(4 * g, 4 * g + 4):
            b.wait("tensor", "dW%d" % o, b.cnt["dW%d" % o])
        for i in range(NT):
            for o in range(4 * g, 4 * g + 4):
                last = (i == NT - 1 and o == 4 * g + 3)
                m = b.op("tensor", lambda e, i=i, o=o, st=(i == 0), sp=(i == NT - 1):
                     e.matmul(psA[:, (o % 4) * 512:(o % 4) * 512 + 411], wtile(i, o),
                              P2[:, i * 411:(i + 1) * 411], start=st, stop=sp),
                     inc="sPE" if last else None)
        b.wait("vector", "sPE", m)
        for o in range(4 * g, 4 * g + 4):
            r = (o % 4) * 512
            b.op("vector", lambda e, o=o, r=r:
                 e.tensor_scalar(u_sb[:, o:o + 1], psA[:, r + 410:r + 411],
                                 bo_sb[:, o:o + 1], None, ALU.add))
            b.op("vector", lambda e, o=o:
                 e.tensor_sub(colt_sb[:, :], bo_sb[:, o:o + 1], u_sb[:, o:o + 1]))
            slot_evac[o % 4] = ("sDVE", b.op("vector", lambda e, o=o, r=r:
                 e.tensor_scalar(DL[:, o * NSEL:(o + 1) * NSEL], psA[:, r:r + NSEL],
                                 colt_sb[:, :], None, ALU.add), inc="sDVE"))
    if debug:
        m = b.cnt["sDVE"]
        b.wait("sync", "sDVE", m)
        b.dma("sync", delta_dbg[:, :], DL[:, :], "dDBG")
        b.dma("sync", u_dbg[:, :], u_sb[:, :], "dDBG")
        b.wait("vector", "dDBG", b.cnt["dDBG"])

    wg3 = r_pit(wgT[:, :])
    wu3 = r_pit(wuT[:, :])
    g3 = r_pit(gated_hbm[:, :])
    slabs = [Bw[:, 0:8192], Bw[:, 8192:16384], Bw[:, 16384:24576], Bw[:, 24576:32768]]
    slab_free = [b.cnt["sPE"], b.cnt["sPE"]]
    gb = (gb0, gb1)
    gb_free = [0, 0]
    gs_free = b.cnt["sDVE"]
    psA_free = 0
    psB_free = b.cnt["sDVE"]
    half_free = s3_last_pm
    b.wait("tensor", "sDVE", b.cnt["sDVE"])
    b.wait("sync", "dWP0", b.cnt["dWP0"])
    b.wait("sync", "dWP1", b.cnt["dWP1"])
    for half in range(2):
        b.wait("sync", "sPE", half_free)
        b.dma("sync", A[:, :].rearrange("p (n t) -> p n t", n=NT),
              xn3[:, :, half * 2048:(half + 1) * 2048], "dXN0")
        for s_ in range(16):
            ps_ = s_ % 2
            b.wait("gpsimd", "sPE", slab_free[ps_])
            b.dma("gpsimd", slabs[2 * ps_][:, :].rearrange("p (n t) -> p n t", n=NT),
                  wg3[:, :, s_ * 512:(s_ + 1) * 512], "dSL%d" % ps_)
            b.dma("gpsimd", slabs[2 * ps_ + 1][:, :].rearrange("p (n t) -> p n t", n=NT),
                  wu3[:, :, s_ * 512:(s_ + 1) * 512], "dSL%d" % ps_)
            for ht in range(4):
                gidx = half * 64 + s_ * 4 + ht
                pg = gidx % 2
                b.wait("tensor", "dSL%d" % ps_, b.cnt["dSL%d" % ps_])
                b.wait("tensor", "dXN0", b.cnt["dXN0"])
                b.wait("tensor", "sACT", psA_free)
                for i in range(NT):
                    for ch in range(4):
                        gate_done = b.op("tensor", lambda e, i=i, ch=ch, sl=slabs[2 * ps_],
                             ht=ht, st=(i == 0), sp=(i == NT - 1):
                             e.matmul(psA[:, ch * 512:(ch + 1) * 512],
                                      sl[:, i * 512 + ht * 128:i * 512 + (ht + 1) * 128],
                                      A[:, i * 2048 + ch * 512:i * 2048 + (ch + 1) * 512],
                                      start=st, stop=sp),
                             inc="sPE" if (i == NT - 1 and ch == 3) else None)
                b.wait("tensor", "sDVE", psB_free)
                for i in range(NT):
                    for ch in range(4):
                        tgt = psS if ch < 2 else psX
                        up_done = b.op("tensor", lambda e, i=i, ch=ch, sl=slabs[2 * ps_ + 1],
                             ht=ht, t=tgt, st=(i == 0), sp=(i == NT - 1):
                             e.matmul(t[:, (ch % 2) * 512:(ch % 2 + 1) * 512],
                                      sl[:, i * 512 + ht * 128:i * 512 + (ht + 1) * 128],
                                      A[:, i * 2048 + ch * 512:i * 2048 + (ch + 1) * 512],
                                      start=st, stop=sp),
                             inc="sPE" if (i == NT - 1 and ch == 3) else None)
                b.wait("scalar", "sPE", gate_done)
                b.wait("scalar", "sDVE", gs_free)
                psA_free = b.op("scalar", lambda e:
                     e.activation(gs_sb[:, :], psA[:, 0:2048], AF.Silu), inc="sACT")
                b.wait("vector", "sPE", up_done)
                b.wait("vector", "sACT", psA_free)
                b.wait("vector", "dGW%d" % pg, gb_free[pg])
                b.op("vector", lambda e, d=gb[pg]:
                     e.tensor_mul(d[:, 0:1024], psS[:, 0:1024], gs_sb[:, 0:1024]))
                psB_free = b.op("vector", lambda e, d=gb[pg]:
                     e.tensor_mul(d[:, 1024:2048], psX[:, 0:1024], gs_sb[:, 1024:2048]),
                     inc="sDVE")
                gs_free = psB_free
                b.wait("scalar", "sDVE", psB_free)
                gb_free[pg] = b.dma("scalar", g3[:, s_ * 4 + ht, half * 2048:(half + 1) * 2048],
                                    gb[pg][:, :], "dGW%d" % pg)
            slab_free[ps_] = b.cnt["sPE"]
        half_free = b.cnt["sPE"]

    xbb = (xb0, xb1)
    outb = (outb0, outb1)
    wdq_free = b.cnt["sPE"]
    ge_free = [b.cnt["sPE"], b.cnt["sPE"]]
    xb_free = [0, 0]
    outb_free = [0, 0]
    psA_slot_free = [0, 0, 0, 0]
    b.wait("sync", "dGW0", b.cnt["dGW0"])
    b.wait("sync", "dGW1", b.cnt["dGW1"])
    b.wait("scalar", "dGW0", b.cnt["dGW0"])
    b.wait("scalar", "dGW1", b.cnt["dGW1"])
    b.wait("tensor", "sDVE", b.cnt["sDVE"])
    b.wait("tensor", "sACT", b.cnt["sACT"])
    for dq in range(4):
        b.wait("gpsimd", "sPE", wdq_free)
        for dtl in range(4):
            b.dma("gpsimd", Bw[:, dtl * 8192:(dtl + 1) * 8192], wd_bf[dq * 4 + dtl, :, :],
                  "dWD%d" % dtl)
        for e_ in range(8):
            pe_ = e_ % 2
            b.wait("scalar", "sDVE", xb_free[pe_])
            b.dma("scalar", xbb[pe_][:, :].rearrange("p (n t) -> p n t", n=4),
                  xT3[:, 4 * dq:4 * dq + 4, e_ * 512:(e_ + 1) * 512], "dXS%d" % pe_)
            for hh in range(2):
                q_ = "sync" if hh == 0 else "scalar"
                b.wait(q_, "sPE", ge_free[hh])
                dst = A[:, hh * 16384:(hh + 1) * 16384].rearrange("p (n t) -> p n t", n=32)
                b.dma(q_, dst, g3[:, hh * 32:(hh + 1) * 32, e_ * 512:(e_ + 1) * 512],
                      "dGE%d" % hh)
                b.wait("tensor", "dGE%d" % hh, b.cnt["dGE%d" % hh])
                for dt in range(4):
                    if hh == 0:
                        b.wait("tensor", "dWD%d" % dt, b.cnt["dWD%d" % dt])
                        b.wait("tensor", "sDVE", psA_slot_free[dt])
                    for h32 in range(32):
                        g_ = hh * 32 + h32
                        ge_free[hh] = b.op("tensor", lambda e, g=g_, dt=dt, hh=hh, h32=h32,
                             st=(hh == 0 and h32 == 0), sp=(hh == 1 and h32 == 31):
                             e.matmul(psA[:, dt * 512:(dt + 1) * 512],
                                      Bw[:, dt * 8192 + g * 128:dt * 8192 + (g + 1) * 128],
                                      A[:, hh * 16384 + h32 * 512:hh * 16384 + (h32 + 1) * 512],
                                      start=st, stop=sp),
                             inc="sPE" if (h32 == 31 and dt == 3) else None)
            mm_done = ge_free[1]
            tl0 = (-512 * e_) % 10
            msta = (512 * e_ + 9) // 10
            scnt = (512 * (e_ + 1) - 1 - (512 * e_ + tl0)) // 10 + 1
            for dt in range(4):
                o = dq * 4 + dt
                po = (e_ * 4 + dt) % 2
                b.wait("vector", "sPE", mm_done)
                b.wait("vector", "dXS%d" % pe_, b.cnt["dXS%d" % pe_])
                b.wait("vector", "dOUT%d" % po, outb_free[po])
                b.op("vector", lambda e, dt=dt, o=o, po=po, pe2=pe_:
                     e.scalar_tensor_tensor(outb[po][:, :], psA[:, dt * 512:(dt + 1) * 512],
                                            u_sb[:, o:o + 1], xbb[pe2][:, dt * 512:(dt + 1) * 512],
                                            ALU.add, ALU.add))
                psA_slot_free[dt] = b.op("vector", lambda e, o=o, po=po, t0=tl0, ms=msta, cn=scnt:
                     e.tensor_add(outb[po][:, t0:t0 + (cn - 1) * 10 + 1:10],
                                  outb[po][:, t0:t0 + (cn - 1) * 10 + 1:10],
                                  DL[:, o * NSEL + ms:o * NSEL + ms + cn]), inc="sDVE")
                xb_free[pe_] = psA_slot_free[dt]
                b.wait("scalar", "sDVE", psA_slot_free[dt])
                outb_free[po] = b.dma("scalar", r_pit(outT[:, :])[:, o, e_ * 512:(e_ + 1) * 512],
                                      outb[po][:, :], "dOUT%d" % po)
        wdq_free = b.cnt["sPE"]
    b.wait("scalar", "dOUT0", b.cnt["dOUT0"])
    b.wait("scalar", "dOUT1", b.cnt["dOUT1"])

    with nc.allow_low_precision("bf16 activations/weights by design (tolerance allows)"):
        with nc.Block() as block:
            @block.sync
            def _(e):
                for fn in b.ops["sync"]:
                    fn(e)

            @block.gpsimd
            def _(e):
                for fn in b.ops["gpsimd"]:
                    fn(e)

            @block.vector
            def _(e):
                for fn in b.ops["vector"]:
                    fn(e)

            @block.scalar
            def _(e):
                for fn in b.ops["scalar"]:
                    fn(e)

            @block.tensor
            def _(e):
                for fn in b.ops["tensor"]:
                    fn(e)

    return nc


def prep_inputs(L0_attn_hidden_states, L0_attn_wq, L0_attn_bq, L0_attn_wk, L0_attn_bk,
                L0_attn_wv, L0_attn_bv, L0_attn_wo, L0_attn_bo,
                L0_ffn_w_gate, L0_ffn_w_up, L0_ffn_w_down):
    x = np.asarray(L0_attn_hidden_states, dtype=np.float32)
    com = {
        "wq": np.ascontiguousarray(np.asarray(L0_attn_wq, dtype=np.float32)),
        "wk": np.ascontiguousarray(np.asarray(L0_attn_wk, dtype=np.float32)),
        "wv": np.ascontiguousarray(np.asarray(L0_attn_wv, dtype=np.float32)),
        "wo": np.ascontiguousarray(np.asarray(L0_attn_wo, dtype=np.float32)),
        "bq": np.ascontiguousarray(np.asarray(L0_attn_bq, dtype=np.float32).reshape(NT, 128).T),
        "bk": np.ascontiguousarray(np.asarray(L0_attn_bk, dtype=np.float32).reshape(NT, 128).T),
        "bv": np.ascontiguousarray(np.asarray(L0_attn_bv, dtype=np.float32).reshape(NT, 128).T),
        "bo": np.ascontiguousarray(np.asarray(L0_attn_bo, dtype=np.float32).reshape(NT, 128).T),
        "wgT": np.ascontiguousarray(np.asarray(L0_ffn_w_gate, dtype=np.float32).T),
        "wuT": np.ascontiguousarray(np.asarray(L0_ffn_w_up, dtype=np.float32).T),
        "wdT": np.ascontiguousarray(np.asarray(L0_ffn_w_down, dtype=np.float32).T),
    }
    in_maps = []
    for bi in range(8):
        m = dict(com)
        m["xT"] = np.ascontiguousarray(x[bi].T)
        in_maps.append(m)
    return in_maps


def kernel(**inputs):
    in_maps = prep_inputs(**inputs)
    nc = build(debug=False)
    res = run_bass_kernel_spmd(nc, in_maps, core_ids=list(range(8)))
    return np.stack([np.ascontiguousarray(r["outT"].T) for r in res.results], axis=0)


# version 08
# speedup vs baseline: 1.2223x; 1.0544x over previous
import sys
if "/opt/trn_rl_repo" not in sys.path:
    sys.path.insert(0, "/opt/trn_rl_repo")
import math
import numpy as np
import concourse.bass as bass
import concourse.mybir as mybir
from concourse.bass_utils import run_bass_kernel_spmd

F32 = mybir.dt.float32
BF16 = mybir.dt.bfloat16
AF = mybir.ActivationFunctionType
ALU = mybir.AluOpType
AX = mybir.AxisListType

D = 2048
NT = 16
S = 4096
NH = 16
CT = 410
NCH = 10
MW = 41
NSEL = 410
FH = 8192
FHT = 64
QSCALE = 1.0 / math.sqrt(128.0)
EPS = 1e-5

ENGS = ("sync", "gpsimd", "vector", "scalar", "tensor")


class Bld:
    AUTO = {"vector": "sDVE", "scalar": "sACT"}

    def __init__(self, nc):
        self.nc = nc
        self.ops = {e: [] for e in ENGS}
        self.sems = {}
        self.cnt = {}

    def sem(self, name):
        if name not in self.sems:
            self.sems[name] = self.nc.ctx.enter_context(self.nc.semaphore(name))
            self.cnt[name] = 0
        return self.sems[name]

    def wait(self, eng, sem, val=None):
        s = self.sem(sem)
        v = self.cnt[sem] if val is None else val
        if v > 0:
            self.ops[eng].append(lambda e, s=s, v=v: e.wait_ge(s, v))

    def op(self, eng, fn, inc=None, incby=1):
        a = self.AUTO.get(eng)
        if a is not None:
            s = self.sem(a)
            v = self.cnt[a]
            if v > 0:
                self.ops[eng].append(lambda e, s=s, v=v: e.wait_ge(s, v))
            if inc is None:
                inc = a
                incby = 1
        if inc is None:
            self.ops[eng].append(fn)
            return 0
        s = self.sem(inc)
        self.cnt[inc] += incby
        self.ops[eng].append(lambda e, fn=fn, s=s, b=incby: fn(e).then_inc(s, b))
        return self.cnt[inc]

    def dma(self, eng, dst, src, sem, **kw):
        return self.op(eng, lambda e, d=dst, s_=src, kw=kw: e.dma_start(d, s_, **kw),
                       inc=sem, incby=16)


def r_pit(dram_ap, p=128):
    return dram_ap.rearrange("(n p) t -> p n t", p=p)


def build(debug=False):
    nc = bass.Bass("TRN2", target_bir_lowering=False)

    xT = nc.dram_tensor("xT", [D, S], F32, kind="ExternalInput")
    wq = nc.dram_tensor("wq", [D, D], F32, kind="ExternalInput")
    wk = nc.dram_tensor("wk", [D, D], F32, kind="ExternalInput")
    wv = nc.dram_tensor("wv", [D, D], F32, kind="ExternalInput")
    wo = nc.dram_tensor("wo", [D, D], F32, kind="ExternalInput")
    bq = nc.dram_tensor("bq", [128, NT], F32, kind="ExternalInput")
    bk = nc.dram_tensor("bk", [128, NT], F32, kind="ExternalInput")
    bv = nc.dram_tensor("bv", [128, NT], F32, kind="ExternalInput")
    bo = nc.dram_tensor("bo", [128, NT], F32, kind="ExternalInput")
    wgT = nc.dram_tensor("wgT", [D, FH], F32, kind="ExternalInput")
    wuT = nc.dram_tensor("wuT", [D, FH], F32, kind="ExternalInput")
    wdT = nc.dram_tensor("wdT", [FH, D], F32, kind="ExternalInput")
    outT = nc.dram_tensor("outT", [D, S], F32, kind="ExternalOutput")

    dbg = "ExternalOutput" if debug else "Internal"
    xn_hbm = nc.dram_tensor("xn_hbm", [D, S], BF16, kind=dbg)
    trow_hbm = nc.dram_tensor("trow_hbm", [NCH, NH, 416], BF16, kind=dbg)
    recip_hbm = nc.dram_tensor("recip_hbm", [NCH, NH * MW], BF16, kind=dbg)
    gated_hbm = nc.dram_tensor("gated_hbm", [FH, S], BF16, kind=dbg)
    wd_bf = nc.dram_tensor("wd_bf", [NT, 128, FHT * 128], BF16, kind="Internal")
    if debug:
        qsel_dbg = nc.dram_tensor("qsel_dbg", [128, NT * NSEL], BF16, kind="ExternalOutput")
        snum_dbg = nc.dram_tensor("snum_dbg", [128, NT * NSEL], BF16, kind="ExternalOutput")
        ctx_dbg = nc.dram_tensor("ctx_dbg", [128, NT * 411], BF16, kind="ExternalOutput")
        delta_dbg = nc.dram_tensor("delta_dbg", [128, NT * NSEL], BF16, kind="ExternalOutput")
        u_dbg = nc.dram_tensor("u_dbg", [128, NT], F32, kind="ExternalOutput")
        vsum_dbg = nc.dram_tensor("vsum_dbg", [128, NT], F32, kind="ExternalOutput")

    BASE = 16544

    def al(name, shape, dtype, offset):
        return nc.alloc_sbuf_tensor_at(name, shape, dtype, offset=BASE + offset)

    A = al("A", [128, 32768], BF16, offset=0)
    Bw = al("Bw", [128, 32768], BF16, offset=65536)
    C = al("C", [128, 13120], BF16, offset=131072)
    P1 = al("P1", [128, 6560], BF16, offset=157312)
    P2 = al("P2", [128, 6576], BF16, offset=170432)
    DL = al("DL", [128, 6560], BF16, offset=183584)
    T = al("T", [128, 2048], F32, offset=196704)
    XF0 = al("XF0", [128, 8192], F32, offset=32768)
    XF1 = al("XF1", [128, 8192], F32, offset=65536)
    WDS0 = al("WDS0", [128, 8192], BF16, offset=32768)
    WDS1 = al("WDS1", [128, 8192], BF16, offset=49152)
    m_sb = al("m_sb", [128, 512], F32, offset=200800)
    r_sb = al("r_sb", [128, 512], F32, offset=202848)
    t1_sb = None
    trw = [al("trw%d" % j, [1, 8 * CT], BF16, offset=157312 + 6560 * j) for j in range(2)]
    trr = [al("trr%d" % j, [1, 8 * CT], BF16, offset=170432 + 6560 * j) for j in range(2)]
    gs_sb = al("gs_sb", [128, 2048], BF16, offset=131072)
    gb0 = al("gb0", [128, 2048], BF16, offset=135168)
    gb1 = al("gb1", [128, 2048], BF16, offset=139264)
    outb0 = al("outb0", [128, 512], F32, offset=131072)
    outb1 = al("outb1", [128, 512], F32, offset=133120)
    xb0 = al("xb0", [128, 2048], F32, offset=135168)
    xb1 = al("xb1", [128, 2048], F32, offset=143360)
    sm = 204896
    bq_sb = al("bq_sb", [128, NT], F32, offset=sm); sm += 64
    bk_sb = al("bk_sb", [128, NT], F32, offset=sm); sm += 64
    bv_sb = al("bv_sb", [128, NT], F32, offset=sm); sm += 64
    bo_sb = al("bo_sb", [128, NT], F32, offset=sm); sm += 64
    u_sb = al("u_sb", [128, NT], F32, offset=sm); sm += 64
    vsum_sb = al("vsum_sb", [128, NT], F32, offset=sm); sm += 64
    upre_sb = al("upre_sb", [128, NT], BF16, offset=sm); sm += 32
    den_sb = al("den_sb", [128, 2 * MW], F32, offset=sm); sm += 352
    num_sb = al("num_sb", [128, 2 * MW], F32, offset=sm); sm += 352
    rrow_sb = al("rrow_sb", [1, NH * MW], BF16, offset=sm); sm += 1344
    ones128 = al("ones128", [128, 128], BF16, offset=sm); sm += 256
    ones1 = al("ones1", [1, 128], BF16, offset=sm); sm += 256
    onesk = al("onesk", [128, 8], BF16, offset=sm); sm += 32
    colt_sb = al("colt_sb", [128, 1], F32, offset=sm); sm += 32
    vt_sb = al("vt_sb", [128, 1], F32, offset=sm); sm += 32
    eps_sb = al("eps_sb", [128, 1], F32, offset=sm); sm += 32
    denr_sb = al("denr_sb", [1, 64], F32, offset=sm); sm += 256
    t1_sb = al("t1_sb", [128, 512], F32, offset=183584)
    ones128f = al("ones128f", [128, 128], F32, offset=sm); sm += 512
    Eb4 = [al("E%d" % j, [128, 2 * CT], BF16, offset=196704 + 1664 * j) for j in range(4)]
    rrw = []
    for j in range(2):
        rrw.append(al("rrw%d" % j, [1, NH * MW], BF16, offset=sm)); sm += 1344
    assert BASE + sm <= 229376 - 64, sm

    psA = nc.alloc_psum_tensor("psA", [128, 2048], F32)
    psS = nc.alloc_psum_tensor("psS", [128, 1024], F32)
    psX = nc.alloc_psum_tensor("psX", [128, 1024], F32)

    b = Bld(nc)
    import builtins
    builtins._BLD = b

    def wtile(i, o):
        return Bw[:, o * 2048 + i * 128:o * 2048 + (i + 1) * 128]

    def load_w_slabs(wsrc, guard=None, olist=range(NT)):
        w3 = r_pit(wsrc[:, :])
        for o in olist:
            if guard is not None:
                b.wait("gpsimd", guard[0], guard[1])
            dst = Bw[:, o * 2048:(o + 1) * 2048].rearrange("p (i c) -> p i c", i=NT)
            b.dma("gpsimd", dst, w3[:, :, o * 128:(o + 1) * 128], "dW%d" % o)

    s_reg = [(psS, 0), (psS, 512), (psX, 0), (psX, 512)]

    b.dma("gpsimd", bq_sb[:, :], bq[:, :], "dB")
    b.dma("gpsimd", bk_sb[:, :], bk[:, :], "dB")
    b.dma("gpsimd", bv_sb[:, :], bv[:, :], "dB")
    b.dma("gpsimd", bo_sb[:, :], bo[:, :], "dB")
    load_w_slabs(wq, None, range(8, 16))
    b.op("vector", lambda e: e.memset(ones128[:, :], 1.0))
    b.op("vector", lambda e: e.memset(ones128f[:, :], 1.0))
    b.op("vector", lambda e: e.memset(ones1[:, :], 1.0))
    b.op("vector", lambda e: e.memset(onesk[:, :], 1.0))
    b.op("vector", lambda e: e.memset(eps_sb[:, :], EPS))
    b.op("vector", lambda e: e.memset(vsum_sb[:, :], 0.0), inc="sDVE")

    XF = (XF0, XF1)
    xstage = (A[:, 0:8192], A[:, 8192:16384])
    xT3 = r_pit(xT[:, :])
    xn3 = r_pit(xn_hbm[:, :])
    xf_free_mark = [0, 0]
    xstage_free_mark = [0, 0]
    selpos = []
    m0 = 0
    for c in range(8):
        tl0 = (-512 * c) % 10
        cnt_ = (512 - tl0 + 9) // 10
        selpos.append((tl0, cnt_, m0))
        m0 += cnt_
    assert m0 == NSEL

    def s0_dma(c):
        par = c % 2
        b.wait("sync", "sDVE", xf_free_mark[par])
        dst = XF[par][:, :].rearrange("p (n t) -> p n t", n=NT)
        b.dma("sync", dst, xT3[:, :, 512 * c:512 * (c + 1)], "dX%d" % par)

    s0_dma(0)
    s0_dma(1)
    sq32 = (al("sq32a", [128, 512], F32, offset=196704),
            al("sq32b", [128, 512], F32, offset=198752))
    mmk = [0] * NT
    statfree = 0
    b.wait("tensor", "sDVE", b.cnt["sDVE"])
    for c in range(8):
        par = c % 2
        for i in range(NT):
            ip = i % 2
            xsl = XF[par][:, 512 * i:512 * (i + 1)]
            if i >= 2:
                b.wait("scalar", "sPE", mmk[i - 2])
            if i == 0:
                b.wait("scalar", "dX%d" % par, 16 * (c // 2 + 1))
            b.op("scalar", lambda e, s=xsl, d=sq32[ip]:
                 e.activation(d[:, :], s, AF.Square), inc="sACT")
            if i == 0:
                b.wait("tensor", "dX%d" % par, 16 * (c // 2 + 1))
                b.wait("tensor", "sDVE", statfree)
            b.op("tensor", lambda e, s=xsl, st=(i == 0), sp=(i == NT - 1):
                 e.matmul(psX[:, 0:512], ones128f[:, :], s, start=st, stop=sp))
            b.wait("tensor", "sACT")
            mmk[i] = b.op("tensor", lambda e, d=sq32[ip], st=(i == 0), sp=(i == NT - 1):
                 e.matmul(psX[:, 512:1024], ones128f[:, :], d[:, :], start=st, stop=sp),
                 inc="sPE")
        b.wait("vector", "sPE", mmk[NT - 1])
        b.op("vector", lambda e: e.tensor_scalar(m_sb[:, :], psX[:, 0:512], 1.0 / D, None, ALU.mult))
        b.op("vector", lambda e: e.tensor_mul(r_sb[:, :], m_sb[:, :], m_sb[:, :]))
        statfree = b.op("vector", lambda e: e.scalar_tensor_tensor(
            r_sb[:, :], psX[:, 512:1024], 1.0 / D, r_sb[:, :], ALU.mult, ALU.subtract), inc="sDVE")
        b.wait("scalar", "sDVE", statfree)
        b.op("scalar", lambda e: e.activation(r_sb[:, :], r_sb[:, :], AF.Sqrt, bias=eps_sb[:, 0:1]),
             inc="sACT")
        b.wait("vector", "sACT")
        b.op("vector", lambda e: e.reciprocal(r_sb[:, :], r_sb[:, :]))
        b.wait("vector", "dXNW%d" % par, xstage_free_mark[par])
        tl0, cnt_, mst = selpos[c]
        for i in range(NT):
            xsl = XF[par][:, 512 * i:512 * (i + 1)]
            stg = xstage[par][:, 512 * i:512 * (i + 1)]
            b.op("vector", lambda e, x=xsl: e.tensor_sub(t1_sb[:, :], x, m_sb[:, :]))
            b.op("vector", lambda e, d=stg: e.tensor_mul(d, t1_sb[:, :], r_sb[:, :]))
            b.op("vector", lambda e, d=P1[:, i * NSEL + mst:i * NSEL + mst + cnt_],
                 s=xstage[par][:, 512 * i + tl0:512 * (i + 1):10]: e.tensor_copy(d, s))
        xf_free_mark[par] = b.op("vector", lambda e: e.engine_nop(), inc="sDVE")
        b.wait("sync", "sDVE", xf_free_mark[par])
        src2 = xstage[par][:, :].rearrange("p (n t) -> p n t", n=NT)
        xstage_free_mark[par] = b.dma("sync", xn3[:, :, 512 * c:512 * (c + 1)], src2,
                                      "dXNW%d" % par)
        if c + 2 < 8:
            s0_dma(c + 2)
    sel_ready = b.cnt["sDVE"]

    slot_evac = [("sDVE", 0), ("sDVE", 0), ("sDVE", 0), ("sDVE", 0)]

    load_w_slabs(wq, ("sDVE", xf_free_mark[1]), range(0, 8))
    b.wait("tensor", "sDVE", sel_ready)
    b.wait("vector", "dB", 64)
    first_g = True
    for g in (2, 3, 0, 1):
        if not first_g:
            b.wait("tensor", "sDVE", b.cnt["sDVE"])
        first_g = False
        for o in range(4 * g, 4 * g + 4):
            b.wait("tensor", "dW%d" % o, b.cnt["dW%d" % o])
        for i in range(NT):
            for o in range(4 * g, 4 * g + 4):
                last = (i == NT - 1 and o == 4 * g + 3)
                m = b.op("tensor", lambda e, i=i, o=o, st=(i == 0), sp=(i == NT - 1):
                     e.matmul(psA[:, (o % 4) * 512:(o % 4) * 512 + NSEL], wtile(i, o),
                              P1[:, i * NSEL:(i + 1) * NSEL], start=st, stop=sp),
                     inc="sPE" if last else None)
        b.wait("vector", "sPE", m)
        for o in range(4 * g, 4 * g + 4):
            slot_evac[o % 4] = ("sDVE", b.op("vector", lambda e, o=o:
                 e.tensor_scalar(P2[:, o * NSEL:(o + 1) * NSEL],
                                 psA[:, (o % 4) * 512:(o % 4) * 512 + NSEL],
                                 bq_sb[:, o:o + 1], QSCALE, ALU.add, ALU.mult), inc="sDVE"))
    qsel_ready = b.cnt["sDVE"]
    s1_pe_done = b.cnt["sPE"]
    if debug:
        b.wait("sync", "sDVE", qsel_ready)
        b.dma("sync", qsel_dbg[:, :], P2[:, 0:NT * NSEL], "dDBG")
        b.wait("vector", "dDBG", b.cnt["dDBG"])

    Cb = (C[:, 0:NT * CT], C[:, NT * CT:2 * NT * CT])
    xnbuf_free = [0, 0]
    cb_free = [0, 0]
    pms = {}

    def xn_dma(c, dve_guard=0):
        par = c % 2
        b.wait("sync", "sPE", xnbuf_free[par])
        if dve_guard:
            b.wait("sync", "sDVE", dve_guard)
        b.wait("sync", "dXNW0", 64)
        b.wait("sync", "dXNW1", 64)
        dst = A[:, par * NT * CT:(par + 1) * NT * CT].rearrange("p (n t) -> p n t", n=NT)
        L = CT if c < NCH - 1 else S - (NCH - 1) * CT
        b.dma("sync", dst[:, :, 0:L], xn3[:, :, c * CT:c * CT + L], "dXN%d" % par)

    def proj_chunk(c, wcnt, bias_col):
        par = c % 2
        kv_len = CT if c < NCH - 1 else S - (NCH - 1) * CT
        xb_ = A[:, par * NT * CT:(par + 1) * NT * CT]
        b.wait("tensor", "dXN%d" % par, b.cnt["dXN%d" % par])
        b.wait("scalar", "sDVE", cb_free[par])
        for o in range(NT):
            reg = (o % 4) * 512
            b.wait("tensor", slot_evac[o % 4][0], slot_evac[o % 4][1])
            b.wait("tensor", "dW%d" % o, wcnt[o])
            for i in range(NT):
                m = b.op("tensor", lambda e, i=i, o=o, r=reg, x=xb_,
                         st=(i == 0), sp=(i == NT - 1):
                         e.matmul(psA[:, r:r + CT], wtile(i, o), x[:, i * CT:(i + 1) * CT],
                                  start=st, stop=sp),
                         inc="sPE" if i == NT - 1 else None)
            b.wait("scalar", "sPE", m)
            ev = b.op("scalar", lambda e, o=o, r=reg, bc=bias_col, dst=Cb[par]:
                 e.activation(dst[:, o * CT:(o + 1) * CT].rearrange(
                                  "p (d m) -> p m d", d=10, m=MW),
                              psA[:, r:r + CT].rearrange("p (m d) -> p m d", m=MW, d=10),
                              AF.Identity, bias=bc[:, o:o + 1]), inc="sACT")
            slot_evac[o % 4] = ("sACT", ev)
        if kv_len < CT:
            b.wait("vector", "sACT", b.cnt["sACT"])
            pad = Cb[par][:, :].rearrange("p (o d m) -> p o d m", o=NT, d=10, m=MW)
            b.op("vector", lambda e, pad=pad:
                 e.memset(pad[:, :, 6:10, MW - 1:MW], 0.0))
        pms[c] = m
        return m

    def kslice3(h, par):
        return Cb[par][:, h * CT:(h + 1) * CT].rearrange("p (d m) -> p d m", d=10, m=MW)

    E_free = [0, 0, 0, 0]
    psR_free = [0, 0, 0, 0]
    trw_free = [0, 0]
    trr_free = [0, 0]

    def scores_chunk(c, evac_mark):
        par = c % 2
        b.wait("vector", "sACT", evac_mark)
        for g in range(8):
            h0 = 2 * g
            rot = g % 4
            pg = g % 2
            hb = h0 // 8
            ps_ = psS if pg == 0 else psX
            Eb = Eb4[rot]
            b.wait("vector", "sPE", E_free[rot])
            q2 = P2[:, h0 * NSEL + MW * c:(h0 + 1) * NSEL + MW * (c + 1)]
            qap = bass.AP(q2.tensor, q2.offset, [[6576, 128], [NSEL, 2], [0, 10], [1, MW]])
            kap = Cb[par][:, h0 * CT:(h0 + 2) * CT].rearrange(
                "p (g d m) -> p g d m", g=2, d=10, m=MW)
            b.op("vector", lambda e, qap=qap, kap=kap, Eb=Eb:
                 e.tensor_mul(Eb[:, :].rearrange("p (g d m) -> p g d m", g=2, d=10),
                              qap, kap), inc="sDVE")
            E_mark = b.cnt["sDVE"]
            b.wait("tensor", "sDVE", E_mark)
            b.wait("tensor", "sDVE", psR_free[pg])
            b.op("tensor", lambda e, Eb=Eb, ps=ps_:
                 e.matmul(ps[0:1, 0:512], onesk[:, 0:1], Eb[:, 0:512],
                          start=True, stop=True))
            E_free[rot] = b.op("tensor", lambda e, Eb=Eb, ps=ps_:
                 e.matmul(ps[0:1, 512:820], onesk[:, 0:1], Eb[:, 512:820],
                          start=True, stop=True), inc="sPE")
            b.wait("scalar", "sPE", E_free[rot])
            b.op("scalar", lambda e, ps=ps_:
                 e.activation(ps[0:1, 0:820], ps[0:1, 0:820], AF.Exp), inc="sACT")
            b.wait("vector", "sACT", b.cnt["sACT"])
            if g % 4 == 0:
                b.wait("vector", "dT%d" % hb, trw_free[hb])
            psR_free[pg] = b.op("vector", lambda e, ps=ps_, t=trw[hb], hl=h0 % 8:
                 e.tensor_scalar(t[0:1, hl * CT:(hl + 2) * CT], ps[0:1, 0:820],
                                 -1.0, None, ALU.add), inc="sDVE")
            if g % 4 == 3:
                m2 = b.cnt["sDVE"]
                b.wait("sync", "sDVE", m2)
                trw_free[hb] = b.dma("sync", trow_hbm[c, hb * 8:hb * 8 + 8, 0:CT],
                                     trw[hb][0:1, :].rearrange("o (g m) -> o g m", g=8),
                                     "dT%d" % hb)

    rrow_free = [0]
    def num_chunk(c, evac_mark):
        par = c % 2
        b.wait("vector", "sACT", evac_mark)
        for g in range(8):
            h0 = 2 * g
            pg = g % 2
            hb = h0 // 8
            ps_ = psS if pg == 0 else psX
            if h0 % 8 == 0:
                b.wait("sync", "sPE", trr_free[hb])
                b.dma("sync", trr[hb][0:1, :].rearrange("o (g m) -> o g m", g=8),
                      trow_hbm[c, hb * 8:hb * 8 + 8, 0:CT], "dTR%d" % hb)
            b.wait("tensor", "dTR%d" % hb, b.cnt["dTR%d" % hb])
            b.wait("tensor", "sDVE", psR_free[pg])
            b.op("tensor", lambda e, t=trr[hb], hl=h0 % 8, ps=ps_:
                 e.matmul(ps[:, 0:512], ones1[:, :], t[0:1, hl * CT:hl * CT + 512],
                          start=True, stop=True))
            trr_free[hb] = b.op("tensor", lambda e, t=trr[hb], hl=h0 % 8, ps=ps_:
                 e.matmul(ps[:, 512:820], ones1[:, :], t[0:1, hl * CT + 512:hl * CT + 820],
                          start=True, stop=True), inc="sPE")
            b.wait("vector", "sPE", trr_free[hb])
            kap = Cb[par][:, h0 * CT:(h0 + 2) * CT].rearrange(
                "p (g d m) -> p g d m", g=2, d=10, m=MW)
            b.op("vector", lambda e, ps=ps_, kap=kap:
                 e.tensor_mul(T[:, 0:820].rearrange("p (g d m) -> p g d m", g=2, d=10),
                              ps[:, 0:820].rearrange("p (g d m) -> p g d m", g=2, d=10),
                              kap))
            b.op("vector", lambda e:
                 e.tensor_reduce(num_sb[:, :],
                                 T[:, 0:820].rearrange("p (g d m) -> p g m d", g=2, d=10),
                                 AX.X, ALU.add))
            psR_free[pg] = b.op("vector", lambda e, ps=ps_:
                 e.tensor_reduce(den_sb[:, :],
                                 ps[:, 0:820].rearrange("p (g d m) -> p g m d", g=2, d=10),
                                 AX.X, ALU.add), inc="sDVE")
            b.op("vector", lambda e: e.tensor_scalar(den_sb[:, :], den_sb[:, :], float(S),
                                                     None, ALU.add))
            b.op("vector", lambda e: e.reciprocal(den_sb[:, :], den_sb[:, :]))
            pap = P1[:, h0 * NSEL + MW * c:(h0 + 1) * NSEL + MW * (c + 1)]
            pap2 = bass.AP(pap.tensor, pap.offset, [[6560, 128], [NSEL, 2], [1, MW]])
            b.op("vector", lambda e, pap2=pap2:
                 e.tensor_mul(pap2, num_sb[:, :].rearrange("p (g m) -> p g m", g=2),
                              den_sb[:, :].rearrange("p (g m) -> p g m", g=2)))
            if g == 0:
                b.wait("vector", "dRW", rrow_free[0])
            b.op("vector", lambda e, h0=h0, t=rrw[c % 2]:
                 e.tensor_copy(t[0:1, h0 * MW:(h0 + 2) * MW], den_sb[0:1, :]), inc="sDVE")
        for i in range(NT):
            b.op("vector", lambda e, i=i, src=Cb[par]:
                 e.tensor_reduce(vt_sb[:, :], src[:, i * CT:(i + 1) * CT], AX.X, ALU.add))
            b.op("vector", lambda e, i=i: e.tensor_add(vsum_sb[:, i:i + 1], vsum_sb[:, i:i + 1],
                                                       vt_sb[:, :]))
        cb_free[par] = b.cnt["sDVE"]
        m2 = b.cnt["sDVE"]
        b.wait("sync", "sDVE", m2)
        rrow_free[0] = b.dma("sync", recip_hbm[c:c + 1, :], rrw[c % 2][0:1, :], "dRW")

    rr_guard = [0]

    wdT3 = r_pit(wdT[:, :])
    WDS = (WDS0, WDS1)
    wds_free = [0, 0]

    def wd_precast(o):
        pw = o % 2
        b.wait("gpsimd", "dWP%d" % pw, wds_free[pw])
        b.dma("gpsimd", WDS[pw][:, :].rearrange("p (g c) -> p g c", g=FHT),
              wdT3[:, :, o * 128:(o + 1) * 128], "dWC")
        b.wait("gpsimd", "dWC", b.cnt["dWC"])
        wds_free[pw] = b.dma("gpsimd", wd_bf[o, :, :], WDS[pw][:, :], "dWP%d" % pw)

    load_w_slabs(wk, ("sPE", s1_pe_done))
    wk_cnt = [b.cnt["dW%d" % o] for o in range(NT)]
    xn_dma(0, dve_guard=sel_ready)
    xn_dma(1, dve_guard=sel_ready)
    ev_mark = {}
    for c in range(NCH):
        if c >= 1:
            scores_chunk(c - 1, ev_mark[c - 1])
        pm = proj_chunk(c, wk_cnt, bk_sb)
        ev_mark[c] = b.cnt["sACT"]
        if c + 2 < NCH:
            xnbuf_free[c % 2] = pm
            xn_dma(c + 2)
        if c < 10:
            wd_precast(c)
    scores_chunk(NCH - 1, ev_mark[NCH - 1])

    load_w_slabs(wv, ("sPE", pms[NCH - 1]))
    wv_cnt = [b.cnt["dW%d" % o] for o in range(NT)]
    b.wait("vector", "dT0", b.cnt["dT0"])
    b.wait("vector", "dT1", b.cnt["dT1"])
    xnbuf_free = [pms[NCH - 2], pms[NCH - 1]]
    xn_dma(0)
    xn_dma(1)
    for c in range(NCH):
        if c >= 1:
            num_chunk(c - 1, ev_mark[c - 1])
        pm = proj_chunk(c, wv_cnt, bv_sb)
        ev_mark[c] = b.cnt["sACT"]
        if c + 2 < NCH:
            xnbuf_free[c % 2] = pm
            xn_dma(c + 2)
        if c + 10 < NT:
            wd_precast(c + 10)
    num_chunk(NCH - 1, ev_mark[NCH - 1])
    s3_last_pm = pms[NCH - 1]
    if debug:
        m = b.op("vector", lambda e: e.engine_nop(), inc="sDVE")
        b.wait("sync", "sDVE", m)
        b.dma("sync", snum_dbg[:, :], P1[:, :], "dDBG")
        b.dma("sync", vsum_dbg[:, :], vsum_sb[:, :], "dDBG")
        b.wait("vector", "dDBG", b.cnt["dDBG"])

    b.op("vector", lambda e: e.tensor_scalar(upre_sb[:, :], vsum_sb[:, :], 1.0 / S, None,
                                             ALU.mult), inc="sDVE")
    psR_free = [b.cnt["sDVE"]] * 4
    rr_guard2 = b.cnt["sPE"]
    for c in range(NCH):
        b.wait("sync", "dRW", b.cnt["dRW"])
        b.wait("sync", "sPE", rr_guard2)
        b.dma("sync", rrow_sb[0:1, :], recip_hbm[c:c + 1, :], "dRR")
        for h in range(NH):
            rot = h % 4
            ps_, off = s_reg[rot]
            b.wait("tensor", "dRR", b.cnt["dRR"])
            b.wait("tensor", "sDVE", psR_free[rot])
            b.op("tensor", lambda e, h=h, ps=ps_, off=off:
                 e.matmul(ps[:, off:off + MW], ones1[:, :], rrow_sb[0:1, h * MW:(h + 1) * MW],
                          start=True, stop=True), inc="sPE")
            b.wait("vector", "sPE", b.cnt["sPE"])
            psR_free[rot] = b.op("vector", lambda e, h=h, c=c, ps=ps_, off=off:
                 e.scalar_tensor_tensor(P2[:, h * 411 + MW * c:h * 411 + MW * (c + 1)],
                                        ps[:, off:off + MW], vsum_sb[:, h:h + 1],
                                        P1[:, h * NSEL + MW * c:h * NSEL + MW * (c + 1)],
                                        ALU.mult, ALU.add), inc="sDVE")
        rr_guard2 = b.cnt["sPE"]
    b.op("vector", lambda e: e.tensor_copy(P2[:, 410:16 * 411:411], upre_sb[:, :]))
    ctx_ready = b.cnt["sDVE"]
    if debug:
        b.wait("sync", "sDVE", ctx_ready)
        b.dma("sync", ctx_dbg[:, :], P2[:, :], "dDBG")
        b.wait("vector", "dDBG", b.cnt["dDBG"])

    load_w_slabs(wo, ("sPE", s3_last_pm))
    b.wait("tensor", "sDVE", ctx_ready)
    for g in range(4):
        for sl in range(4):
            b.wait("tensor", slot_evac[sl][0], slot_evac[sl][1])
        for o in range(4 * g, 4 * g + 4):
            b.wait("tensor", "dW%d" % o, b.cnt["dW%d" % o])
        for i in range(NT):
            for o in range(4 * g, 4 * g + 4):
                last = (i == NT - 1 and o == 4 * g + 3)
                m = b.op("tensor", lambda e, i=i, o=o, st=(i == 0), sp=(i == NT - 1):
                     e.matmul(psA[:, (o % 4) * 512:(o % 4) * 512 + 411], wtile(i, o),
                              P2[:, i * 411:(i + 1) * 411], start=st, stop=sp),
                     inc="sPE" if last else None)
        b.wait("vector", "sPE", m)
        for o in range(4 * g, 4 * g + 4):
            r = (o % 4) * 512
            b.op("vector", lambda e, o=o, r=r:
                 e.tensor_scalar(u_sb[:, o:o + 1], psA[:, r + 410:r + 411],
                                 bo_sb[:, o:o + 1], None, ALU.add))
            b.op("vector", lambda e, o=o:
                 e.tensor_sub(colt_sb[:, :], bo_sb[:, o:o + 1], u_sb[:, o:o + 1]))
            slot_evac[o % 4] = ("sDVE", b.op("vector", lambda e, o=o, r=r:
                 e.tensor_scalar(DL[:, o * NSEL:(o + 1) * NSEL], psA[:, r:r + NSEL],
                                 colt_sb[:, :], None, ALU.add), inc="sDVE"))
    if debug:
        m = b.cnt["sDVE"]
        b.wait("sync", "sDVE", m)
        b.dma("sync", delta_dbg[:, :], DL[:, :], "dDBG")
        b.dma("sync", u_dbg[:, :], u_sb[:, :], "dDBG")
        b.wait("vector", "dDBG", b.cnt["dDBG"])

    wg3 = r_pit(wgT[:, :])
    wu3 = r_pit(wuT[:, :])
    g3 = r_pit(gated_hbm[:, :])
    slabs = [Bw[:, 0:8192], Bw[:, 8192:16384], Bw[:, 16384:24576], Bw[:, 24576:32768]]
    slab_free = [b.cnt["sPE"], b.cnt["sPE"]]
    gb = (gb0, gb1)
    gb_free = [0, 0]
    gs_free = b.cnt["sDVE"]
    psA_free = 0
    psB_free = b.cnt["sDVE"]
    half_free = s3_last_pm
    b.wait("tensor", "sDVE", b.cnt["sDVE"])
    b.wait("sync", "dWP0", b.cnt["dWP0"])
    b.wait("sync", "dWP1", b.cnt["dWP1"])
    for half in range(2):
        b.wait("sync", "sPE", half_free)
        b.dma("sync", A[:, :].rearrange("p (n t) -> p n t", n=NT),
              xn3[:, :, half * 2048:(half + 1) * 2048], "dXN0")
        for s_ in range(16):
            ps_ = s_ % 2
            b.wait("gpsimd", "sPE", slab_free[ps_])
            b.dma("gpsimd", slabs[2 * ps_][:, :].rearrange("p (n t) -> p n t", n=NT),
                  wg3[:, :, s_ * 512:(s_ + 1) * 512], "dSL%d" % ps_)
            b.dma("gpsimd", slabs[2 * ps_ + 1][:, :].rearrange("p (n t) -> p n t", n=NT),
                  wu3[:, :, s_ * 512:(s_ + 1) * 512], "dSL%d" % ps_)
            for ht in range(4):
                gidx = half * 64 + s_ * 4 + ht
                pg = gidx % 2
                b.wait("tensor", "dSL%d" % ps_, b.cnt["dSL%d" % ps_])
                b.wait("tensor", "dXN0", b.cnt["dXN0"])
                b.wait("tensor", "sACT", psA_free)
                for i in range(NT):
                    for ch in range(4):
                        gate_done = b.op("tensor", lambda e, i=i, ch=ch, sl=slabs[2 * ps_],
                             ht=ht, st=(i == 0), sp=(i == NT - 1):
                             e.matmul(psA[:, ch * 512:(ch + 1) * 512],
                                      sl[:, i * 512 + ht * 128:i * 512 + (ht + 1) * 128],
                                      A[:, i * 2048 + ch * 512:i * 2048 + (ch + 1) * 512],
                                      start=st, stop=sp),
                             inc="sPE" if (i == NT - 1 and ch == 3) else None)
                b.wait("tensor", "sDVE", psB_free)
                for i in range(NT):
                    for ch in range(4):
                        tgt = psS if ch < 2 else psX
                        up_done = b.op("tensor", lambda e, i=i, ch=ch, sl=slabs[2 * ps_ + 1],
                             ht=ht, t=tgt, st=(i == 0), sp=(i == NT - 1):
                             e.matmul(t[:, (ch % 2) * 512:(ch % 2 + 1) * 512],
                                      sl[:, i * 512 + ht * 128:i * 512 + (ht + 1) * 128],
                                      A[:, i * 2048 + ch * 512:i * 2048 + (ch + 1) * 512],
                                      start=st, stop=sp),
                             inc="sPE" if (i == NT - 1 and ch == 3) else None)
                b.wait("scalar", "sPE", gate_done)
                b.wait("scalar", "sDVE", gs_free)
                psA_free = b.op("scalar", lambda e:
                     e.activation(gs_sb[:, :], psA[:, 0:2048], AF.Silu), inc="sACT")
                b.wait("vector", "sPE", up_done)
                b.wait("vector", "sACT", psA_free)
                b.wait("vector", "dGW%d" % pg, gb_free[pg])
                b.op("vector", lambda e, d=gb[pg]:
                     e.tensor_mul(d[:, 0:1024], psS[:, 0:1024], gs_sb[:, 0:1024]))
                psB_free = b.op("vector", lambda e, d=gb[pg]:
                     e.tensor_mul(d[:, 1024:2048], psX[:, 0:1024], gs_sb[:, 1024:2048]),
                     inc="sDVE")
                gs_free = psB_free
                b.wait("scalar", "sDVE", psB_free)
                gb_free[pg] = b.dma("scalar", g3[:, s_ * 4 + ht, half * 2048:(half + 1) * 2048],
                                    gb[pg][:, :], "dGW%d" % pg)
            slab_free[ps_] = b.cnt["sPE"]
        half_free = b.cnt["sPE"]

    xbb = (xb0, xb1)
    outb = (outb0, outb1)
    wdq_free = b.cnt["sPE"]
    ge_free = [b.cnt["sPE"], b.cnt["sPE"]]
    xb_free = [0, 0]
    outb_free = [0, 0]
    psA_slot_free = [0, 0, 0, 0]
    b.wait("sync", "dGW0", b.cnt["dGW0"])
    b.wait("sync", "dGW1", b.cnt["dGW1"])
    b.wait("scalar", "dGW0", b.cnt["dGW0"])
    b.wait("scalar", "dGW1", b.cnt["dGW1"])
    b.wait("tensor", "sDVE", b.cnt["sDVE"])
    b.wait("tensor", "sACT", b.cnt["sACT"])
    for dq in range(4):
        for dtl in range(4):
            if dq == 0 and dtl < 2:
                b.wait("gpsimd", "sPE", slab_free[0])
            else:
                b.wait("gpsimd", "sPE", wdq_free)
            b.dma("gpsimd", Bw[:, dtl * 8192:(dtl + 1) * 8192], wd_bf[dq * 4 + dtl, :, :],
                  "dWD%d" % dtl)
        for e_ in range(8):
            pe_ = e_ % 2
            b.wait("scalar", "sDVE", xb_free[pe_])
            b.dma("scalar", xbb[pe_][:, :].rearrange("p (n t) -> p n t", n=4),
                  xT3[:, 4 * dq:4 * dq + 4, e_ * 512:(e_ + 1) * 512], "dXS%d" % pe_)
            for hh in range(2):
                q_ = "sync" if hh == 0 else "scalar"
                b.wait(q_, "sPE", ge_free[hh])
                dst = A[:, hh * 16384:(hh + 1) * 16384].rearrange("p (n t) -> p n t", n=32)
                b.dma(q_, dst, g3[:, hh * 32:(hh + 1) * 32, e_ * 512:(e_ + 1) * 512],
                      "dGE%d" % hh)
                b.wait("tensor", "dGE%d" % hh, b.cnt["dGE%d" % hh])
                for dt in range(4):
                    if hh == 0:
                        b.wait("tensor", "dWD%d" % dt, b.cnt["dWD%d" % dt])
                        b.wait("tensor", "sDVE", psA_slot_free[dt])
                    for h32 in range(32):
                        g_ = hh * 32 + h32
                        ge_free[hh] = b.op("tensor", lambda e, g=g_, dt=dt, hh=hh, h32=h32,
                             st=(hh == 0 and h32 == 0), sp=(hh == 1 and h32 == 31):
                             e.matmul(psA[:, dt * 512:(dt + 1) * 512],
                                      Bw[:, dt * 8192 + g * 128:dt * 8192 + (g + 1) * 128],
                                      A[:, hh * 16384 + h32 * 512:hh * 16384 + (h32 + 1) * 512],
                                      start=st, stop=sp),
                             inc="sPE" if (h32 == 31 and dt == 3) else None)
            mm_done = ge_free[1]
            tl0 = (-512 * e_) % 10
            msta = (512 * e_ + 9) // 10
            scnt = (512 * (e_ + 1) - 1 - (512 * e_ + tl0)) // 10 + 1
            for dt in range(4):
                o = dq * 4 + dt
                po = (e_ * 4 + dt) % 2
                b.wait("vector", "sPE", mm_done)
                b.wait("vector", "dXS%d" % pe_, b.cnt["dXS%d" % pe_])
                b.wait("vector", "dOUT%d" % po, outb_free[po])
                b.op("vector", lambda e, dt=dt, o=o, po=po, pe2=pe_:
                     e.scalar_tensor_tensor(outb[po][:, :], psA[:, dt * 512:(dt + 1) * 512],
                                            u_sb[:, o:o + 1], xbb[pe2][:, dt * 512:(dt + 1) * 512],
                                            ALU.add, ALU.add))
                psA_slot_free[dt] = b.op("vector", lambda e, o=o, po=po, t0=tl0, ms=msta, cn=scnt:
                     e.tensor_add(outb[po][:, t0:t0 + (cn - 1) * 10 + 1:10],
                                  outb[po][:, t0:t0 + (cn - 1) * 10 + 1:10],
                                  DL[:, o * NSEL + ms:o * NSEL + ms + cn]), inc="sDVE")
                xb_free[pe_] = psA_slot_free[dt]
                b.wait("scalar", "sDVE", psA_slot_free[dt])
                outb_free[po] = b.dma("scalar", r_pit(outT[:, :])[:, o, e_ * 512:(e_ + 1) * 512],
                                      outb[po][:, :], "dOUT%d" % po)
        wdq_free = b.cnt["sPE"]
    b.wait("scalar", "dOUT0", b.cnt["dOUT0"])
    b.wait("scalar", "dOUT1", b.cnt["dOUT1"])

    with nc.allow_low_precision("bf16 activations/weights by design (tolerance allows)"):
        with nc.Block() as block:
            @block.sync
            def _(e):
                for fn in b.ops["sync"]:
                    fn(e)

            @block.gpsimd
            def _(e):
                for fn in b.ops["gpsimd"]:
                    fn(e)

            @block.vector
            def _(e):
                for fn in b.ops["vector"]:
                    fn(e)

            @block.scalar
            def _(e):
                for fn in b.ops["scalar"]:
                    fn(e)

            @block.tensor
            def _(e):
                for fn in b.ops["tensor"]:
                    fn(e)

    return nc


def prep_inputs(L0_attn_hidden_states, L0_attn_wq, L0_attn_bq, L0_attn_wk, L0_attn_bk,
                L0_attn_wv, L0_attn_bv, L0_attn_wo, L0_attn_bo,
                L0_ffn_w_gate, L0_ffn_w_up, L0_ffn_w_down):
    x = np.asarray(L0_attn_hidden_states, dtype=np.float32)
    com = {
        "wq": np.ascontiguousarray(np.asarray(L0_attn_wq, dtype=np.float32)),
        "wk": np.ascontiguousarray(np.asarray(L0_attn_wk, dtype=np.float32)),
        "wv": np.ascontiguousarray(np.asarray(L0_attn_wv, dtype=np.float32)),
        "wo": np.ascontiguousarray(np.asarray(L0_attn_wo, dtype=np.float32)),
        "bq": np.ascontiguousarray(np.asarray(L0_attn_bq, dtype=np.float32).reshape(NT, 128).T),
        "bk": np.ascontiguousarray(np.asarray(L0_attn_bk, dtype=np.float32).reshape(NT, 128).T),
        "bv": np.ascontiguousarray(np.asarray(L0_attn_bv, dtype=np.float32).reshape(NT, 128).T),
        "bo": np.ascontiguousarray(np.asarray(L0_attn_bo, dtype=np.float32).reshape(NT, 128).T),
        "wgT": np.ascontiguousarray(np.asarray(L0_ffn_w_gate, dtype=np.float32).T),
        "wuT": np.ascontiguousarray(np.asarray(L0_ffn_w_up, dtype=np.float32).T),
        "wdT": np.ascontiguousarray(np.asarray(L0_ffn_w_down, dtype=np.float32).T),
    }
    in_maps = []
    for bi in range(8):
        m = dict(com)
        m["xT"] = np.ascontiguousarray(x[bi].T)
        in_maps.append(m)
    return in_maps


def kernel(**inputs):
    in_maps = prep_inputs(**inputs)
    nc = build(debug=False)
    res = run_bass_kernel_spmd(nc, in_maps, core_ids=list(range(8)))
    return np.stack([np.ascontiguousarray(r["outT"].T) for r in res.results], axis=0)


# version 09
# speedup vs baseline: 1.2347x; 1.0101x over previous
import sys
if "/opt/trn_rl_repo" not in sys.path:
    sys.path.insert(0, "/opt/trn_rl_repo")
import math
import numpy as np
import concourse.bass as bass
import concourse.mybir as mybir
from concourse.bass_utils import run_bass_kernel_spmd

F32 = mybir.dt.float32
BF16 = mybir.dt.bfloat16
AF = mybir.ActivationFunctionType
ALU = mybir.AluOpType
AX = mybir.AxisListType

D = 2048
NT = 16
S = 4096
NH = 16
CT = 410
NCH = 10
MW = 41
NSEL = 410
FH = 8192
FHT = 64
QSCALE = 1.0 / math.sqrt(128.0)
EPS = 1e-5

ENGS = ("sync", "gpsimd", "vector", "scalar", "tensor")


class Bld:
    AUTO = {"vector": "sDVE", "scalar": "sACT"}

    def __init__(self, nc):
        self.nc = nc
        self.ops = {e: [] for e in ENGS}
        self.sems = {}
        self.cnt = {}

    def sem(self, name):
        if name not in self.sems:
            self.sems[name] = self.nc.ctx.enter_context(self.nc.semaphore(name))
            self.cnt[name] = 0
        return self.sems[name]

    def wait(self, eng, sem, val=None):
        s = self.sem(sem)
        v = self.cnt[sem] if val is None else val
        if v > 0:
            self.ops[eng].append(lambda e, s=s, v=v: e.wait_ge(s, v))

    def op(self, eng, fn, inc=None, incby=1):
        a = self.AUTO.get(eng)
        if a is not None:
            s = self.sem(a)
            v = self.cnt[a]
            if v > 0:
                self.ops[eng].append(lambda e, s=s, v=v: e.wait_ge(s, v))
            if inc is None:
                inc = a
                incby = 1
        if inc is None:
            self.ops[eng].append(fn)
            return 0
        s = self.sem(inc)
        self.cnt[inc] += incby
        self.ops[eng].append(lambda e, fn=fn, s=s, b=incby: fn(e).then_inc(s, b))
        return self.cnt[inc]

    def dma(self, eng, dst, src, sem, **kw):
        return self.op(eng, lambda e, d=dst, s_=src, kw=kw: e.dma_start(d, s_, **kw),
                       inc=sem, incby=16)


def r_pit(dram_ap, p=128):
    return dram_ap.rearrange("(n p) t -> p n t", p=p)


def build(debug=False):
    nc = bass.Bass("TRN2", target_bir_lowering=False)

    xT = nc.dram_tensor("xT", [D, S], F32, kind="ExternalInput")
    wq = nc.dram_tensor("wq", [D, D], F32, kind="ExternalInput")
    wk = nc.dram_tensor("wk", [D, D], F32, kind="ExternalInput")
    wv = nc.dram_tensor("wv", [D, D], F32, kind="ExternalInput")
    wo = nc.dram_tensor("wo", [D, D], F32, kind="ExternalInput")
    bq = nc.dram_tensor("bq", [128, NT], F32, kind="ExternalInput")
    bk = nc.dram_tensor("bk", [128, NT], F32, kind="ExternalInput")
    bv = nc.dram_tensor("bv", [128, NT], F32, kind="ExternalInput")
    bo = nc.dram_tensor("bo", [128, NT], F32, kind="ExternalInput")
    wgT = nc.dram_tensor("wgT", [D, FH], F32, kind="ExternalInput")
    wuT = nc.dram_tensor("wuT", [D, FH], F32, kind="ExternalInput")
    wdT = nc.dram_tensor("wdT", [FH, D], F32, kind="ExternalInput")
    outT = nc.dram_tensor("outT", [D, S], F32, kind="ExternalOutput")

    dbg = "ExternalOutput" if debug else "Internal"
    xn_hbm = nc.dram_tensor("xn_hbm", [D, S], BF16, kind=dbg)
    trow_hbm = nc.dram_tensor("trow_hbm", [NCH, NH, 416], BF16, kind=dbg)
    recip_hbm = nc.dram_tensor("recip_hbm", [NCH, NH * MW], BF16, kind=dbg)
    gated_hbm = nc.dram_tensor("gated_hbm", [FH, S], BF16, kind=dbg)
    wd_bf = nc.dram_tensor("wd_bf", [NT, 128, FHT * 128], BF16, kind="Internal")
    if debug:
        qsel_dbg = nc.dram_tensor("qsel_dbg", [128, NT * NSEL], BF16, kind="ExternalOutput")
        snum_dbg = nc.dram_tensor("snum_dbg", [128, NT * NSEL], BF16, kind="ExternalOutput")
        ctx_dbg = nc.dram_tensor("ctx_dbg", [128, NT * 411], BF16, kind="ExternalOutput")
        delta_dbg = nc.dram_tensor("delta_dbg", [128, NT * NSEL], BF16, kind="ExternalOutput")
        u_dbg = nc.dram_tensor("u_dbg", [128, NT], F32, kind="ExternalOutput")
        vsum_dbg = nc.dram_tensor("vsum_dbg", [128, NT], F32, kind="ExternalOutput")

    BASE = 16544

    def al(name, shape, dtype, offset):
        return nc.alloc_sbuf_tensor_at(name, shape, dtype, offset=BASE + offset)

    A = al("A", [128, 32768], BF16, offset=0)
    Bw = al("Bw", [128, 32768], BF16, offset=65536)
    C = al("C", [128, 13120], BF16, offset=131072)
    P1 = al("P1", [128, 6560], BF16, offset=157312)
    P2 = al("P2", [128, 6576], BF16, offset=170432)
    DL = al("DL", [128, 6560], BF16, offset=183584)
    T = al("T", [128, 2048], F32, offset=196704)
    XF0 = al("XF0", [128, 8192], F32, offset=32768)
    XF1 = al("XF1", [128, 8192], F32, offset=65536)
    WDS0 = al("WDS0", [128, 8192], BF16, offset=32768)
    WDS1 = al("WDS1", [128, 8192], BF16, offset=49152)
    m_sb = al("m_sb", [128, 512], F32, offset=200800)
    r_sb = al("r_sb", [128, 512], F32, offset=202848)
    t1_sb = None
    trw = [al("trw%d" % j, [1, 8 * CT], BF16, offset=157312 + 6560 * j) for j in range(2)]
    trr = [al("trr%d" % j, [1, 8 * CT], BF16, offset=170432 + 6560 * j) for j in range(2)]
    gs_sb = al("gs_sb", [128, 2048], BF16, offset=131072)
    gb0 = al("gb0", [128, 2048], BF16, offset=135168)
    gb1 = al("gb1", [128, 2048], BF16, offset=139264)
    outb0 = al("outb0", [128, 512], F32, offset=131072)
    outb1 = al("outb1", [128, 512], F32, offset=133120)
    xb0 = al("xb0", [128, 2048], F32, offset=135168)
    xb1 = al("xb1", [128, 2048], F32, offset=143360)
    sm = 204896
    bq_sb = al("bq_sb", [128, NT], F32, offset=sm); sm += 64
    bk_sb = al("bk_sb", [128, NT], F32, offset=sm); sm += 64
    bv_sb = al("bv_sb", [128, NT], F32, offset=sm); sm += 64
    bo_sb = al("bo_sb", [128, NT], F32, offset=sm); sm += 64
    u_sb = al("u_sb", [128, NT], F32, offset=sm); sm += 64
    vsum_sb = al("vsum_sb", [128, NT], F32, offset=sm); sm += 64
    upre_sb = al("upre_sb", [128, NT], BF16, offset=sm); sm += 32
    den_sb = al("den_sb", [128, 2 * MW], F32, offset=sm); sm += 352
    num_sb = al("num_sb", [128, 2 * MW], F32, offset=sm); sm += 352
    rrow_sb = al("rrow_sb", [1, NH * MW], BF16, offset=sm); sm += 1344
    ones128 = al("ones128", [128, 128], BF16, offset=sm); sm += 256
    ones1 = al("ones1", [1, 128], BF16, offset=sm); sm += 256
    onesk = al("onesk", [128, 8], BF16, offset=sm); sm += 32
    colt_sb = al("colt_sb", [128, 1], F32, offset=sm); sm += 32
    vt_sb = al("vt_sb", [128, 1], F32, offset=sm); sm += 32
    eps_sb = al("eps_sb", [128, 1], F32, offset=sm); sm += 32
    denr_sb = al("denr_sb", [1, 64], F32, offset=sm); sm += 256
    t1_sb = al("t1_sb", [128, 512], F32, offset=183584)
    ones128f = al("ones128f", [128, 128], F32, offset=sm); sm += 512
    Eb4 = [al("E%d" % j, [128, 2 * CT], BF16, offset=196704 + 1664 * j) for j in range(4)]
    rrw = []
    for j in range(2):
        rrw.append(al("rrw%d" % j, [1, NH * MW], BF16, offset=sm)); sm += 1344
    assert BASE + sm <= 229376 - 64, sm

    psA = nc.alloc_psum_tensor("psA", [128, 2048], F32)
    psS = nc.alloc_psum_tensor("psS", [128, 1024], F32)
    psX = nc.alloc_psum_tensor("psX", [128, 1024], F32)

    b = Bld(nc)
    import builtins
    builtins._BLD = b

    def wtile(i, o):
        return Bw[:, o * 2048 + i * 128:o * 2048 + (i + 1) * 128]

    def load_w_slabs_per_o(wsrc, c_last):
        w3 = r_pit(wsrc[:, :])
        for o in range(NT):
            b.wait("gpsimd", "sPE", pms[(c_last, o)])
            dst = Bw[:, o * 2048:(o + 1) * 2048].rearrange("p (i c) -> p i c", i=NT)
            b.dma("gpsimd", dst, w3[:, :, o * 128:(o + 1) * 128], "dW%d" % o)

    def load_w_slabs(wsrc, guard=None, olist=range(NT)):
        w3 = r_pit(wsrc[:, :])
        for o in olist:
            if guard is not None:
                b.wait("gpsimd", guard[0], guard[1])
            dst = Bw[:, o * 2048:(o + 1) * 2048].rearrange("p (i c) -> p i c", i=NT)
            b.dma("gpsimd", dst, w3[:, :, o * 128:(o + 1) * 128], "dW%d" % o)

    s_reg = [(psS, 0), (psS, 512), (psX, 0), (psX, 512)]

    b.dma("gpsimd", bq_sb[:, :], bq[:, :], "dB")
    b.dma("gpsimd", bk_sb[:, :], bk[:, :], "dB")
    b.dma("gpsimd", bv_sb[:, :], bv[:, :], "dB")
    b.dma("gpsimd", bo_sb[:, :], bo[:, :], "dB")
    load_w_slabs(wq, None, range(8, 16))
    b.op("vector", lambda e: e.memset(ones128[:, :], 1.0))
    b.op("vector", lambda e: e.memset(ones128f[:, :], 1.0))
    b.op("vector", lambda e: e.memset(ones1[:, :], 1.0))
    b.op("vector", lambda e: e.memset(onesk[:, :], 1.0))
    b.op("vector", lambda e: e.memset(eps_sb[:, :], EPS))
    b.op("vector", lambda e: e.memset(vsum_sb[:, :], 0.0), inc="sDVE")

    XF = (XF0, XF1)
    xstage = (A[:, 0:8192], A[:, 8192:16384])
    xT3 = r_pit(xT[:, :])
    xn3 = r_pit(xn_hbm[:, :])
    xf_free_mark = [0, 0]
    xstage_free_mark = [0, 0]
    selpos = []
    m0 = 0
    for c in range(8):
        tl0 = (-512 * c) % 10
        cnt_ = (512 - tl0 + 9) // 10
        selpos.append((tl0, cnt_, m0))
        m0 += cnt_
    assert m0 == NSEL

    xf_q_mark = [[0] * 4, [0] * 4]

    def s0_dma(c):
        par = c % 2
        dst = XF[par][:, :].rearrange("p (n t) -> p n t", n=NT)
        for k in range(4):
            b.wait("sync", "sDVE", xf_q_mark[par][k])
            b.dma("sync", dst[:, 4 * k:4 * k + 4, :],
                  xT3[:, 4 * k:4 * k + 4, 512 * c:512 * (c + 1)], "dX%dq%d" % (par, k))

    s0_dma(0)
    s0_dma(1)
    sq32 = (al("sq32a", [128, 512], F32, offset=196704),
            al("sq32b", [128, 512], F32, offset=198752))
    mmk = [0] * NT
    statfree = 0
    b.wait("tensor", "sDVE", b.cnt["sDVE"])
    for c in range(8):
        par = c % 2
        for i in range(NT):
            ip = i % 2
            xsl = XF[par][:, 512 * i:512 * (i + 1)]
            if i >= 2:
                b.wait("scalar", "sPE", mmk[i - 2])
            if i % 4 == 0:
                b.wait("scalar", "dX%dq%d" % (par, i // 4), 16 * (c // 2 + 1))
            b.op("scalar", lambda e, s=xsl, d=sq32[ip]:
                 e.activation(d[:, :], s, AF.Square), inc="sACT")
            if i % 4 == 0:
                b.wait("tensor", "dX%dq%d" % (par, i // 4), 16 * (c // 2 + 1))
            if i == 0:
                b.wait("tensor", "sDVE", statfree)
            b.op("tensor", lambda e, s=xsl, st=(i == 0), sp=(i == NT - 1):
                 e.matmul(psX[:, 0:512], ones128f[:, :], s, start=st, stop=sp))
            b.wait("tensor", "sACT")
            mmk[i] = b.op("tensor", lambda e, d=sq32[ip], st=(i == 0), sp=(i == NT - 1):
                 e.matmul(psX[:, 512:1024], ones128f[:, :], d[:, :], start=st, stop=sp),
                 inc="sPE")
        b.wait("vector", "sPE", mmk[NT - 1])
        b.op("vector", lambda e: e.tensor_scalar(m_sb[:, :], psX[:, 0:512], 1.0 / D, None, ALU.mult))
        b.op("vector", lambda e: e.tensor_mul(r_sb[:, :], m_sb[:, :], m_sb[:, :]))
        statfree = b.op("vector", lambda e: e.scalar_tensor_tensor(
            r_sb[:, :], psX[:, 512:1024], 1.0 / D, r_sb[:, :], ALU.mult, ALU.subtract), inc="sDVE")
        b.wait("scalar", "sDVE", statfree)
        b.op("scalar", lambda e: e.activation(r_sb[:, :], r_sb[:, :], AF.Sqrt, bias=eps_sb[:, 0:1]),
             inc="sACT")
        b.wait("vector", "sACT")
        b.op("vector", lambda e: e.reciprocal(r_sb[:, :], r_sb[:, :]))
        b.wait("vector", "dXNW%d" % par, xstage_free_mark[par])
        tl0, cnt_, mst = selpos[c]
        for i in range(NT):
            xsl = XF[par][:, 512 * i:512 * (i + 1)]
            stg = xstage[par][:, 512 * i:512 * (i + 1)]
            b.op("vector", lambda e, x=xsl: e.tensor_sub(t1_sb[:, :], x, m_sb[:, :]))
            b.op("vector", lambda e, d=stg: e.tensor_mul(d, t1_sb[:, :], r_sb[:, :]))
            b.op("vector", lambda e, d=P1[:, i * NSEL + mst:i * NSEL + mst + cnt_],
                 s=xstage[par][:, 512 * i + tl0:512 * (i + 1):10]: e.tensor_copy(d, s))
            if i % 4 == 3:
                xf_q_mark[par][i // 4] = b.cnt["sDVE"]
        xf_free_mark[par] = b.op("vector", lambda e: e.engine_nop(), inc="sDVE")
        b.wait("sync", "sDVE", xf_free_mark[par])
        src2 = xstage[par][:, :].rearrange("p (n t) -> p n t", n=NT)
        xstage_free_mark[par] = b.dma("sync", xn3[:, :, 512 * c:512 * (c + 1)], src2,
                                      "dXNW%d" % par)
        if c + 2 < 8:
            s0_dma(c + 2)
    sel_ready = b.cnt["sDVE"]

    slot_evac = [("sDVE", 0), ("sDVE", 0), ("sDVE", 0), ("sDVE", 0)]

    load_w_slabs(wq, ("sDVE", xf_free_mark[1]), range(0, 8))
    b.wait("tensor", "sDVE", sel_ready)
    b.wait("vector", "dB", 64)
    first_g = True
    for g in (2, 3, 0, 1):
        if not first_g:
            b.wait("tensor", "sDVE", b.cnt["sDVE"])
        first_g = False
        for o in range(4 * g, 4 * g + 4):
            b.wait("tensor", "dW%d" % o, b.cnt["dW%d" % o])
        for i in range(NT):
            for o in range(4 * g, 4 * g + 4):
                last = (i == NT - 1 and o == 4 * g + 3)
                m = b.op("tensor", lambda e, i=i, o=o, st=(i == 0), sp=(i == NT - 1):
                     e.matmul(psA[:, (o % 4) * 512:(o % 4) * 512 + NSEL], wtile(i, o),
                              P1[:, i * NSEL:(i + 1) * NSEL], start=st, stop=sp),
                     inc="sPE" if last else None)
        b.wait("vector", "sPE", m)
        for o in range(4 * g, 4 * g + 4):
            slot_evac[o % 4] = ("sDVE", b.op("vector", lambda e, o=o:
                 e.tensor_scalar(P2[:, o * NSEL:(o + 1) * NSEL],
                                 psA[:, (o % 4) * 512:(o % 4) * 512 + NSEL],
                                 bq_sb[:, o:o + 1], QSCALE, ALU.add, ALU.mult), inc="sDVE"))
    qsel_ready = b.cnt["sDVE"]
    s1_pe_done = b.cnt["sPE"]
    if debug:
        b.wait("sync", "sDVE", qsel_ready)
        b.dma("sync", qsel_dbg[:, :], P2[:, 0:NT * NSEL], "dDBG")
        b.wait("vector", "dDBG", b.cnt["dDBG"])

    Cb = (C[:, 0:NT * CT], C[:, NT * CT:2 * NT * CT])
    xnbuf_free = [0, 0]
    cb_free = [0, 0]
    pms = {}

    def xn_dma(c, dve_guard=0):
        par = c % 2
        b.wait("sync", "sPE", xnbuf_free[par])
        if dve_guard:
            b.wait("sync", "sDVE", dve_guard)
        b.wait("sync", "dXNW0", 64)
        b.wait("sync", "dXNW1", 64)
        dst = A[:, par * NT * CT:(par + 1) * NT * CT].rearrange("p (n t) -> p n t", n=NT)
        L = CT if c < NCH - 1 else S - (NCH - 1) * CT
        b.dma("sync", dst[:, :, 0:L], xn3[:, :, c * CT:c * CT + L], "dXN%d" % par)

    def proj_chunk(c, wcnt, bias_col):
        par = c % 2
        kv_len = CT if c < NCH - 1 else S - (NCH - 1) * CT
        xb_ = A[:, par * NT * CT:(par + 1) * NT * CT]
        b.wait("tensor", "dXN%d" % par, b.cnt["dXN%d" % par])
        b.wait("scalar", "sDVE", cb_free[par])
        for o in range(NT):
            reg = (o % 4) * 512
            b.wait("tensor", slot_evac[o % 4][0], slot_evac[o % 4][1])
            b.wait("tensor", "dW%d" % o, wcnt[o])
            for i in range(NT):
                m = b.op("tensor", lambda e, i=i, o=o, r=reg, x=xb_,
                         st=(i == 0), sp=(i == NT - 1):
                         e.matmul(psA[:, r:r + CT], wtile(i, o), x[:, i * CT:(i + 1) * CT],
                                  start=st, stop=sp),
                         inc="sPE" if i == NT - 1 else None)
            pms[(c, o)] = m
            b.wait("scalar", "sPE", m)
            ev = b.op("scalar", lambda e, o=o, r=reg, bc=bias_col, dst=Cb[par]:
                 e.activation(dst[:, o * CT:(o + 1) * CT].rearrange(
                                  "p (d m) -> p m d", d=10, m=MW),
                              psA[:, r:r + CT].rearrange("p (m d) -> p m d", m=MW, d=10),
                              AF.Identity, bias=bc[:, o:o + 1]), inc="sACT")
            slot_evac[o % 4] = ("sACT", ev)
        if kv_len < CT:
            b.wait("vector", "sACT", b.cnt["sACT"])
            pad = Cb[par][:, :].rearrange("p (o d m) -> p o d m", o=NT, d=10, m=MW)
            b.op("vector", lambda e, pad=pad:
                 e.memset(pad[:, :, 6:10, MW - 1:MW], 0.0))
        pms[c] = m
        return m

    def kslice3(h, par):
        return Cb[par][:, h * CT:(h + 1) * CT].rearrange("p (d m) -> p d m", d=10, m=MW)

    E_free = [0, 0, 0, 0]
    psR_free = [0, 0, 0, 0]
    trw_free = [0, 0]
    trr_free = [0, 0]

    def scores_chunk(c, evac_mark):
        par = c % 2
        b.wait("vector", "sACT", evac_mark)
        for g in range(8):
            h0 = 2 * g
            rot = g % 4
            pg = g % 2
            hb = h0 // 8
            ps_ = psS if pg == 0 else psX
            Eb = Eb4[rot]
            b.wait("vector", "sPE", E_free[rot])
            q2 = P2[:, h0 * NSEL + MW * c:(h0 + 1) * NSEL + MW * (c + 1)]
            qap = bass.AP(q2.tensor, q2.offset, [[6576, 128], [NSEL, 2], [0, 10], [1, MW]])
            kap = Cb[par][:, h0 * CT:(h0 + 2) * CT].rearrange(
                "p (g d m) -> p g d m", g=2, d=10, m=MW)
            b.op("vector", lambda e, qap=qap, kap=kap, Eb=Eb:
                 e.tensor_mul(Eb[:, :].rearrange("p (g d m) -> p g d m", g=2, d=10),
                              qap, kap), inc="sDVE")
            E_mark = b.cnt["sDVE"]
            b.wait("tensor", "sDVE", E_mark)
            b.wait("tensor", "sDVE", psR_free[pg])
            b.op("tensor", lambda e, Eb=Eb, ps=ps_:
                 e.matmul(ps[0:1, 0:512], onesk[:, 0:1], Eb[:, 0:512],
                          start=True, stop=True))
            E_free[rot] = b.op("tensor", lambda e, Eb=Eb, ps=ps_:
                 e.matmul(ps[0:1, 512:820], onesk[:, 0:1], Eb[:, 512:820],
                          start=True, stop=True), inc="sPE")
            b.wait("scalar", "sPE", E_free[rot])
            b.op("scalar", lambda e, ps=ps_:
                 e.activation(ps[0:1, 0:820], ps[0:1, 0:820], AF.Exp), inc="sACT")
            b.wait("vector", "sACT", b.cnt["sACT"])
            if g % 4 == 0:
                b.wait("vector", "dT%d" % hb, trw_free[hb])
            psR_free[pg] = b.op("vector", lambda e, ps=ps_, t=trw[hb], hl=h0 % 8:
                 e.tensor_scalar(t[0:1, hl * CT:(hl + 2) * CT], ps[0:1, 0:820],
                                 -1.0, None, ALU.add), inc="sDVE")
            if g % 4 == 3:
                m2 = b.cnt["sDVE"]
                b.wait("sync", "sDVE", m2)
                trw_free[hb] = b.dma("sync", trow_hbm[c, hb * 8:hb * 8 + 8, 0:CT],
                                     trw[hb][0:1, :].rearrange("o (g m) -> o g m", g=8),
                                     "dT%d" % hb)

    rrow_free = [0]
    def num_chunk(c, evac_mark):
        par = c % 2
        b.wait("vector", "sACT", evac_mark)
        for g in range(8):
            h0 = 2 * g
            pg = g % 2
            hb = h0 // 8
            ps_ = psS if pg == 0 else psX
            if h0 % 8 == 0:
                b.wait("sync", "sPE", trr_free[hb])
                b.dma("sync", trr[hb][0:1, :].rearrange("o (g m) -> o g m", g=8),
                      trow_hbm[c, hb * 8:hb * 8 + 8, 0:CT], "dTR%d" % hb)
            b.wait("tensor", "dTR%d" % hb, b.cnt["dTR%d" % hb])
            b.wait("tensor", "sDVE", psR_free[pg])
            b.op("tensor", lambda e, t=trr[hb], hl=h0 % 8, ps=ps_:
                 e.matmul(ps[:, 0:512], ones1[:, :], t[0:1, hl * CT:hl * CT + 512],
                          start=True, stop=True))
            trr_free[hb] = b.op("tensor", lambda e, t=trr[hb], hl=h0 % 8, ps=ps_:
                 e.matmul(ps[:, 512:820], ones1[:, :], t[0:1, hl * CT + 512:hl * CT + 820],
                          start=True, stop=True), inc="sPE")
            b.wait("vector", "sPE", trr_free[hb])
            kap = Cb[par][:, h0 * CT:(h0 + 2) * CT].rearrange(
                "p (g d m) -> p g d m", g=2, d=10, m=MW)
            b.op("vector", lambda e, ps=ps_, kap=kap:
                 e.tensor_mul(T[:, 0:820].rearrange("p (g d m) -> p g d m", g=2, d=10),
                              ps[:, 0:820].rearrange("p (g d m) -> p g d m", g=2, d=10),
                              kap))
            b.op("vector", lambda e:
                 e.tensor_reduce(num_sb[:, :],
                                 T[:, 0:820].rearrange("p (g d m) -> p g m d", g=2, d=10),
                                 AX.X, ALU.add))
            psR_free[pg] = b.op("vector", lambda e, ps=ps_:
                 e.tensor_reduce(den_sb[:, :],
                                 ps[:, 0:820].rearrange("p (g d m) -> p g m d", g=2, d=10),
                                 AX.X, ALU.add), inc="sDVE")
            b.op("vector", lambda e: e.tensor_scalar(den_sb[:, :], den_sb[:, :], float(S),
                                                     None, ALU.add))
            b.op("vector", lambda e: e.reciprocal(den_sb[:, :], den_sb[:, :]))
            pap = P1[:, h0 * NSEL + MW * c:(h0 + 1) * NSEL + MW * (c + 1)]
            pap2 = bass.AP(pap.tensor, pap.offset, [[6560, 128], [NSEL, 2], [1, MW]])
            b.op("vector", lambda e, pap2=pap2:
                 e.tensor_mul(pap2, num_sb[:, :].rearrange("p (g m) -> p g m", g=2),
                              den_sb[:, :].rearrange("p (g m) -> p g m", g=2)))
            if g == 0:
                b.wait("vector", "dRW", rrow_free[0])
            b.op("vector", lambda e, h0=h0, t=rrw[c % 2]:
                 e.tensor_copy(t[0:1, h0 * MW:(h0 + 2) * MW], den_sb[0:1, :]), inc="sDVE")
        for i in range(NT):
            b.op("vector", lambda e, i=i, src=Cb[par]:
                 e.tensor_reduce(vt_sb[:, :], src[:, i * CT:(i + 1) * CT], AX.X, ALU.add))
            b.op("vector", lambda e, i=i: e.tensor_add(vsum_sb[:, i:i + 1], vsum_sb[:, i:i + 1],
                                                       vt_sb[:, :]))
        cb_free[par] = b.cnt["sDVE"]
        m2 = b.cnt["sDVE"]
        b.wait("sync", "sDVE", m2)
        rrow_free[0] = b.dma("sync", recip_hbm[c:c + 1, :], rrw[c % 2][0:1, :], "dRW")

    rr_guard = [0]

    wdT3 = r_pit(wdT[:, :])
    WDS = (WDS0, WDS1)
    wds_free = [0, 0]

    def wd_precast(o):
        pw = o % 2
        b.wait("gpsimd", "dWP%d" % pw, wds_free[pw])
        b.dma("gpsimd", WDS[pw][:, :].rearrange("p (g c) -> p g c", g=FHT),
              wdT3[:, :, o * 128:(o + 1) * 128], "dWC")
        b.wait("gpsimd", "dWC", b.cnt["dWC"])
        wds_free[pw] = b.dma("gpsimd", wd_bf[o, :, :], WDS[pw][:, :], "dWP%d" % pw)

    load_w_slabs(wk, ("sPE", s1_pe_done))
    wk_cnt = [b.cnt["dW%d" % o] for o in range(NT)]
    xn_dma(0, dve_guard=sel_ready)
    xn_dma(1, dve_guard=sel_ready)
    ev_mark = {}
    for c in range(NCH):
        if c >= 1:
            scores_chunk(c - 1, ev_mark[c - 1])
        pm = proj_chunk(c, wk_cnt, bk_sb)
        ev_mark[c] = b.cnt["sACT"]
        if c + 2 < NCH:
            xnbuf_free[c % 2] = pm
            xn_dma(c + 2)
        if c < 10:
            wd_precast(c)
    scores_chunk(NCH - 1, ev_mark[NCH - 1])

    load_w_slabs_per_o(wv, NCH - 1)
    wv_cnt = [b.cnt["dW%d" % o] for o in range(NT)]
    b.wait("vector", "dT0", b.cnt["dT0"])
    b.wait("vector", "dT1", b.cnt["dT1"])
    xnbuf_free = [pms[NCH - 2], pms[NCH - 1]]
    xn_dma(0)
    xn_dma(1)
    for c in range(NCH):
        if c >= 1:
            num_chunk(c - 1, ev_mark[c - 1])
        pm = proj_chunk(c, wv_cnt, bv_sb)
        ev_mark[c] = b.cnt["sACT"]
        if c + 2 < NCH:
            xnbuf_free[c % 2] = pm
            xn_dma(c + 2)
        if c + 10 < NT:
            wd_precast(c + 10)
    num_chunk(NCH - 1, ev_mark[NCH - 1])
    s3_last_pm = pms[NCH - 1]
    if debug:
        m = b.op("vector", lambda e: e.engine_nop(), inc="sDVE")
        b.wait("sync", "sDVE", m)
        b.dma("sync", snum_dbg[:, :], P1[:, :], "dDBG")
        b.dma("sync", vsum_dbg[:, :], vsum_sb[:, :], "dDBG")
        b.wait("vector", "dDBG", b.cnt["dDBG"])

    b.op("vector", lambda e: e.tensor_scalar(upre_sb[:, :], vsum_sb[:, :], 1.0 / S, None,
                                             ALU.mult), inc="sDVE")
    psR2 = [b.cnt["sDVE"], b.cnt["sDVE"]]
    b.wait("sync", "dRW", b.cnt["dRW"])
    b.dma("sync", rrw[0][0:1, 0:NH * MW], recip_hbm[0:1, :], "dRR0")
    b.dma("sync", rrw[1][0:1, 0:NH * MW], recip_hbm[1:2, :], "dRR1")
    for c in range(NCH):
        pc = c % 2
        for half in range(2):
            ps_ = psS if half == 0 else psX
            b.wait("tensor", "dRR%d" % pc, 16 * (c // 2 + 1))
            b.wait("tensor", "sDVE", psR2[half])
            b.op("tensor", lambda e, ps=ps_, t=rrw[pc], half=half:
                 e.matmul(ps[:, 0:328], ones1[:, :],
                          t[0:1, half * 328:(half + 1) * 328], start=True, stop=True),
                 inc="sPE")
            mmark = b.cnt["sPE"]
            b.wait("vector", "sPE", mmark)
            for hl in range(8):
                h = half * 8 + hl
                psR2[half] = b.op("vector", lambda e, h=h, hl=hl, c=c, ps=ps_:
                     e.scalar_tensor_tensor(P2[:, h * 411 + MW * c:h * 411 + MW * (c + 1)],
                                            ps[:, hl * MW:(hl + 1) * MW], vsum_sb[:, h:h + 1],
                                            P1[:, h * NSEL + MW * c:h * NSEL + MW * (c + 1)],
                                            ALU.mult, ALU.add), inc="sDVE")
        if c + 2 < NCH:
            b.wait("sync", "sPE", b.cnt["sPE"])
            b.dma("sync", rrw[pc][0:1, 0:NH * MW], recip_hbm[c + 2:c + 3, :], "dRR%d" % pc)
    b.op("vector", lambda e: e.tensor_copy(P2[:, 410:16 * 411:411], upre_sb[:, :]))
    ctx_ready = b.cnt["sDVE"]
    if debug:
        b.wait("sync", "sDVE", ctx_ready)
        b.dma("sync", ctx_dbg[:, :], P2[:, :], "dDBG")
        b.wait("vector", "dDBG", b.cnt["dDBG"])

    load_w_slabs_per_o(wo, NCH - 1)
    b.wait("tensor", "sDVE", ctx_ready)
    for g in range(4):
        for sl in range(4):
            b.wait("tensor", slot_evac[sl][0], slot_evac[sl][1])
        for o in range(4 * g, 4 * g + 4):
            b.wait("tensor", "dW%d" % o, b.cnt["dW%d" % o])
        for i in range(NT):
            for o in range(4 * g, 4 * g + 4):
                last = (i == NT - 1 and o == 4 * g + 3)
                m = b.op("tensor", lambda e, i=i, o=o, st=(i == 0), sp=(i == NT - 1):
                     e.matmul(psA[:, (o % 4) * 512:(o % 4) * 512 + 411], wtile(i, o),
                              P2[:, i * 411:(i + 1) * 411], start=st, stop=sp),
                     inc="sPE" if last else None)
        b.wait("vector", "sPE", m)
        for o in range(4 * g, 4 * g + 4):
            r = (o % 4) * 512
            b.op("vector", lambda e, o=o, r=r:
                 e.tensor_scalar(u_sb[:, o:o + 1], psA[:, r + 410:r + 411],
                                 bo_sb[:, o:o + 1], None, ALU.add))
            b.op("vector", lambda e, o=o:
                 e.tensor_sub(colt_sb[:, :], bo_sb[:, o:o + 1], u_sb[:, o:o + 1]))
            slot_evac[o % 4] = ("sDVE", b.op("vector", lambda e, o=o, r=r:
                 e.tensor_scalar(DL[:, o * NSEL:(o + 1) * NSEL], psA[:, r:r + NSEL],
                                 colt_sb[:, :], None, ALU.add), inc="sDVE"))
    if debug:
        m = b.cnt["sDVE"]
        b.wait("sync", "sDVE", m)
        b.dma("sync", delta_dbg[:, :], DL[:, :], "dDBG")
        b.dma("sync", u_dbg[:, :], u_sb[:, :], "dDBG")
        b.wait("vector", "dDBG", b.cnt["dDBG"])

    wg3 = r_pit(wgT[:, :])
    wu3 = r_pit(wuT[:, :])
    g3 = r_pit(gated_hbm[:, :])
    slabs = [Bw[:, 0:8192], Bw[:, 8192:16384], Bw[:, 16384:24576], Bw[:, 24576:32768]]
    slab_free = [b.cnt["sPE"], b.cnt["sPE"]]
    gb = (gb0, gb1)
    gb_free = [0, 0]
    gs_free = b.cnt["sDVE"]
    psA_free = 0
    psB_free = b.cnt["sDVE"]
    half_free = s3_last_pm
    b.wait("tensor", "sDVE", b.cnt["sDVE"])
    b.wait("sync", "dWP0", b.cnt["dWP0"])
    b.wait("sync", "dWP1", b.cnt["dWP1"])
    for half in range(2):
        b.wait("sync", "sPE", half_free)
        b.dma("sync", A[:, :].rearrange("p (n t) -> p n t", n=NT),
              xn3[:, :, half * 2048:(half + 1) * 2048], "dXN0")
        for s_ in range(16):
            ps_ = s_ % 2
            b.wait("gpsimd", "sPE", slab_free[ps_])
            b.dma("gpsimd", slabs[2 * ps_][:, :].rearrange("p (n t) -> p n t", n=NT),
                  wg3[:, :, s_ * 512:(s_ + 1) * 512], "dSL%d" % ps_)
            b.dma("gpsimd", slabs[2 * ps_ + 1][:, :].rearrange("p (n t) -> p n t", n=NT),
                  wu3[:, :, s_ * 512:(s_ + 1) * 512], "dSL%d" % ps_)
            for ht in range(4):
                gidx = half * 64 + s_ * 4 + ht
                pg = gidx % 2
                b.wait("tensor", "dSL%d" % ps_, b.cnt["dSL%d" % ps_])
                b.wait("tensor", "dXN0", b.cnt["dXN0"])
                b.wait("tensor", "sACT", psA_free)
                for i in range(NT):
                    for ch in range(4):
                        gate_done = b.op("tensor", lambda e, i=i, ch=ch, sl=slabs[2 * ps_],
                             ht=ht, st=(i == 0), sp=(i == NT - 1):
                             e.matmul(psA[:, ch * 512:(ch + 1) * 512],
                                      sl[:, i * 512 + ht * 128:i * 512 + (ht + 1) * 128],
                                      A[:, i * 2048 + ch * 512:i * 2048 + (ch + 1) * 512],
                                      start=st, stop=sp),
                             inc="sPE" if (i == NT - 1 and ch == 3) else None)
                b.wait("tensor", "sDVE", psB_free)
                for i in range(NT):
                    for ch in range(4):
                        tgt = psS if ch < 2 else psX
                        up_done = b.op("tensor", lambda e, i=i, ch=ch, sl=slabs[2 * ps_ + 1],
                             ht=ht, t=tgt, st=(i == 0), sp=(i == NT - 1):
                             e.matmul(t[:, (ch % 2) * 512:(ch % 2 + 1) * 512],
                                      sl[:, i * 512 + ht * 128:i * 512 + (ht + 1) * 128],
                                      A[:, i * 2048 + ch * 512:i * 2048 + (ch + 1) * 512],
                                      start=st, stop=sp),
                             inc="sPE" if (i == NT - 1 and ch == 3) else None)
                b.wait("scalar", "sPE", gate_done)
                b.wait("scalar", "sDVE", gs_free)
                psA_free = b.op("scalar", lambda e:
                     e.activation(gs_sb[:, :], psA[:, 0:2048], AF.Silu), inc="sACT")
                b.wait("vector", "sPE", up_done)
                b.wait("vector", "sACT", psA_free)
                b.wait("vector", "dGW%d" % pg, gb_free[pg])
                b.op("vector", lambda e, d=gb[pg]:
                     e.tensor_mul(d[:, 0:1024], psS[:, 0:1024], gs_sb[:, 0:1024]))
                psB_free = b.op("vector", lambda e, d=gb[pg]:
                     e.tensor_mul(d[:, 1024:2048], psX[:, 0:1024], gs_sb[:, 1024:2048]),
                     inc="sDVE")
                gs_free = psB_free
                b.wait("scalar", "sDVE", psB_free)
                gb_free[pg] = b.dma("scalar", g3[:, s_ * 4 + ht, half * 2048:(half + 1) * 2048],
                                    gb[pg][:, :], "dGW%d" % pg)
            slab_free[ps_] = b.cnt["sPE"]
        half_free = b.cnt["sPE"]

    xbb = (xb0, xb1)
    outb = (outb0, outb1)
    wdq_free = b.cnt["sPE"]
    ge_free = [b.cnt["sPE"], b.cnt["sPE"]]
    xb_free = [0, 0]
    outb_free = [0, 0]
    psA_slot_free = [0, 0, 0, 0]
    b.wait("sync", "dGW0", b.cnt["dGW0"])
    b.wait("sync", "dGW1", b.cnt["dGW1"])
    b.wait("scalar", "dGW0", b.cnt["dGW0"])
    b.wait("scalar", "dGW1", b.cnt["dGW1"])
    b.wait("tensor", "sDVE", b.cnt["sDVE"])
    b.wait("tensor", "sACT", b.cnt["sACT"])
    for dq in range(4):
        for dtl in range(4):
            if dq == 0 and dtl < 2:
                b.wait("gpsimd", "sPE", slab_free[0])
            else:
                b.wait("gpsimd", "sPE", wdq_free)
            b.dma("gpsimd", Bw[:, dtl * 8192:(dtl + 1) * 8192], wd_bf[dq * 4 + dtl, :, :],
                  "dWD%d" % dtl)
        for e_ in range(8):
            pe_ = e_ % 2
            b.wait("scalar", "sDVE", xb_free[pe_])
            b.dma("scalar", xbb[pe_][:, :].rearrange("p (n t) -> p n t", n=4),
                  xT3[:, 4 * dq:4 * dq + 4, e_ * 512:(e_ + 1) * 512], "dXS%d" % pe_)
            for hh in range(2):
                q_ = "sync" if hh == 0 else "scalar"
                b.wait(q_, "sPE", ge_free[hh])
                dst = A[:, hh * 16384:(hh + 1) * 16384].rearrange("p (n t) -> p n t", n=32)
                b.dma(q_, dst, g3[:, hh * 32:(hh + 1) * 32, e_ * 512:(e_ + 1) * 512],
                      "dGE%d" % hh)
                b.wait("tensor", "dGE%d" % hh, b.cnt["dGE%d" % hh])
                for dt in range(4):
                    if hh == 0:
                        b.wait("tensor", "dWD%d" % dt, b.cnt["dWD%d" % dt])
                        b.wait("tensor", "sDVE", psA_slot_free[dt])
                    for h32 in range(32):
                        g_ = hh * 32 + h32
                        ge_free[hh] = b.op("tensor", lambda e, g=g_, dt=dt, hh=hh, h32=h32,
                             st=(hh == 0 and h32 == 0), sp=(hh == 1 and h32 == 31):
                             e.matmul(psA[:, dt * 512:(dt + 1) * 512],
                                      Bw[:, dt * 8192 + g * 128:dt * 8192 + (g + 1) * 128],
                                      A[:, hh * 16384 + h32 * 512:hh * 16384 + (h32 + 1) * 512],
                                      start=st, stop=sp),
                             inc="sPE" if (h32 == 31 and dt == 3) else None)
            mm_done = ge_free[1]
            tl0 = (-512 * e_) % 10
            msta = (512 * e_ + 9) // 10
            scnt = (512 * (e_ + 1) - 1 - (512 * e_ + tl0)) // 10 + 1
            for dt in range(4):
                o = dq * 4 + dt
                po = (e_ * 4 + dt) % 2
                b.wait("vector", "sPE", mm_done)
                b.wait("vector", "dXS%d" % pe_, b.cnt["dXS%d" % pe_])
                b.wait("vector", "dOUT%d" % po, outb_free[po])
                b.op("vector", lambda e, dt=dt, o=o, po=po, pe2=pe_:
                     e.scalar_tensor_tensor(outb[po][:, :], psA[:, dt * 512:(dt + 1) * 512],
                                            u_sb[:, o:o + 1], xbb[pe2][:, dt * 512:(dt + 1) * 512],
                                            ALU.add, ALU.add))
                psA_slot_free[dt] = b.op("vector", lambda e, o=o, po=po, t0=tl0, ms=msta, cn=scnt:
                     e.tensor_add(outb[po][:, t0:t0 + (cn - 1) * 10 + 1:10],
                                  outb[po][:, t0:t0 + (cn - 1) * 10 + 1:10],
                                  DL[:, o * NSEL + ms:o * NSEL + ms + cn]), inc="sDVE")
                xb_free[pe_] = psA_slot_free[dt]
                b.wait("scalar", "sDVE", psA_slot_free[dt])
                outb_free[po] = b.dma("scalar", r_pit(outT[:, :])[:, o, e_ * 512:(e_ + 1) * 512],
                                      outb[po][:, :], "dOUT%d" % po)
        wdq_free = b.cnt["sPE"]
    b.wait("scalar", "dOUT0", b.cnt["dOUT0"])
    b.wait("scalar", "dOUT1", b.cnt["dOUT1"])

    with nc.allow_low_precision("bf16 activations/weights by design (tolerance allows)"):
        with nc.Block() as block:
            @block.sync
            def _(e):
                for fn in b.ops["sync"]:
                    fn(e)

            @block.gpsimd
            def _(e):
                for fn in b.ops["gpsimd"]:
                    fn(e)

            @block.vector
            def _(e):
                for fn in b.ops["vector"]:
                    fn(e)

            @block.scalar
            def _(e):
                for fn in b.ops["scalar"]:
                    fn(e)

            @block.tensor
            def _(e):
                for fn in b.ops["tensor"]:
                    fn(e)

    return nc


def prep_inputs(L0_attn_hidden_states, L0_attn_wq, L0_attn_bq, L0_attn_wk, L0_attn_bk,
                L0_attn_wv, L0_attn_bv, L0_attn_wo, L0_attn_bo,
                L0_ffn_w_gate, L0_ffn_w_up, L0_ffn_w_down):
    x = np.asarray(L0_attn_hidden_states, dtype=np.float32)
    com = {
        "wq": np.ascontiguousarray(np.asarray(L0_attn_wq, dtype=np.float32)),
        "wk": np.ascontiguousarray(np.asarray(L0_attn_wk, dtype=np.float32)),
        "wv": np.ascontiguousarray(np.asarray(L0_attn_wv, dtype=np.float32)),
        "wo": np.ascontiguousarray(np.asarray(L0_attn_wo, dtype=np.float32)),
        "bq": np.ascontiguousarray(np.asarray(L0_attn_bq, dtype=np.float32).reshape(NT, 128).T),
        "bk": np.ascontiguousarray(np.asarray(L0_attn_bk, dtype=np.float32).reshape(NT, 128).T),
        "bv": np.ascontiguousarray(np.asarray(L0_attn_bv, dtype=np.float32).reshape(NT, 128).T),
        "bo": np.ascontiguousarray(np.asarray(L0_attn_bo, dtype=np.float32).reshape(NT, 128).T),
        "wgT": np.ascontiguousarray(np.asarray(L0_ffn_w_gate, dtype=np.float32).T),
        "wuT": np.ascontiguousarray(np.asarray(L0_ffn_w_up, dtype=np.float32).T),
        "wdT": np.ascontiguousarray(np.asarray(L0_ffn_w_down, dtype=np.float32).T),
    }
    in_maps = []
    for bi in range(8):
        m = dict(com)
        m["xT"] = np.ascontiguousarray(x[bi].T)
        in_maps.append(m)
    return in_maps


def kernel(**inputs):
    in_maps = prep_inputs(**inputs)
    nc = build(debug=False)
    res = run_bass_kernel_spmd(nc, in_maps, core_ids=list(range(8)))
    return np.stack([np.ascontiguousarray(r["outT"].T) for r in res.results], axis=0)
